# version 02
# baseline (speedup 1.0000x reference)
import contextlib
import math
import numpy as np
import ml_dtypes
import concourse.bass as bass
import concourse.mybir as mybir
from concourse.bass_utils import run_bass_kernel_spmd
from concourse.bass import IndirectOffsetOnAxis

F32 = mybir.dt.float32
BF16 = mybir.dt.bfloat16
AF = mybir.ActivationFunctionType
ALU = mybir.AluOpType
AX = mybir.AxisListType
BF = ml_dtypes.bfloat16

N_CORES = 8
B_SZ = 2
T_SEQ = 4096
D_MODEL = 2048
BLK = 512
N_BLK = T_SEQ // BLK
TOK_PER_CORE = 2 * BLK

ENGINES = ("sync", "tensor", "vector", "scalar", "gpsimd")
SAME_ENGINE_FIFO = False
GROUPS4 = [[0, 1, 2, 3], [4, 5, 6, 7]]


def core_blocks(core):
    j = core % 4
    return [j, 7 - j]



class Op:
    __slots__ = ("engine", "emit", "kind", "deps", "idx", "sem_val", "milestone",
                 "tag", "buf", "sem_key", "phase")

    def __init__(self, engine, emit, kind, tag=""):
        self.engine = engine
        self.emit = emit
        self.kind = kind
        self.deps = []
        self.idx = -1
        self.sem_val = None
        self.milestone = False
        self.tag = tag
        self.buf = None
        self.sem_key = None
        self.phase = 0


class Builder:

    def __init__(self, nc):
        self.nc = nc
        self.stack = contextlib.ExitStack()
        self.phase_stack = None
        self.phase_idx = 0
        self.eng_ops = {e: [] for e in ENGINES}
        self.last_write = {}
        self.readers = {}
        self.names = set()
        self._finalized = False

    def begin_phase(self):
        assert self.phase_stack is None
        self.phase_idx += 1
        self.phase_stack = contextlib.ExitStack()

    def end_phase(self):
        lasts = [ops[-1] for ops in self.eng_ops.values() if ops]
        for e in ENGINES:
            if not self.eng_ops[e]:
                continue
            o = Op(e, lambda eng: eng.nop(), "compute", tag="barrier")
            o.deps = [d for d in lasts if d.engine != e]
            o.idx = len(self.eng_ops[e])
            self.eng_ops[e].append(o)
        barrier_ops = {e: self.eng_ops[e][-1] for e in ENGINES if self.eng_ops[e]}
        for k in list(self.last_write):
            lw = self.last_write[k]
            b = barrier_ops.get(lw.engine)
            if b is not None and b is not lw:
                pass
        for e, b in barrier_ops.items():
            pass
        self.last_write = {"__barrier__": None}
        self.last_write.clear()
        for e, b in barrier_ops.items():
            self.last_write[f"__bar_{e}__"] = b
        self.readers = {}
        self._barrier_ops = list(barrier_ops.values())
        self.phase_stack.close()
        self.phase_stack = None

    def _uname(self, name):
        if self.phase_idx:
            name = f"{name}__p{self.phase_idx}"
        assert name not in self.names, f"duplicate tensor name {name}"
        self.names.add(name)
        return name

    def _stk(self):
        return self.phase_stack if self.phase_stack is not None else self.stack

    def sbuf(self, name, shape, dtype):
        name = self._uname(name)
        return self._stk().enter_context(
            self.nc.sbuf_tensor(name, list(shape), dtype))

    def psum(self, name, shape, dtype=F32):
        name = self._uname(name)
        return self._stk().enter_context(
            self.nc.psum_tensor(name, list(shape), dtype))

    def dram(self, name, shape, dtype, kind=None):
        self._uname(name)
        if kind is None:
            return self.nc.dram_tensor(name, list(shape), dtype)
        return self.nc.dram_tensor(name, list(shape), dtype, kind=kind)

    def op(self, engine, emit, reads=(), writes=(), kind="compute", tag=""):
        o = Op(engine, emit, kind, tag)
        deps = {}
        for bk in ("__bar_sync__", "__bar_tensor__", "__bar_vector__",
                   "__bar_scalar__", "__bar_gpsimd__"):
            w = self.last_write.get(bk)
            if w is not None:
                deps[id(w)] = w
        for k in reads:
            w = self.last_write.get(k)
            if w is not None:
                deps[id(w)] = w
        for k in writes:
            w = self.last_write.get(k)
            if w is not None:
                deps[id(w)] = w
            for r in self.readers.get(k, ()):
                deps[id(r)] = r
        def _drop(d):
            if d is o:
                return True
            if SAME_ENGINE_FIFO and d.engine == engine and d.kind == "compute"                     and kind == "compute":
                return True
            return False
        o.deps = [d for d in deps.values() if not _drop(d)]
        o.idx = len(self.eng_ops[engine])
        o.phase = self.phase_idx
        self.eng_ops[engine].append(o)
        for k in reads:
            self.readers.setdefault(k, []).append(o)
        for k in writes:
            self.last_write[k] = o
            self.readers[k] = []
        return o

    def dma(self, queue, out, in_, reads, writes, tag="", **kw):
        assert len(writes) >= 1
        o = self.op(queue, lambda e: e.dma_start(out=out, in_=in_, **kw),
                    reads=reads, writes=writes, kind="dma", tag=tag)
        o.buf = writes[0]
        return o

    def dma_t(self, queue, out, in_, reads, writes, tag=""):
        o = self.op(queue, lambda e: e.dma_start(out=out, in_=in_, transpose=True),
                    reads=reads, writes=writes, kind="dma", tag=tag)
        o.buf = writes[0]
        return o

    def cc(self, kind, op, groups, ins, outs, reads, writes, tag="cc"):
        def emit(e):
            return e.collective_compute(kind, op, replica_groups=groups,
                                        ins=ins, outs=outs)
        return self.op("gpsimd", emit, reads=reads, writes=writes, kind="cc", tag=tag)

    def finalize(self):
        assert not self._finalized
        self._finalized = True
        NPOOL = 64
        phase_slots = {}
        slot_of = {}
        all_dmas = []
        for e in ENGINES:
            for o in self.eng_ops[e]:
                if o.kind == "dma":
                    pk = (o.phase, o.buf)
                    if pk not in slot_of:
                        n = phase_slots.setdefault(o.phase, 0)
                        assert n < NPOOL, f"too many dma buffers in phase {o.phase}"
                        slot_of[pk] = (o.phase * 17 + n) % NPOOL
                        phase_slots[o.phase] = n + 1
                    all_dmas.append(o)
        dma_count = {}
        per_pk = {}
        for e in ENGINES:
            for o in self.eng_ops[e]:
                if o.kind == "dma":
                    per_pk.setdefault((o.phase, o.buf), []).append(o)
        slot_total_before = {}
        slot_totals = {}
        for pk in sorted(per_pk, key=lambda t: t[0]):
            s = slot_of[pk]
            base = slot_totals.get(s, 0)
            cnt = 0
            for o in per_pk[pk]:
                cnt += 16
                o.sem_key = ("ds", s)
                o.sem_val = base + cnt
                o.milestone = True
            slot_totals[s] = base + cnt
        dma_count = {("ds", s): v for s, v in slot_totals.items()}
        for e in ENGINES:
            for o in self.eng_ops[e]:
                if o.kind == "cc":
                    o.sem_key = ("cc",)
                    o.milestone = True
                elif o.kind == "compute":
                    o.sem_key = ("c", e)
        for e in ENGINES:
            for o in self.eng_ops[e]:
                for d in o.deps:
                    d.milestone = True
        for e in ENGINES:
            c_comp = 0
            c_cc = 0
            for o in self.eng_ops[e]:
                if o.kind == "compute":
                    if o.milestone:
                        c_comp += 1
                    o.sem_val = c_comp
                elif o.kind == "cc":
                    c_cc += 1
                    o.sem_val = c_cc

        sems = {}
        for e in ENGINES:
            sems[("c", e)] = self.stack.enter_context(self.nc.semaphore(f"sem_{e}_c"))
        sems[("cc",)] = self.stack.enter_context(self.nc.semaphore("sem_cc"))
        for sk in dma_count:
            sems[sk] = self.stack.enter_context(
                self.nc.semaphore("sem_ds_" + str(sk[1])))

        tail = {}
        for e in ENGINES:
            for o in self.eng_ops[e]:
                if o.kind == "dma":
                    tail[o.sem_key] = (e, dma_count[o.sem_key])
        tail_by_engine = {}
        for sk, (e, v) in tail.items():
            tail_by_engine.setdefault(e, []).append((sk, v))

        eng_ops = self.eng_ops

        def make_section(ename):
            ops = eng_ops[ename]
            tails = tail_by_engine.get(ename, [])

            def section(eng):
                waited = {}
                for o in ops:
                    tgt = {}
                    for d in o.deps:
                        sk = d.sem_key
                        if tgt.get(sk, -1) < d.sem_val:
                            tgt[sk] = d.sem_val
                    for sk, v in sorted(tgt.items(), key=str):
                        if waited.get(sk, -1) >= v:
                            continue
                        waited[sk] = v
                        eng.wait_ge(sems[sk], v)
                    ins = o.emit(eng)
                    if o.milestone:
                        ins.then_inc(sems[o.sem_key],
                                     16 if o.kind == "dma" else 1)
                for sk, v in tails:
                    if waited.get(sk, -1) < v:
                        eng.wait_ge(sems[sk], v)
            return section

        with self.nc.Block() as block:
            for ename in ENGINES:
                if not eng_ops[ename]:
                    continue
                getattr(block, ename)(make_section(ename))
        return self

    def run(self, in_maps, core_ids=None, **kw):
        res = run_bass_kernel_spmd(self.nc, in_maps,
                                   core_ids=core_ids or list(range(N_CORES)), **kw)
        self.stack.close()
        return res


class Ctx:

    def __init__(self, B: Builder):
        self.B = B
        self.nc = B.nc
        self.pool = {}
        self.uid = 0

    def nm(self, base):
        self.uid += 1
        return f"{base}_{self.uid}"

    def sb(self, key, shape, dtype):
        if key not in self.pool:
            self.pool[key] = self.B.sbuf(key, shape, dtype)
        t = self.pool[key]
        return t

    def ps(self, key, shape=(128, 512), dtype=F32):
        if key not in self.pool:
            self.pool[key] = self.B.psum(key, shape, dtype)
        return self.pool[key]

    def onescol_f32(self):
        new = "c_ones_col32" not in self.pool
        t = self.sb("c_ones_col32", [128, 1], F32)
        if new:
            self.B.op("vector", lambda e: e.memset(t[:, :], 1.0),
                      writes=["c_ones_col32"])
        return t

    def onesrow_f32(self):
        new = "c_ones_row32" not in self.pool
        t = self.sb("c_ones_row32", [1, 128], F32)
        if new:
            self.B.op("vector", lambda e: e.memset(t[:, :], 1.0),
                      writes=["c_ones_row32"])
        return t

    def ones_col(self):
        new = "c_ones_col" not in self.pool
        t = self.sb("c_ones_col", [128, 1], BF16)
        if new:
            self.B.op("vector", lambda e: e.memset(t[:, :], 1.0),
                      writes=["c_ones_col"])
        return t

    def ones_row(self):
        new = "c_ones_row" not in self.pool
        t = self.sb("c_ones_row", [1, 128], BF16)
        if new:
            self.B.op("vector", lambda e: e.memset(t[:, :], 1.0),
                      writes=["c_ones_row"])
        return t

    def constcol(self, value, parts=1, dtype=F32):
        key = f"c_col_{parts}_{dtype}_{value}".replace(".", "p").replace("-", "m")
        new = key not in self.pool
        t = self.sb(key, [parts, 1], dtype)
        if new:
            self.B.op("vector", lambda e: e.memset(t[:, :], value), writes=[key])
        return t

    def bcast_row(self, row_ap, row_key, dst, dst_key, n, ps_key="psG"):
        B = self.B
        ps = self.ps(ps_key)
        ones = self.ones_row()
        B.op("tensor", lambda e: e.matmul(ps[:, 0:n], ones[:, :], row_ap,
                                          start=True, stop=True),
             reads=[row_key, "c_ones_row"], writes=[ps_key])
        B.op("vector", lambda e: e.tensor_copy(out=dst, in_=ps[:, 0:n]),
             reads=[ps_key], writes=[dst_key])



def shard_tokens(x_btd):
    out = []
    for c in range(N_CORES):
        b = c // 4
        blks = core_blocks(c)
        out.append(np.ascontiguousarray(np.concatenate(
            [x_btd[b, blk * BLK:(blk + 1) * BLK] for blk in blks], axis=0)))
    return out


def unshard_tokens(shards):
    D = shards[0].shape[-1]
    x = np.empty((B_SZ, T_SEQ, D), dtype=np.asarray(shards[0]).dtype)
    for c in range(N_CORES):
        b = c // 4
        blks = core_blocks(c)
        for i, blk in enumerate(blks):
            x[b, blk * BLK:(blk + 1) * BLK] = shards[c][i * BLK:(i + 1) * BLK]
    return x


def sinusoidal_pe(t, d):
    pos = np.arange(t, dtype=np.float64)[:, None]
    i = np.arange(0, d, 2, dtype=np.float64)
    inv_freq = np.exp(-(np.log(10000.0) * i / d))
    ang = pos * inv_freq
    pe = np.zeros((t, d), dtype=np.float64)
    pe[:, 0::2] = np.sin(ang)
    pe[:, 1::2] = np.cos(ang)
    return pe.astype(np.float32)


def hadamard(n):
    h = np.array([[1.0]], dtype=np.float64)
    while h.shape[0] < n:
        h = np.block([[h, h], [h, -h]])
    return h.astype(np.float32)



def build_ln(ctx, x_sb, x_key, xn_sb, xn_key, tok, d_model=D_MODEL):
    B = ctx.B
    ntile = d_model // 128
    onec = ctx.onescol_f32()
    st_s = ctx.ps("psC")
    st_q = ctx.ps("psD")
    for i in range(ntile):
        B.op("tensor",
             lambda e, i=i: e.matmul(st_s[0:1, 0:tok], onec[:, :],
                                     x_sb[:, i * tok:(i + 1) * tok],
                                     start=(i == 0), stop=(i == ntile - 1)),
             reads=[x_key, "c_ones_col32"], writes=["psC"])
    sq = ctx.sb("ln_sq4", [128, 4 * 512], F32)
    qtr = ntile // 4
    for g in range(4):
        for i in range(qtr):
            B.op("scalar",
                 lambda e, g=g, i=i: e.activation(
                     out=sq[:, i * tok:(i + 1) * tok],
                     in_=x_sb[:, (g * qtr + i) * tok:(g * qtr + i + 1) * tok],
                     func=AF.Square),
                 reads=[x_key], writes=["ln_sq4"])
        for i in range(qtr):
            B.op("tensor",
                 lambda e, g=g, i=i: e.matmul(st_q[0:1, 0:tok], onec[:, :],
                                              sq[:, i * tok:(i + 1) * tok],
                                              start=(g == 0 and i == 0),
                                              stop=(g == 3 and i == qtr - 1)),
                 reads=["ln_sq4", "c_ones_col32"], writes=["psD"])
    mean = ctx.sb("ln_mean", [1, 512], F32)
    msq = ctx.sb("ln_msq", [1, 512], F32)
    var = ctx.sb("ln_var", [1, 512], F32)
    rstd = ctx.sb("ln_rstd", [1, 512], F32)
    nmr = ctx.sb("ln_nmr", [1, 512], F32)
    inv_d = 1.0 / d_model
    B.op("scalar", lambda e: e.activation(out=mean[:, 0:tok], in_=st_s[0:1, 0:tok],
                                          func=AF.Copy, scale=inv_d),
         reads=["psC"], writes=["ln_mean"])
    B.op("scalar", lambda e: e.activation(out=msq[:, 0:tok], in_=st_q[0:1, 0:tok],
                                          func=AF.Copy, scale=inv_d),
         reads=["psD"], writes=["ln_msq"])
    B.op("vector", lambda e: e.tensor_mul(out=var[:, 0:tok], in0=mean[:, 0:tok],
                                          in1=mean[:, 0:tok]),
         reads=["ln_mean"], writes=["ln_var"])
    B.op("vector", lambda e: e.tensor_sub(out=var[:, 0:tok], in0=msq[:, 0:tok],
                                          in1=var[:, 0:tok]),
         reads=["ln_msq", "ln_var"], writes=["ln_var"])
    epsc = ctx.constcol(1e-5)
    B.op("scalar", lambda e: e.activation(out=var[:, 0:tok], in_=var[:, 0:tok],
                                          func=AF.Ln, bias=epsc[:, :]),
         reads=["ln_var", epsc.name], writes=["ln_var"])
    B.op("scalar", lambda e: e.activation(out=rstd[:, 0:tok], in_=var[:, 0:tok],
                                          func=AF.Exp, scale=-0.5),
         reads=["ln_var"], writes=["ln_rstd"])
    B.op("vector", lambda e: e.tensor_mul(out=nmr[:, 0:tok], in0=mean[:, 0:tok],
                                          in1=rstd[:, 0:tok]),
         reads=["ln_mean", "ln_rstd"], writes=["ln_nmr"])
    B.op("scalar", lambda e: e.activation(out=nmr[:, 0:tok], in_=nmr[:, 0:tok],
                                          func=AF.Copy, scale=-1.0),
         reads=["ln_nmr"], writes=["ln_nmr"])
    rstd_b = ctx.sb("ln_rstd_b", [128, 512], F32)
    nmr_b = ctx.sb("ln_nmr_b", [128, 512], F32)
    onerow = ctx.onesrow_f32()
    bc1 = ctx.ps("psG")
    bc2 = ctx.ps("psH")
    B.op("tensor", lambda e: e.matmul(bc1[:, 0:tok], onerow[:, :], rstd[:, 0:tok],
                                      start=True, stop=True),
         reads=["ln_rstd", "c_ones_row32"], writes=["psG"])
    B.op("vector", lambda e: e.tensor_copy(out=rstd_b[:, 0:tok], in_=bc1[:, 0:tok]),
         reads=["psG"], writes=["ln_rstd_b"])
    B.op("tensor", lambda e: e.matmul(bc2[:, 0:tok], onerow[:, :], nmr[:, 0:tok],
                                      start=True, stop=True),
         reads=["ln_nmr", "c_ones_row32"], writes=["psH"])
    B.op("vector", lambda e: e.tensor_copy(out=nmr_b[:, 0:tok], in_=bc2[:, 0:tok]),
         reads=["psH"], writes=["ln_nmr_b"])
    t32 = ctx.sb("ln_t32", [128, 512], F32)
    for i in range(ntile):
        B.op("vector",
             lambda e, i=i: e.tensor_mul(out=t32[:, 0:tok],
                                         in0=x_sb[:, i * tok:(i + 1) * tok],
                                         in1=rstd_b[:, 0:tok]),
             reads=[x_key, "ln_rstd_b"], writes=["ln_t32"])
        B.op("vector",
             lambda e, i=i: e.tensor_add(out=xn_sb[:, i * tok:(i + 1) * tok],
                                         in0=t32[:, 0:tok],
                                         in1=nmr_b[:, 0:tok]),
             reads=["ln_t32", "ln_nmr_b"], writes=[xn_key])



BIG_WEIGHTS = {
    "w1T": (2048, 5632),
    "w3T": (2048, 5632),
    "w2T": (5632, 2048),
    "wupT": (2048, 8192),
    "wgateT": (2048, 8192),
    "wdownT": (8192, 2048),
    "b1T": (16, 8 * 5632),
    "b3T": (16, 8 * 5632),
    "a2T": (5632, 128),
}


def build_wprep(B: Builder, io=None):
    if io is None:
        io = {}
        for name, (din, dout) in BIG_WEIGHTS.items():
            io[name] = B.dram(name, [din, dout], F32, kind="ExternalInput")
            io[name + "_bf"] = B.dram(name + "_bf", [din, dout], BF16,
                                      kind="ExternalOutput")
    bounce = [B.sbuf("wp_bounce0", [128, 8192], BF16),
              B.sbuf("wp_bounce1", [128, 8192], BF16)]
    it = 0
    for name, (din, dout) in BIG_WEIGHTS.items():
        srcd = io[name].ap()
        dst = io[name + "_bf"].ap()
        p = min(din, 128)
        c = min(dout, 8192)
        for a in range((din + 127) // 128):
            pr = min(128, din - a * 128)
            for j0 in range(0, dout, c):
                w = min(c, dout - j0)
                bb = bounce[it % 2]
                kb = f"wp_bounce{it % 2}"
                it += 1
                B.dma("gpsimd", bb[0:pr, 0:w],
                      srcd[a * 128:a * 128 + pr, j0:j0 + w],
                      reads=[name], writes=[kb])
                B.dma("sync", dst[a * 128:a * 128 + pr, j0:j0 + w], bb[0:pr, 0:w],
                      reads=[kb], writes=[name + "_bf"])
    return B


def host_big_weights(inputs):
    t = lambda a: np.ascontiguousarray(np.asarray(a).T, dtype=np.float32)
    b1 = np.asarray(inputs["L0_ffn_b1"], np.float32)
    b3 = np.asarray(inputs["L0_ffn_b3"], np.float32)
    a2 = np.asarray(inputs["L0_ffn_a2"], np.float32)
    return {
        "w1T": t(inputs["L0_ffn_w1"]),
        "w3T": t(inputs["L0_ffn_w3"]),
        "w2T": t(inputs["L0_ffn_w2"]),
        "wupT": t(inputs["L1_ffn_W_up"]),
        "wgateT": t(inputs["L1_ffn_W_gate"]),
        "wdownT": t(inputs["L1_ffn_W_down"]),
        "b1T": np.ascontiguousarray(b1.transpose(2, 0, 1).reshape(16, 8 * 5632)),
        "b3T": np.ascontiguousarray(b3.transpose(2, 0, 1).reshape(16, 8 * 5632)),
        "a2T": np.ascontiguousarray(a2.transpose(2, 0, 1).reshape(5632, 128)),
    }


def col_layout(v, n=None):
    v = np.asarray(v, dtype=np.float32).reshape(-1)
    a = v.shape[0] // 128
    return np.ascontiguousarray(v.reshape(a, 128).T)





def butterfly_group8(ctx, buf, kbuf, tile_idx, tok):
    B = ctx.B
    s0 = ctx.sb("bfly_s0", [128, 8 * 512], F32)
    s1 = ctx.sb("bfly_s1", [128, 8 * 512], F32)
    for j, ti in enumerate(tile_idx):
        B.op("vector",
             lambda e, j=j, ti=ti: e.tensor_copy(
                 out=s0[:, j * tok:(j + 1) * tok],
                 in_=buf[:, ti * tok:(ti + 1) * tok]),
             reads=[kbuf], writes=["bfly_s0"])
    cur, ck, alt, ak = s0, "bfly_s0", s1, "bfly_s1"
    for h in (1, 2, 4):
        for base in range(0, 8, 2 * h):
            for o in range(h):
                j0, j1 = base + o, base + o + h
                B.op("vector",
                     lambda e, j0=j0, j1=j1, cur=cur, alt=alt: e.tensor_add(
                         out=alt[:, j0 * tok:(j0 + 1) * tok],
                         in0=cur[:, j0 * tok:(j0 + 1) * tok],
                         in1=cur[:, j1 * tok:(j1 + 1) * tok]),
                     reads=[ck], writes=[ak])
                B.op("vector",
                     lambda e, j0=j0, j1=j1, cur=cur, alt=alt: e.tensor_sub(
                         out=alt[:, j1 * tok:(j1 + 1) * tok],
                         in0=cur[:, j0 * tok:(j0 + 1) * tok],
                         in1=cur[:, j1 * tok:(j1 + 1) * tok]),
                     reads=[ck], writes=[ak])
        cur, ck, alt, ak = alt, ak, cur, ck
    for j, ti in enumerate(tile_idx):
        B.op("vector",
             lambda e, j=j, ti=ti, cur=cur: e.tensor_copy(
                 out=buf[:, ti * tok:(ti + 1) * tok],
                 in_=cur[:, j * tok:(j + 1) * tok]),
             reads=[ck], writes=[kbuf])


def butterfly_inplace(ctx, buf, kbuf, ntiles, strides, tok):
    rem = list(strides)
    while rem:
        take = rem[:3]
        rem = rem[3:]
        base_stride = take[0]
        span = take[-1] * 2
        s = base_stride
        nblk = 8 * s if len(take) == 3 else (4 * s if len(take) == 2 else 2 * s)
        per = nblk // s
        for blk0 in range(0, ntiles, nblk):
            for o in range(s):
                idx = [blk0 + o + j * s for j in range(per)]
                if per == 8:
                    butterfly_group8(ctx, buf, kbuf, idx, tok)
                else:
                    butterfly_small(ctx, buf, kbuf, idx, tok)


def butterfly_small(ctx, buf, kbuf, tile_idx, tok):
    B = ctx.B
    n = len(tile_idx)
    s0 = ctx.sb("bfly_s0", [128, 8 * 512], F32)
    s1 = ctx.sb("bfly_s1", [128, 8 * 512], F32)
    for j, ti in enumerate(tile_idx):
        B.op("vector",
             lambda e, j=j, ti=ti: e.tensor_copy(
                 out=s0[:, j * tok:(j + 1) * tok],
                 in_=buf[:, ti * tok:(ti + 1) * tok]),
             reads=[kbuf], writes=["bfly_s0"])
    cur, ck, alt, ak = s0, "bfly_s0", s1, "bfly_s1"
    h = 1
    while h < n:
        for base in range(0, n, 2 * h):
            for o in range(h):
                j0, j1 = base + o, base + o + h
                B.op("vector",
                     lambda e, j0=j0, j1=j1, cur=cur, alt=alt: e.tensor_add(
                         out=alt[:, j0 * tok:(j0 + 1) * tok],
                         in0=cur[:, j0 * tok:(j0 + 1) * tok],
                         in1=cur[:, j1 * tok:(j1 + 1) * tok]),
                     reads=[ck], writes=[ak])
                B.op("vector",
                     lambda e, j0=j0, j1=j1, cur=cur, alt=alt: e.tensor_sub(
                         out=alt[:, j1 * tok:(j1 + 1) * tok],
                         in0=cur[:, j0 * tok:(j0 + 1) * tok],
                         in1=cur[:, j1 * tok:(j1 + 1) * tok]),
                     reads=[ck], writes=[ak])
        cur, ck, alt, ak = alt, ak, cur, ck
        h *= 2
    for j, ti in enumerate(tile_idx):
        B.op("vector",
             lambda e, j=j, ti=ti, cur=cur: e.tensor_copy(
                 out=buf[:, ti * tok:(ti + 1) * tok],
                 in_=cur[:, j * tok:(j + 1) * tok]),
             reads=[ck], writes=[kbuf])




def build_f1_chunk(ctx: Ctx, chunk, tok=512):
    B = ctx.B
    io = ctx.io
    t0 = chunk * tok
    INTER = 8192
    x_sb = ctx.sb("f1_x", [128, 16 * tok], F32)
    kx = "f1_x"
    xv = io["x_in"].ap().rearrange("(a p) t -> p a t", p=128)
    B.dma("sync", x_sb[:].rearrange("p (a t) -> p a t", a=16),
          xv[:, :, t0:t0 + tok], reads=["x_in"], writes=[kx])
    xn = ctx.sb("f1_xn", [128, 16 * tok], BF16)
    kxn = "f1_xn"
    build_ln(ctx, x_sb, kx, xn, kxn, tok)
    for i in range(16):
        B.op("vector",
             lambda e, i=i: e.tensor_scalar(
                 out=xn[:, i * tok:(i + 1) * tok], in0=xn[:, i * tok:(i + 1) * tok],
                 scalar1=io["p_su_ug"][:, i:i + 1], scalar2=None, op0=ALU.mult),
             reads=[kxn, "p_su_ug"], writes=[kxn])
    psh = ctx.ps("psA")
    for i in range(16):
        B.op("tensor",
             lambda e, i=i: e.matmul(psh[:, 0:tok], ctx.pool["c_h128a"][:, :],
                                     xn[:, i * tok:(i + 1) * tok],
                                     start=True, stop=True),
             reads=[kxn, "c_h128a"], writes=["psA"])
        B.op("vector",
             lambda e, i=i: e.tensor_copy(out=xn[:, i * tok:(i + 1) * tok],
                                          in_=psh[:, 0:tok]),
             reads=["psA"], writes=[kxn])
    butterfly_inplace(ctx, xn, kxn, 16, (1, 2, 4, 8), tok)
    fw, kfw = xn, kxn
    hbuf = ctx.sb("f1_hbuf", [128, 64 * tok], BF16)
    khb = "f1_hbuf"
    upsl = [ctx.sb("f1_upsl0", [128, 16 * 512], BF16)]
    gtsl = [ctx.sb("f1_gtsl0", [128, 16 * 512], BF16)]
    wupv = io["wupT_bf"].ap().rearrange("(a p) n -> p a n", p=128)
    wgtv = io["wgateT_bf"].ap().rearrange("(a p) n -> p a n", p=128)
    psu = ctx.ps("psB")
    psg = ctx.ps("psE")
    tup = ctx.sb("f1_tup", [128, 512], F32)
    tgt = ctx.sb("f1_tgt", [128, 512], F32)
    for s in range(16):
        sl = 0
        ku, kg = "f1_upsl0", "f1_gtsl0"
        B.dma("sync", upsl[sl][:].rearrange("p (a n) -> p a n", a=16),
              wupv[:, :, s * 512:(s + 1) * 512], reads=["wupT_bf"], writes=[ku])
        B.dma("scalar", gtsl[sl][:].rearrange("p (a n) -> p a n", a=16),
              wgtv[:, :, s * 512:(s + 1) * 512], reads=["wgateT_bf"], writes=[kg])
        for dc in range(4):
            dcg = s * 4 + dc
            for a in range(16):
                B.op("tensor",
                     lambda e, a=a, dc=dc, sl=sl: e.matmul(
                         psu[:, 0:tok], upsl[sl][:, a * 512 + dc * 128:a * 512 + (dc + 1) * 128],
                         fw[:, a * tok:(a + 1) * tok],
                         start=(a == 0), stop=(a == 15)),
                     reads=[ku, kfw], writes=["psB"])
            for a in range(16):
                B.op("tensor",
                     lambda e, a=a, dc=dc, sl=sl: e.matmul(
                         psg[:, 0:tok], gtsl[sl][:, a * 512 + dc * 128:a * 512 + (dc + 1) * 128],
                         fw[:, a * tok:(a + 1) * tok],
                         start=(a == 0), stop=(a == 15)),
                     reads=[kg, kfw], writes=["psE"])
            B.op("scalar",
                 lambda e, dcg=dcg: e.activation(
                     out=tup[:, 0:tok], in_=psu[:, 0:tok], func=AF.Copy,
                     scale=io["p_up_eff"][:, dcg:dcg + 1]),
                 reads=["psB", "p_up_eff"], writes=["f1_tup"])
            B.op("scalar",
                 lambda e, dcg=dcg: e.activation(
                     out=tgt[:, 0:tok], in_=psg[:, 0:tok], func=AF.Silu,
                     scale=io["p_gate_eff"][:, dcg:dcg + 1]),
                 reads=["psE", "p_gate_eff"], writes=["f1_tgt"])
            B.op("vector",
                 lambda e, dcg=dcg: e.tensor_mul(
                     out=hbuf[:, dcg * tok:(dcg + 1) * tok],
                     in0=tgt[:, 0:tok], in1=tup[:, 0:tok]),
                 reads=["f1_tup", "f1_tgt"], writes=[khb])
    for i in range(64):
        B.op("tensor",
             lambda e, i=i: e.matmul(psh[:, 0:tok], ctx.pool["c_h128b"][:, :],
                                     hbuf[:, i * tok:(i + 1) * tok],
                                     start=True, stop=True),
             reads=[khb, "c_h128b"], writes=["psA"])
        B.op("vector",
             lambda e, i=i: e.tensor_copy(out=hbuf[:, i * tok:(i + 1) * tok],
                                          in_=psh[:, 0:tok]),
             reads=["psA"], writes=[khb])
    butterfly_inplace(ctx, hbuf, khb, 64, (1, 2, 4, 8, 16, 32), tok)
    fw8, kfw8 = hbuf, khb
    dsl = [ctx.sb("f1_upsl0", [128, 16 * 512], BF16)]
    wdnv = io["wdownT_bf"].ap().rearrange("(a p) n -> p a n", p=128)
    psd = ctx.ps("psF")
    tdn = ctx.sb("f1_tup", [128, 512], F32)
    xov = io["x_out"].ap().rearrange("(a p) t -> p a t", p=128)
    for s in range(16):
        sl = 0
        kd = "f1_upsl0"
        B.dma("sync", dsl[sl][:].rearrange("p (a n) -> p a n", a=64),
              wdnv[:, :, s * 128:(s + 1) * 128], reads=["wdownT_bf"], writes=[kd])
        for a in range(64):
            B.op("tensor",
                 lambda e, a=a, sl=sl: e.matmul(
                     psd[:, 0:tok], dsl[sl][:, a * 128:(a + 1) * 128],
                     fw8[:, a * tok:(a + 1) * tok],
                     start=(a == 0), stop=(a == 63)),
                 reads=[kd, kfw8], writes=["psF"])
        B.op("scalar",
             lambda e, s=s: e.activation(out=tdn[:, 0:tok], in_=psd[:, 0:tok],
                                         func=AF.Copy,
                                         scale=io["p_dp_eff"][:, s:s + 1]),
             reads=["psF", "p_dp_eff"], writes=["f1_tup"])
        B.op("vector",
             lambda e, s=s: e.tensor_add(out=x_sb[:, s * tok:(s + 1) * tok],
                                         in0=x_sb[:, s * tok:(s + 1) * tok],
                                         in1=tdn[:, 0:tok]),
             reads=["f1_tup"], writes=[kx])
        B.dma("scalar", xov[:, s, t0:t0 + tok], x_sb[:, s * tok:(s + 1) * tok],
              reads=[kx], writes=["x_out"])


def build_f1(B: Builder, io=None):
    ctx = Ctx(B)
    if io is None:
        io = {}
        io["x_in"] = B.dram("x_in", [D_MODEL, TOK_PER_CORE], F32,
                            kind="ExternalInput")
        io["x_out"] = B.dram("x_out", [D_MODEL, TOK_PER_CORE], F32,
                             kind="ExternalOutput")
        io["wupT_bf"] = B.dram("wupT_bf", [2048, 8192], BF16, kind="ExternalInput")
        io["wgateT_bf"] = B.dram("wgateT_bf", [2048, 8192], BF16,
                                 kind="ExternalInput")
        io["wdownT_bf"] = B.dram("wdownT_bf", [8192, 2048], BF16,
                                 kind="ExternalInput")
    ctx.io = io
    par = {}
    for nm, cols in (("p_su_ug", 16), ("p_su_dp", 64), ("p_wsu_up", 64),
                     ("p_wsu_gate", 64), ("p_wsdp", 16)):
        d = io.get(nm + "_d") or B.dram(nm + "_d", [128, cols], F32,
                                        kind="ExternalInput")
        t = ctx.sb(nm if nm in ("p_su_ug",) else nm + "_s", [128, cols], F32)
        B.dma("gpsimd", t[:, :], d.ap(), reads=[], writes=[t.name])
        par[nm] = t
        io[nm] = t
    for nm in ("c_h128a", "c_h128b"):
        d = io.get(nm + "_d") or B.dram(nm + "_d", [128, 128], F32,
                                        kind="ExternalInput")
        t = ctx.sb(nm, [128, 128], BF16)
        B.dma("gpsimd", t[:, :], d.ap(), reads=[], writes=[nm])
    up_eff = ctx.sb("p_up_eff", [128, 64], F32)
    B.op("vector", lambda e: e.tensor_mul(out=up_eff[:, :], in0=par["p_wsu_up"][:, :],
                                          in1=par["p_su_dp"][:, :]),
         reads=[par["p_wsu_up"].name, par["p_su_dp"].name], writes=["p_up_eff"])
    B.op("scalar", lambda e: e.activation(out=up_eff[:, :], in_=up_eff[:, :],
                                          func=AF.Copy, scale=64.0),
         reads=["p_up_eff"], writes=["p_up_eff"])
    gt_eff = ctx.sb("p_gate_eff", [128, 64], F32)
    B.op("scalar", lambda e: e.activation(out=gt_eff[:, :],
                                          in_=par["p_wsu_gate"][:, :],
                                          func=AF.Copy, scale=64.0),
         reads=[par["p_wsu_gate"].name], writes=["p_gate_eff"])
    dp_eff = ctx.sb("p_dp_eff", [128, 16], F32)
    B.op("scalar", lambda e: e.activation(out=dp_eff[:, :], in_=par["p_wsdp"][:, :],
                                          func=AF.Copy, scale=64.0),
         reads=[par["p_wsdp"].name], writes=["p_dp_eff"])
    io["p_up_eff"] = up_eff
    io["p_gate_eff"] = gt_eff
    io["p_dp_eff"] = dp_eff
    for chunk in range(2):
        build_f1_chunk(ctx, chunk, tok=512)
    return B


def host_f1_inputs(inputs, wbf, x_shard_fm):
    I = 8192
    ws = np.asarray(inputs["L1_ffn_Wscale_ug"], np.float32)
    return {
        "x_in": x_shard_fm,
        "wupT_bf": wbf["wupT_bf"],
        "wgateT_bf": wbf["wgateT_bf"],
        "wdownT_bf": wbf["wdownT_bf"],
        "p_su_ug_d": col_layout(inputs["L1_ffn_SU_ug"]),
        "p_su_dp_d": col_layout(inputs["L1_ffn_SU_dp"]),
        "p_wsu_up_d": col_layout(ws[:I]),
        "p_wsu_gate_d": col_layout(ws[I:]),
        "p_wsdp_d": col_layout(inputs["L1_ffn_Wscale_dp"]),
        "c_h128a_d": hadamard(128) * np.float32(1.0 / (math.sqrt(2048) * 64.0)),
        "c_h128b_d": hadamard(128) * np.float32(1.0 / (math.sqrt(8192) * 64.0)),
    }



F_DIM = 5632
N_EXP = 8
N_FSL = 11


def build_f0_chunk(ctx: Ctx, chunk, tok=256):
    B = ctx.B
    io = ctx.io
    t0 = chunk * tok
    nfs = 4
    x_sb = ctx.sb("f0_x", [128, 16 * 256], F32)
    xv = io["x_in"].ap().rearrange("(a p) t -> p a t", p=128)
    B.dma("sync", x_sb[:].rearrange("p (a t) -> p a t", a=16),
          xv[:, :, t0:t0 + tok], reads=["x_in"], writes=["f0_x"])
    xn = ctx.sb("f0_xn", [128, 16 * 256], BF16)
    build_ln(ctx, x_sb, "f0_x", xn, "f0_xn", tok)
    psl = ctx.ps("psF")
    for a in range(16):
        B.op("tensor",
             lambda e, a=a: e.matmul(psl[0:8, 0:tok], io["gateT"][:, a * 8:(a + 1) * 8],
                                     xn[:, a * tok:(a + 1) * tok],
                                     start=(a == 0), stop=(a == 15)),
             reads=["f0_xn", "gateT"], writes=["psF"])
    lt_f = ctx.sb("f0_lt", [8, 256], F32)
    B.op("vector", lambda e: e.tensor_copy(out=lt_f[:, 0:tok], in_=psl[0:8, 0:tok]),
         reads=["psF"], writes=["f0_lt"])
    wfull = ctx.sb("f0_wfull", [128, 2 * 8], F32)
    pst = ctx.ps("psG")
    for s in range(tok // 128):
        B.op("tensor",
             lambda e, s=s: e.transpose(pst[:, 0:8], lt_f[0:8, s * 128:(s + 1) * 128],
                                        io["ident32"][0:8, 0:8]),
             reads=["f0_lt", "ident32"], writes=["psG"])
        lt = ctx.sb(f"f0_ltT{s}", [128, 8], F32)
        B.op("vector", lambda e, s=s: e.tensor_copy(out=lt[:, :], in_=pst[:, 0:8]),
             reads=["psG"], writes=[f"f0_ltT{s}"])
        m1 = ctx.sb(f"f0_m1{s}", [128, 1], F32)
        m2 = ctx.sb(f"f0_m2{s}", [128, 1], F32)
        eq1 = ctx.sb(f"f0_eq1{s}", [128, 8], F32)
        eq2 = ctx.sb(f"f0_eq2{s}", [128, 8], F32)
        l2 = ctx.sb(f"f0_l2{s}", [128, 8], F32)
        B.op("vector", lambda e, s=s: e.reduce_max(out=m1[:, :], in_=lt[:, :],
                                                   axis=AX.X),
             reads=[f"f0_ltT{s}"], writes=[f"f0_m1{s}"])
        B.op("vector", lambda e, s=s: e.tensor_scalar(out=eq1[:, :], in0=lt[:, :],
                                                      scalar1=m1[:, :], scalar2=None,
                                                      op0=ALU.is_ge),
             reads=[f"f0_ltT{s}", f"f0_m1{s}"], writes=[f"f0_eq1{s}"])
        B.op("vector", lambda e, s=s: e.scalar_tensor_tensor(
                 out=l2[:, :], in0=eq1[:, :], scalar=-1e30, in1=lt[:, :],
                 op0=ALU.mult, op1=ALU.add),
             reads=[f"f0_eq1{s}", f"f0_ltT{s}"], writes=[f"f0_l2{s}"])
        B.op("vector", lambda e, s=s: e.reduce_max(out=m2[:, :], in_=l2[:, :],
                                                   axis=AX.X),
             reads=[f"f0_l2{s}"], writes=[f"f0_m2{s}"])
        B.op("vector", lambda e, s=s: e.tensor_scalar(out=eq2[:, :], in0=l2[:, :],
                                                      scalar1=m2[:, :], scalar2=None,
                                                      op0=ALU.is_ge),
             reads=[f"f0_l2{s}", f"f0_m2{s}"], writes=[f"f0_eq2{s}"])
        dcol = ctx.sb(f"f0_d{s}", [128, 1], F32)
        B.op("vector", lambda e, s=s: e.tensor_sub(out=dcol[:, :], in0=m2[:, :],
                                                   in1=m1[:, :]),
             reads=[f"f0_m2{s}", f"f0_m1{s}"], writes=[f"f0_d{s}"])
        ec = ctx.sb(f"f0_e{s}", [128, 1], F32)
        onec1 = ctx.constcol(1.0, 128)
        B.op("scalar", lambda e, s=s: e.activation(out=ec[:, :], in_=dcol[:, :],
                                                   func=AF.Exp),
             reads=[f"f0_d{s}"], writes=[f"f0_e{s}"])
        w1c = ctx.sb(f"f0_w1{s}", [128, 1], F32)
        B.op("vector", lambda e, s=s: e.tensor_scalar(out=w1c[:, :], in0=ec[:, :],
                                                      scalar1=1.0, scalar2=None,
                                                      op0=ALU.add),
             reads=[f"f0_e{s}"], writes=[f"f0_w1{s}"])
        B.op("vector", lambda e, s=s: e.reciprocal(out=w1c[:, :], in_=w1c[:, :]),
             reads=[f"f0_w1{s}"], writes=[f"f0_w1{s}"])
        w2c = ctx.sb(f"f0_w2{s}", [128, 1], F32)
        B.op("vector", lambda e, s=s: e.tensor_mul(out=w2c[:, :], in0=ec[:, :],
                                                   in1=w1c[:, :]),
             reads=[f"f0_e{s}", f"f0_w1{s}"], writes=[f"f0_w2{s}"])
        B.op("vector", lambda e, s=s: e.tensor_scalar(
                 out=eq1[:, :], in0=eq1[:, :], scalar1=w1c[:, :], scalar2=None,
                 op0=ALU.mult),
             reads=[f"f0_eq1{s}", f"f0_w1{s}"], writes=[f"f0_eq1{s}"])
        B.op("vector", lambda e, s=s: e.tensor_scalar(
                 out=wfull[:, s * 8:(s + 1) * 8], in0=eq2[:, :],
                 scalar1=w2c[:, :], scalar2=None, op0=ALU.mult),
             reads=[f"f0_eq2{s}", f"f0_w2{s}"], writes=["f0_wfull"])
        B.op("vector", lambda e, s=s: e.tensor_add(
                 out=wfull[:, s * 8:(s + 1) * 8],
                 in0=wfull[:, s * 8:(s + 1) * 8], in1=eq1[:, :]),
             reads=[f"f0_eq1{s}"], writes=["f0_wfull"])
        B.op("tensor",
             lambda e, s=s: e.transpose(pst[0:8, 0:128],
                                        wfull[:, s * 8:(s + 1) * 8],
                                        io["ident32"][:, :]),
             reads=["f0_wfull", "ident32"], writes=["psG"])
        B.op("vector", lambda e, s=s: e.tensor_copy(out=lt_f[:, s * 128:(s + 1) * 128],
                                                    in_=pst[0:8, 0:128]),
             reads=["psG"], writes=["f0_lt"])
    webc = ctx.sb("f0_webc", [128, 8 * 256], F32)
    psb = ctx.ps("psH")
    for ee in range(N_EXP):
        B.op("tensor",
             lambda e, ee=ee: e.matmul(psb[:, 0:tok],
                                       io["sel32"][:, ee * 128:(ee + 1) * 128],
                                       lt_f[:, 0:tok], start=True, stop=True),
             reads=["f0_lt", "sel32"], writes=["psH"])
        B.op("vector",
             lambda e, ee=ee: e.tensor_copy(out=webc[:, ee * tok:(ee + 1) * tok],
                                            in_=psb[:, 0:tok]),
             reads=["psH"], writes=["f0_webc"])
    u1b = ctx.sb("f0_u1", [16, 8 * 256], BF16)
    u3b = ctx.sb("f0_u3", [16, 8 * 256], BF16)
    psu = ctx.ps("psC")
    psv = ctx.ps("psD")
    for ee in range(N_EXP):
        for a in range(16):
            B.op("tensor",
                 lambda e, ee=ee, a=a: e.matmul(
                     psu[0:16, 0:tok], io["a1l"][:, a * 128 + ee * 16:a * 128 + ee * 16 + 16],
                     xn[:, a * tok:(a + 1) * tok], start=(a == 0), stop=(a == 15)),
                 reads=["f0_xn", "a1l"], writes=["psC"])
        B.op("scalar",
             lambda e, ee=ee: e.activation(out=u1b[:, ee * tok:(ee + 1) * tok],
                                           in_=psu[0:16, 0:tok], func=AF.Copy,
                                           scale=2.0),
             reads=["psC"], writes=["f0_u1"])
        for a in range(16):
            B.op("tensor",
                 lambda e, ee=ee, a=a: e.matmul(
                     psv[0:16, 0:tok], io["a3l"][:, a * 128 + ee * 16:a * 128 + ee * 16 + 16],
                     xn[:, a * tok:(a + 1) * tok], start=(a == 0), stop=(a == 15)),
                 reads=["f0_xn", "a3l"], writes=["psD"])
        B.op("scalar",
             lambda e, ee=ee: e.activation(out=u3b[:, ee * tok:(ee + 1) * tok],
                                           in_=psv[0:16, 0:tok], func=AF.Copy,
                                           scale=2.0),
             reads=["psD"], writes=["f0_u3"])
    z_sb = ctx.sb("f0_z", [16, 8 * 256], F32)
    w1sl = ctx.sb("f0_w1sl", [128, 16 * 512], BF16)
    w3sl = ctx.sb("f0_w3sl", [128, 16 * 512], BF16)
    w2sl = ctx.sb("f0_w2sl", [128, 4 * 2048], BF16)
    b1sl = ctx.sb("f0_b1sl", [16, 8 * 512], BF16)
    b3sl = ctx.sb("f0_b3sl", [16, 8 * 512], BF16)
    a2sl = ctx.sb("f0_a2sl", [128, 4 * 128], BF16)
    base1 = ctx.sb("f0_base1", [128, 4 * 256], BF16)
    base3 = ctx.sb("f0_base3", [128, 4 * 256], BF16)
    acts = ctx.sb("f0_acts", [128, 4 * 256], F32)
    th1 = ctx.sb("f0_th1", [128, 256], F32)
    ts_ = ctx.sb("f0_ts", [128, 256], F32)
    th3 = ctx.sb("f0_th3", [128, 256], F32)
    tact = ctx.sb("f0_tact", [128, 256], F32)
    wact = ctx.sb("f0_wact", [128, 256], BF16)
    w1v = io["w1T_bf"].ap().rearrange("(a p) n -> p a n", p=128)
    w3v = io["w3T_bf"].ap().rearrange("(a p) n -> p a n", p=128)
    w2v = io["w2T_bf"].ap().rearrange("(a p) n -> p a n", p=128)
    b1v = io["b1T_bf"].ap().rearrange("r (ee f) -> r ee f", ee=8)
    b3v = io["b3T_bf"].ap().rearrange("r (ee f) -> r ee f", ee=8)
    a2v = io["a2T_bf"].ap().rearrange("(a p) c -> p a c", p=128)
    pb1 = ctx.ps("psA")
    pb3 = ctx.ps("psB")
    pz = ctx.ps("psE")
    for sl in range(N_FSL):
        f0 = sl * 512
        B.dma("sync", w1sl[:].rearrange("p (a n) -> p a n", a=16),
              w1v[:, :, f0:f0 + 512], reads=["w1T_bf"], writes=["f0_w1sl"])
        B.dma("scalar", w3sl[:].rearrange("p (a n) -> p a n", a=16),
              w3v[:, :, f0:f0 + 512], reads=["w3T_bf"], writes=["f0_w3sl"])
        B.dma("sync", w2sl[:].rearrange("p (a n) -> p a n", a=4),
              w2v[:, 4 * sl:4 * sl + 4, :], reads=["w2T_bf"], writes=["f0_w2sl"])
        B.dma("gpsimd", b1sl[:].rearrange("r (ee f) -> r ee f", ee=8),
              b1v[:, :, f0:f0 + 512], reads=["b1T_bf"], writes=["f0_b1sl"])
        B.dma("gpsimd", b3sl[:].rearrange("r (ee f) -> r ee f", ee=8),
              b3v[:, :, f0:f0 + 512], reads=["b3T_bf"], writes=["f0_b3sl"])
        B.dma("gpsimd", a2sl[:].rearrange("p (a c) -> p a c", a=4),
              a2v[:, 4 * sl:4 * sl + 4, :], reads=["a2T_bf"], writes=["f0_a2sl"])
        for j in range(nfs):
            for a in range(16):
                B.op("tensor",
                     lambda e, j=j, a=a: e.matmul(
                         pb1[:, 0:tok], w1sl[:, a * 512 + j * 128:a * 512 + (j + 1) * 128],
                         xn[:, a * tok:(a + 1) * tok], start=(a == 0), stop=(a == 15)),
                     reads=["f0_w1sl", "f0_xn"], writes=["psA"])
            B.op("scalar",
                 lambda e, j=j: e.activation(out=base1[:, j * tok:(j + 1) * tok],
                                             in_=pb1[:, 0:tok], func=AF.Copy),
                 reads=["psA"], writes=["f0_base1"])
            for a in range(16):
                B.op("tensor",
                     lambda e, j=j, a=a: e.matmul(
                         pb3[:, 0:tok], w3sl[:, a * 512 + j * 128:a * 512 + (j + 1) * 128],
                         xn[:, a * tok:(a + 1) * tok], start=(a == 0), stop=(a == 15)),
                     reads=["f0_w3sl", "f0_xn"], writes=["psB"])
            B.op("scalar",
                 lambda e, j=j: e.activation(out=base3[:, j * tok:(j + 1) * tok],
                                             in_=pb3[:, 0:tok], func=AF.Copy),
                 reads=["psB"], writes=["f0_base3"])
        for ee in range(N_EXP):
            for j in range(nfs):
                B.op("tensor",
                     lambda e, ee=ee, j=j: e.matmul(
                         pb1[:, 0:tok],
                         b1sl[:, ee * 512 + j * 128:ee * 512 + (j + 1) * 128],
                         u1b[:, ee * tok:(ee + 1) * tok], start=True, stop=True),
                     reads=["f0_b1sl", "f0_u1", "f0_base1"], writes=["psA"])
                B.op("tensor",
                     lambda e, ee=ee, j=j: e.matmul(
                         pb3[:, 0:tok],
                         b3sl[:, ee * 512 + j * 128:ee * 512 + (j + 1) * 128],
                         u3b[:, ee * tok:(ee + 1) * tok], start=True, stop=True),
                     reads=["f0_b3sl", "f0_u3", "f0_base3"], writes=["psB"])
                B.op("vector",
                     lambda e, j=j: e.tensor_add(out=th1[:, 0:tok], in0=pb1[:, 0:tok],
                                                 in1=base1[:, j * tok:(j + 1) * tok]),
                     reads=["psA", "f0_base1"], writes=["f0_th1"])
                B.op("scalar",
                     lambda e: e.activation(out=ts_[:, 0:tok], in_=th1[:, 0:tok],
                                            func=AF.Silu),
                     reads=["f0_th1"], writes=["f0_ts"])
                B.op("vector",
                     lambda e, j=j: e.tensor_add(out=th3[:, 0:tok], in0=pb3[:, 0:tok],
                                                 in1=base3[:, j * tok:(j + 1) * tok]),
                     reads=["psB", "f0_base3"], writes=["f0_th3"])
                B.op("vector",
                     lambda e: e.tensor_mul(out=tact[:, 0:tok], in0=ts_[:, 0:tok],
                                            in1=th3[:, 0:tok]),
                     reads=["f0_ts", "f0_th3"], writes=["f0_tact"])
                B.op("vector",
                     lambda e, ee=ee: e.tensor_mul(out=wact[:, 0:tok],
                                                   in0=tact[:, 0:tok],
                                                   in1=webc[:, ee * tok:(ee + 1) * tok]),
                     reads=["f0_tact", "f0_webc"], writes=["f0_wact"])
                if ee == 0:
                    B.op("vector",
                         lambda e, j=j: e.tensor_copy(out=acts[:, j * tok:(j + 1) * tok],
                                                      in_=wact[:, 0:tok]),
                         reads=["f0_wact"], writes=["f0_acts"])
                else:
                    B.op("vector",
                         lambda e, j=j: e.tensor_add(out=acts[:, j * tok:(j + 1) * tok],
                                                     in0=acts[:, j * tok:(j + 1) * tok],
                                                     in1=wact[:, 0:tok]),
                         reads=["f0_wact"], writes=["f0_acts"])
                B.op("tensor",
                     lambda e, ee=ee, j=j: e.matmul(
                         pz[0:16, 0:tok], a2sl[:, j * 128 + ee * 16:j * 128 + ee * 16 + 16],
                         wact[:, 0:tok], start=(j == 0), stop=(j == nfs - 1)),
                     reads=["f0_a2sl", "f0_wact"], writes=["psE"])
            if sl == 0:
                B.op("vector",
                     lambda e, ee=ee: e.tensor_copy(out=z_sb[:, ee * tok:(ee + 1) * tok],
                                                    in_=pz[0:16, 0:tok]),
                     reads=["psE"], writes=["f0_z"])
            else:
                B.op("vector",
                     lambda e, ee=ee: e.tensor_add(out=z_sb[:, ee * tok:(ee + 1) * tok],
                                                   in0=z_sb[:, ee * tok:(ee + 1) * tok],
                                                   in1=pz[0:16, 0:tok]),
                     reads=["psE"], writes=["f0_z"])
        actb = ctx.sb("f0_actb", [128, 4 * 256], BF16)
        for j in range(nfs):
            B.op("vector",
                 lambda e, j=j: e.tensor_copy(out=actb[:, j * tok:(j + 1) * tok],
                                              in_=acts[:, j * tok:(j + 1) * tok]),
                 reads=["f0_acts"], writes=["f0_actb"])
        for d in range(16):
            B.op("tensor",
                 lambda e, d=d, sl=sl: _mm_group(
                     e, pb1[:, 0:tok],
                     [(w2sl[:, j * 2048 + d * 128:j * 2048 + (d + 1) * 128],
                       actb[:, j * tok:(j + 1) * tok]) for j in range(nfs)]),
                 reads=["f0_w2sl", "f0_actb"], writes=["psA"])
            B.op("vector",
                 lambda e, d=d: e.tensor_add(out=x_sb[:, d * tok:(d + 1) * tok],
                                             in0=x_sb[:, d * tok:(d + 1) * tok],
                                             in1=pb1[:, 0:tok]),
                 reads=["psA"], writes=["f0_x"])
    zb = ctx.sb("f0_zb", [16, 8 * 256], BF16)
    for ee in range(N_EXP):
        B.op("scalar",
             lambda e, ee=ee: e.activation(out=zb[:, ee * tok:(ee + 1) * tok],
                                           in_=z_sb[:, ee * tok:(ee + 1) * tok],
                                           func=AF.Copy, scale=2.0),
             reads=["f0_z"], writes=["f0_zb"])
    xov = io["x_out"].ap().rearrange("(a p) t -> p a t", p=128)
    for d in range(16):
        for ee in range(N_EXP):
            B.op("tensor",
                 lambda e, d=d, ee=ee: e.matmul(
                     pb3[:, 0:tok],
                     io["b2l"][:, ee * 2048 + d * 128:ee * 2048 + (d + 1) * 128],
                     zb[:, ee * tok:(ee + 1) * tok], start=(ee == 0),
                     stop=(ee == N_EXP - 1)),
                 reads=["f0_zb", "b2l"], writes=["psB"])
        B.op("vector",
             lambda e, d=d: e.tensor_add(out=x_sb[:, d * tok:(d + 1) * tok],
                                         in0=x_sb[:, d * tok:(d + 1) * tok],
                                         in1=pb3[:, 0:tok]),
             reads=["psB"], writes=["f0_x"])
        B.dma("scalar", xov[:, d, t0:t0 + tok], x_sb[:, d * tok:(d + 1) * tok],
              reads=["f0_x"], writes=["x_out"])


def _mm_group(e, out_ap, pairs):
    ins = None
    n = len(pairs)
    for i, (lhsT, rhs) in enumerate(pairs):
        ins = e.matmul(out_ap, lhsT, rhs, start=(i == 0), stop=(i == n - 1))
    return ins


def build_f0(B: Builder, io=None):
    ctx = Ctx(B)
    if io is None:
        io = {}
        io["x_in"] = B.dram("x_in", [D_MODEL, TOK_PER_CORE], F32,
                            kind="ExternalInput")
        io["x_out"] = B.dram("x_out", [D_MODEL, TOK_PER_CORE], F32,
                             kind="ExternalOutput")
        for nm, shp in (("w1T_bf", (2048, 5632)), ("w3T_bf", (2048, 5632)),
                        ("w2T_bf", (5632, 2048)), ("b1T_bf", (16, 8 * 5632)),
                        ("b3T_bf", (16, 8 * 5632)), ("a2T_bf", (5632, 128))):
            io[nm] = B.dram(nm, list(shp), BF16, kind="ExternalInput")
    ctx.io = io
    gate_d = io.get("gateT_d") or B.dram("gateT_d", [2048, 8], F32,
                                          kind="ExternalInput")
    gate_s = ctx.sb("gateT", [128, 16 * 8], BF16)
    B.dma("gpsimd", gate_s[:].rearrange("p (a g) -> p a g", a=16),
          gate_d.ap().rearrange("(a p) g -> p a g", p=128), reads=[],
          writes=["gateT"])
    io["gateT"] = gate_s
    for nm, cols in (("a1l", 16 * 128), ("a3l", 16 * 128)):
        d = io.get(nm + "_d") or B.dram(nm + "_d", [2048, 128], F32,
                                        kind="ExternalInput")
        t = ctx.sb(nm, [128, cols], BF16)
        B.dma("gpsimd", t[:].rearrange("p (a c) -> p a c", a=16),
              d.ap().rearrange("(a p) c -> p a c", p=128), reads=[], writes=[nm])
        io[nm] = t
    b2_d = io.get("b2l_d") or B.dram("b2l_d", [16, 8 * 2048], F32,
                                      kind="ExternalInput")
    b2_s = ctx.sb("b2l", [16, 8 * 2048], BF16)
    B.dma("gpsimd", b2_s[:, :], b2_d.ap(), reads=[], writes=["b2l"])
    io["b2l"] = b2_s
    id_d = io.get("ident32_d") or B.dram("ident32_d", [128, 128], F32,
                                          kind="ExternalInput")
    id_s = ctx.sb("ident32", [128, 128], F32)
    B.dma("sync", id_s[:, :], id_d.ap(), reads=[], writes=["ident32"])
    io["ident32"] = id_s
    sel_d = io.get("sel32_d") or B.dram("sel32_d", [8, 8 * 128], F32,
                                         kind="ExternalInput")
    sel_s = ctx.sb("sel32", [8, 8 * 128], F32)
    B.dma("sync", sel_s[:, :], sel_d.ap(), reads=[], writes=["sel32"])
    io["sel32"] = sel_s
    for chunk in range(4):
        build_f0_chunk(ctx, chunk, tok=256)
    return B


def host_f0_inputs(inputs, wbf, x_shard_fm):
    a1 = np.asarray(inputs["L0_ffn_a1"], np.float32)
    a3 = np.asarray(inputs["L0_ffn_a3"], np.float32)
    b2 = np.asarray(inputs["L0_ffn_b2"], np.float32)
    sel = np.zeros((8, 8 * 128), np.float32)
    for ee in range(8):
        sel[ee, ee * 128:(ee + 1) * 128] = 1.0
    return {
        "x_in": x_shard_fm,
        "w1T_bf": wbf["w1T_bf"], "w3T_bf": wbf["w3T_bf"], "w2T_bf": wbf["w2T_bf"],
        "b1T_bf": wbf["b1T_bf"], "b3T_bf": wbf["b3T_bf"], "a2T_bf": wbf["a2T_bf"],
        "gateT_d": np.ascontiguousarray(np.asarray(inputs["L0_ffn_gate_w"],
                                                   np.float32).T),
        "a1l_d": np.ascontiguousarray(a1.transpose(2, 0, 1).reshape(2048, 128)),
        "a3l_d": np.ascontiguousarray(a3.transpose(2, 0, 1).reshape(2048, 128)),
        "b2l_d": np.ascontiguousarray(b2.transpose(2, 0, 1).reshape(16, 8 * 2048)),
        "ident32_d": np.eye(128, dtype=np.float32),
        "sel32_d": sel,
    }



def build_a2a(B: Builder, io=None):
    ctx = Ctx(B)
    if io is None:
        io = {}
        io["x_in"] = B.dram("x_in", [D_MODEL, TOK_PER_CORE], F32,
                            kind="ExternalInput")
        io["wqT"] = B.dram("wqT", [2048, 2048], F32, kind="ExternalInput")
        io["wkvT"] = B.dram("wkvT", [2048, 4096], F32, kind="ExternalInput")
        io["q_fm"] = B.dram("q_fm", [2048, TOK_PER_CORE], BF16,
                            kind="ExternalOutput")
        io["vkk"] = B.dram("vkk", [32, 64 * 33], F32, kind="ExternalOutput")
    q_out = io["q_fm"]
    vkk_out = io["vkk"]
    ctx.io = io
    wq_v = io["wqT"].ap().rearrange("(a p) n -> p a n", p=128)
    wkv_v = io["wkvT"].ap().rearrange("(a p) n -> p a n", p=128)
    qv = q_out.ap().rearrange("(a p) t -> p a t", p=128)
    wsl = ctx.sb("a2_wsl", [128, 16 * 512], BF16)
    ktok = ctx.sb("a2_ktok", [128, 8 * 2048], BF16)
    vtok = ctx.sb("a2_vtok", [128, 8 * 2048], BF16)
    pq = ctx.ps("psA")
    tq = ctx.sb("a2_tq", [128, 512], BF16)
    x_sb = ctx.sb("a2_x", [128, 16 * 512], F32)
    xn_tmp = ctx.sb("a2_xntmp", [128, 16 * 512], BF16)
    for chunk in range(2):
        t0 = chunk * 512
        xv = io["x_in"].ap().rearrange("(a p) t -> p a t", p=128)
        B.dma("sync", x_sb[:].rearrange("p (a t) -> p a t", a=16),
              xv[:, :, t0:t0 + 512], reads=["x_in"], writes=["a2_x"])
        build_ln(ctx, x_sb, "a2_x", xn_tmp, "a2_xntmp", 512)
        for s in range(4):
            B.dma("gpsimd", wsl[:].rearrange("p (a n) -> p a n", a=16),
                  wq_v[:, :, s * 512:(s + 1) * 512], reads=["wqT"], writes=["a2_wsl"])
            for dc in range(4):
                for a in range(16):
                    B.op("tensor",
                         lambda e, s=s, dc=dc, a=a: e.matmul(
                             pq[:, 0:512],
                             wsl[:, a * 512 + dc * 128:a * 512 + (dc + 1) * 128],
                             xn_tmp[:, a * 512:(a + 1) * 512],
                             start=(a == 0), stop=(a == 15)),
                         reads=["a2_wsl", "a2_xntmp"], writes=["psA"])
                B.op("scalar",
                     lambda e: e.activation(out=tq[:, 0:512], in_=pq[:, 0:512],
                                            func=AF.Relu),
                     reads=["psA"], writes=["a2_tq"])
                B.dma("scalar", qv[:, s * 4 + dc, t0:t0 + 512], tq[:, 0:512],
                      reads=["a2_tq"], writes=["q_fm"])
        for s in range(8):
            B.dma("gpsimd", wsl[:].rearrange("p (a n) -> p a n", a=16),
                  wkv_v[:, :, s * 512:(s + 1) * 512], reads=["wkvT"],
                  writes=["a2_wsl"])
            for ts in range(4):
                gsub = chunk * 4 + ts
                for a in range(16):
                    B.op("tensor",
                         lambda e, s=s, ts=ts, a=a: e.matmul(
                             pq[:, 0:512],
                             xn_tmp[:, a * 512 + ts * 128:a * 512 + ts * 128 + 128],
                             wsl[:, a * 512:(a + 1) * 512],
                             start=(a == 0), stop=(a == 15)),
                         reads=["a2_wsl", "a2_xntmp"], writes=["psA"])
                for half in range(4):
                    feat0 = s * 512 + half * 128
                    is_k = feat0 < 2048
                    col = gsub * 2048 + (feat0 % 2048)
                    dst, kdst = (ktok, "a2_ktok") if is_k else (vtok, "a2_vtok")
                    fn = AF.Relu if is_k else AF.Copy
                    B.op("scalar",
                         lambda e, half=half, col=col, dst=dst, fn=fn: e.activation(
                             out=dst[:, col:col + 128],
                             in_=pq[:, half * 128:(half + 1) * 128], func=fn),
                         reads=["psA"], writes=[kdst])
    vkk_sb = ctx.sb("a2_vkk", [32, 64 * 33], F32)
    pvk = ctx.ps("psB")
    onec = ctx.ones_col()
    for h in range(64):
        b = h // 4
        off = (h % 4) * 32
        for sub in range(8):
            B.op("tensor",
                 lambda e, b=b, off=off, sub=sub: e.matmul(
                     pvk[0:32, 0:32],
                     ktok[:, sub * 2048 + b * 128 + off:sub * 2048 + b * 128 + off + 32],
                     vtok[:, sub * 2048 + b * 128 + off:sub * 2048 + b * 128 + off + 32],
                     start=(sub == 0), stop=(sub == 7)),
                 reads=["a2_ktok", "a2_vtok"], writes=["psB"])
        for sub in range(8):
            B.op("tensor",
                 lambda e, b=b, off=off, sub=sub: e.matmul(
                     pvk[0:32, 32:33],
                     ktok[:, sub * 2048 + b * 128 + off:sub * 2048 + b * 128 + off + 32],
                     onec[:, :], start=(sub == 0), stop=(sub == 7)),
                 reads=["a2_ktok", "c_ones_col"], writes=["psB"])
        B.op("vector",
             lambda e, h=h: e.tensor_copy(out=vkk_sb[:, h * 33:(h + 1) * 33],
                                          in_=pvk[0:32, 0:33]),
             reads=["psB"], writes=["a2_vkk"])
    B.dma("scalar", vkk_out.ap(), vkk_sb[:, :], reads=["a2_vkk"], writes=["vkk"])
    return B


def build_a2b(B: Builder, io=None):
    ctx = Ctx(B)
    if io is None:
        io = {}
        io["x_in"] = B.dram("x_in", [D_MODEL, TOK_PER_CORE], F32,
                            kind="ExternalInput")
        io["x_out"] = B.dram("x_out", [D_MODEL, TOK_PER_CORE], F32,
                             kind="ExternalOutput")
        io["q_fm"] = B.dram("q_fm", [2048, TOK_PER_CORE], BF16,
                            kind="ExternalInput")
        io["wprojT"] = B.dram("wprojT", [2048, 2048], F32, kind="ExternalInput")
        io["bproj"] = B.dram("bproj", [128, 16], F32, kind="ExternalInput")
        for i in range(4):
            io[f"vkk{i}"] = B.dram(f"vkk{i}", [32, 64 * 33], F32,
                                   kind="ExternalInput").ap()
    ctx.io = io
    vkk = ctx.sb("b_vkk", [32, 64 * 33], F32)
    tmp = ctx.sb("b_vkktmp", [32, 64 * 33], F32)
    B.dma("sync", vkk[:, :], io["vkk0"], reads=["d_vkkag"], writes=["b_vkk"])
    for i in range(1, 4):
        B.dma("sync", tmp[:, :], io[f"vkk{i}"], reads=["d_vkkag"],
              writes=["b_vkktmp"])
        B.op("vector", lambda e: e.tensor_add(out=vkk[:, :], in0=vkk[:, :],
                                              in1=tmp[:, :]),
             reads=["b_vkktmp"], writes=["b_vkk"])
    vkb = ctx.sb("b_vkb", [32, 64 * 33], BF16)
    B.op("vector", lambda e: e.tensor_copy(out=vkb[:, :], in_=vkk[:, :]),
         reads=["b_vkk"], writes=["b_vkb"])
    bpr = ctx.sb("b_bpr", [128, 16], F32)
    B.dma("gpsimd", bpr[:, :], io["bproj"].ap(), reads=[], writes=["b_bpr"])
    qsb = ctx.sb("b_q", [128, 16 * 1024], BF16)
    B.dma("sync", qsb[:].rearrange("p (a t) -> p a t", a=16),
          io["q_fm"].ap().rearrange("(a p) t -> p a t", p=128), reads=[],
          writes=["b_q"])
    attn = ctx.sb("b_attn", [128, 16 * 1024], BF16)
    pn = ctx.ps("psA")
    pd = ctx.ps("psB")
    pbc = ctx.ps("psC")
    dn = ctx.sb("b_dn", [1, 512], F32)
    rb = ctx.sb("b_rb", [32, 512], BF16)
    onesr = ctx.onesrow_f32()
    qst = ctx.sb("b_qst", [32, 512], BF16)
    for h in range(64):
        b = h // 4
        off = (h % 4) * 32
        for tc in range(2):
            t0 = tc * 512
            B.op("vector",
                 lambda e, b=b, off=off, t0=t0: e.tensor_copy(
                     out=qst[:, 0:512],
                     in_=qsb[off:off + 32, b * 1024 + t0:b * 1024 + t0 + 512]),
                 reads=["b_q"], writes=["b_qst"])
            B.op("tensor",
                 lambda e, h=h: e.matmul(
                     pn[0:32, 0:512], vkb[:, h * 33:h * 33 + 32],
                     qst[:, 0:512], start=True, stop=True),
                 reads=["b_vkb", "b_qst"], writes=["psA"])
            B.op("tensor",
                 lambda e, h=h: e.matmul(
                     pd[0:1, 0:512], vkb[:, h * 33 + 32:h * 33 + 33],
                     qst[:, 0:512], start=True, stop=True),
                 reads=["b_vkb", "b_qst"], writes=["psB"])
            B.op("vector",
                 lambda e: e.tensor_scalar(out=dn[:, 0:512], in0=pd[0:1, 0:512],
                                           scalar1=1e-15, scalar2=None,
                                           op0=ALU.add),
                 reads=["psB"], writes=["b_dn"])
            B.op("vector", lambda e: e.reciprocal(out=dn[:, 0:512], in_=dn[:, 0:512]),
                 reads=["b_dn"], writes=["b_dn"])
            B.op("tensor",
                 lambda e: e.matmul(pbc[0:32, 0:512], onesr[:, 0:32], dn[:, 0:512],
                                    start=True, stop=True),
                 reads=["b_dn", "c_ones_row32"], writes=["psC"])
            B.op("vector", lambda e: e.tensor_copy(out=rb[:, 0:512],
                                                   in_=pbc[0:32, 0:512]),
                 reads=["psC"], writes=["b_rb"])
            B.op("vector",
                 lambda e, b=b, off=off, t0=t0: e.tensor_mul(
                     out=attn[off:off + 32, b * 1024 + t0:b * 1024 + t0 + 512],
                     in0=pn[0:32, 0:512], in1=rb[:, 0:512]),
                 reads=["psA", "b_rb"], writes=["b_attn"])
    x_sb = ctx.sb("b_x", [128, 16 * 512], F32)
    wsl = ctx.sb("b_wsl", [128, 16 * 512], BF16)
    wp_v = io["wprojT"].ap().rearrange("(a p) n -> p a n", p=128)
    xv = io["x_in"].ap().rearrange("(a p) t -> p a t", p=128)
    xov = io["x_out"].ap().rearrange("(a p) t -> p a t", p=128)
    po = ctx.ps("psD")
    to_ = ctx.sb("b_to", [128, 512], F32)
    for chunk in range(2):
        t0 = chunk * 512
        B.dma("sync", x_sb[:].rearrange("p (a t) -> p a t", a=16),
              xv[:, :, t0:t0 + 512], reads=["x_in"], writes=["b_x"])
        for s in range(4):
            B.dma("gpsimd", wsl[:].rearrange("p (a n) -> p a n", a=16),
                  wp_v[:, :, s * 512:(s + 1) * 512], reads=["wprojT"],
                  writes=["b_wsl"])
            for dc in range(4):
                d = s * 4 + dc
                for a in range(16):
                    B.op("tensor",
                         lambda e, s=s, dc=dc, a=a, t0=t0: e.matmul(
                             po[:, 0:512],
                             wsl[:, a * 512 + dc * 128:a * 512 + (dc + 1) * 128],
                             attn[:, a * 1024 + t0:a * 1024 + t0 + 512],
                             start=(a == 0), stop=(a == 15)),
                         reads=["b_wsl", "b_attn"], writes=["psD"])
                B.op("vector",
                     lambda e, d=d: e.tensor_add(out=x_sb[:, d * 512:(d + 1) * 512],
                                                 in0=x_sb[:, d * 512:(d + 1) * 512],
                                                 in1=po[:, 0:512]),
                     reads=["psD"], writes=["b_x"])
                B.op("vector",
                     lambda e, d=d: e.tensor_scalar(
                         out=x_sb[:, d * 512:(d + 1) * 512],
                         in0=x_sb[:, d * 512:(d + 1) * 512],
                         scalar1=bpr[:, d:d + 1], scalar2=None, op0=ALU.add),
                     reads=["b_bpr"], writes=["b_x"])
                B.dma("scalar", xov[:, d, t0:t0 + 512], x_sb[:, d * 512:(d + 1) * 512],
                      reads=["b_x"], writes=["x_out"])
    return B


def host_a2_inputs(inputs):
    wqkv = np.asarray(inputs["L0_attn_b_w_qkv"], np.float32)
    return {
        "wqT": np.ascontiguousarray(wqkv[0:2048].T),
        "wkvT": np.ascontiguousarray(wqkv[2048:6144].T),
        "wprojT": np.ascontiguousarray(
            np.asarray(inputs["L0_attn_b_w_proj"], np.float32).T),
        "bproj": col_layout(inputs["L0_attn_b_b_proj"]),
    }



def build_a4a(B: Builder, io=None):
    ctx = Ctx(B)
    if io is None:
        io = {}
        io["x_in"] = B.dram("x_in", [D_MODEL, TOK_PER_CORE], F32,
                            kind="ExternalInput")
        io["xln"] = B.dram("xln", [D_MODEL, TOK_PER_CORE], BF16,
                           kind="ExternalOutput")
        io["xdbl"] = B.dram("xdbl", [160, TOK_PER_CORE], F32,
                            kind="ExternalOutput")
        io["xprojT"] = B.dram("xprojT", [2048, 160], F32, kind="ExternalInput")
    xln_out = io["xln"]
    xdbl_out = io["xdbl"]
    ctx.io = io
    xp = ctx.sb("a4_xp", [128, 16 * 160], BF16)
    B.dma("gpsimd", xp[:].rearrange("p (a n) -> p a n", a=16),
          io["xprojT"].ap().rearrange("(a p) n -> p a n", p=128), reads=[],
          writes=["a4_xp"])
    x_sb = ctx.sb("a4_x", [128, 16 * 512], F32)
    xn = ctx.sb("a4_xn", [128, 16 * 512], BF16)
    pq = ctx.ps("psA")
    xlv = xln_out.ap().rearrange("(a p) t -> p a t", p=128)
    xdv = xdbl_out.ap()
    t32 = ctx.sb("a4_t32", [128, 512], F32)
    for chunk in range(2):
        t0 = chunk * 512
        xv = io["x_in"].ap().rearrange("(a p) t -> p a t", p=128)
        B.dma("sync", x_sb[:].rearrange("p (a t) -> p a t", a=16),
              xv[:, :, t0:t0 + 512], reads=["x_in"], writes=["a4_x"])
        build_ln(ctx, x_sb, "a4_x", xn, "a4_xn", 512)
        for a in range(16):
            B.dma("scalar", xlv[:, a, t0:t0 + 512], xn[:, a * 512:(a + 1) * 512],
                  reads=["a4_xn"], writes=["xln"])
        for a in range(16):
            B.op("tensor",
                 lambda e, a=a: e.matmul(pq[:, 0:512], xp[:, a * 160:a * 160 + 128],
                                         xn[:, a * 512:(a + 1) * 512],
                                         start=(a == 0), stop=(a == 15)),
                 reads=["a4_xp", "a4_xn"], writes=["psA"])
        B.op("vector", lambda e: e.tensor_copy(out=t32[:, 0:512], in_=pq[:, 0:512]),
             reads=["psA"], writes=["a4_t32"])
        B.dma("scalar", xdv[0:128, t0:t0 + 512], t32[:, 0:512],
              reads=["a4_t32"], writes=["xdbl"])
        for a in range(16):
            B.op("tensor",
                 lambda e, a=a: e.matmul(pq[0:32, 0:512],
                                         xp[:, a * 160 + 128:a * 160 + 160],
                                         xn[:, a * 512:(a + 1) * 512],
                                         start=(a == 0), stop=(a == 15)),
                 reads=["a4_xp", "a4_xn"], writes=["psA"])
        B.op("vector", lambda e: e.tensor_copy(out=t32[0:32, 0:512],
                                               in_=pq[0:32, 0:512]),
             reads=["psA"], writes=["a4_t32"])
        B.dma("scalar", xdv[128:160, t0:t0 + 512], t32[0:32, 0:512],
              reads=["a4_t32"], writes=["xdbl"])
    return B


def build_a4b(B: Builder, io=None):
    ctx = Ctx(B)
    if io is None:
        io = {}
        io["xsl"] = B.dram("xsl", [512, T_SEQ], BF16, kind="ExternalInput")
        io["dt"] = B.dram("dt", [128, T_SEQ], F32, kind="ExternalInput")
        io["bc"] = B.dram("bc", [32, T_SEQ], F32, kind="ExternalInput")
        io["dtpT"] = B.dram("dtpT", [128, 512], F32, kind="ExternalInput")
        io["dtb"] = B.dram("dtb", [128, 4], F32, kind="ExternalInput")
        io["alog"] = B.dram("alog", [128, 4 * 16], F32, kind="ExternalInput")
        io["dcol"] = B.dram("dcol", [128, 4], F32, kind="ExternalInput")
        io["y"] = B.dram("y", [512, T_SEQ], F32, kind="ExternalOutput")
    y_out = io["y"]
    ctx.io = io
    dtp = ctx.sb("s_dtp", [128, 512], BF16)
    B.dma("gpsimd", dtp[:, :], io["dtpT"].ap(), reads=[], writes=["s_dtp"])
    dtb = ctx.sb("s_dtb", [128, 4], F32)
    B.dma("sync", dtb[:, :], io["dtb"].ap(), reads=[], writes=["s_dtb"])
    an = ctx.sb("s_an", [128, 64], F32)
    B.dma("sync", an[:, :], io["alog"].ap(), reads=[], writes=["s_an"])
    B.op("scalar", lambda e: e.activation(out=an[:, :], in_=an[:, :], func=AF.Exp),
         reads=["s_an"], writes=["s_an"])
    B.op("scalar", lambda e: e.activation(out=an[:, :], in_=an[:, :], func=AF.Copy,
                                          scale=-1.0),
         reads=["s_an"], writes=["s_an"])
    dc = ctx.sb("s_dc", [128, 4], F32)
    B.dma("sync", dc[:, :], io["dcol"].ap(), reads=[], writes=["s_dc"])
    dtf = ctx.sb("s_dtf", [128, T_SEQ], BF16)
    B.dma("gpsimd", dtf[:, :], io["dt"].ap(), reads=[], writes=["s_dtf"])
    bcs = ctx.sb("s_bc", [32, T_SEQ], F32)
    B.dma("sync", bcs[:, :], io["bc"].ap(), reads=[], writes=["s_bc"])
    brow = ctx.sb("s_brow", [1, T_SEQ], F32)
    crow = ctx.sb("s_crow", [1, T_SEQ], F32)
    onesr = ctx.onesrow_f32()
    delta = ctx.sb("s_delta", [128, T_SEQ], F32)
    dx = ctx.sb("s_dx", [128, T_SEQ], BF16)
    xg = ctx.sb("s_xg", [128, T_SEQ], BF16)
    dA = ctx.sb("s_dA", [128, T_SEQ], F32)
    dBx = ctx.sb("s_dBx", [128, T_SEQ], BF16)
    hh = ctx.sb("s_h", [128, T_SEQ], F32)
    yg = ctx.sb("s_y", [128, T_SEQ], F32)
    brep = ctx.sb("s_brep", [128, T_SEQ], BF16)
    crep = ctx.sb("s_crep", [128, T_SEQ], BF16)
    hc = ctx.sb("s_hc", [128, 2048], F32)
    pp = ctx.ps("psA")
    t1 = ctx.sb("s_t1", [128, 512], F32)
    t2 = ctx.sb("s_t2", [128, 512], F32)
    for g in range(4):
        for tc in range(8):
            c0 = tc * 512
            B.op("tensor",
                 lambda e, g=g, c0=c0: e.matmul(pp[:, 0:512],
                                                dtp[:, g * 128:(g + 1) * 128],
                                                dtf[:, c0:c0 + 512],
                                                start=True, stop=True),
                 reads=["s_dtp", "s_dtf"], writes=["psA"])
            B.op("scalar",
                 lambda e, g=g: e.activation(out=t1[:, 0:512], in_=pp[:, 0:512],
                                             func=AF.Relu, bias=dtb[:, g:g + 1]),
                 reads=["psA", "s_dtb"], writes=["s_t1"])
            B.op("scalar",
                 lambda e, g=g: e.activation(out=t2[:, 0:512], in_=pp[:, 0:512],
                                             func=AF.Abs, bias=dtb[:, g:g + 1]),
                 reads=["psA", "s_dtb"], writes=["s_t2"])
            B.op("scalar",
                 lambda e: e.activation(out=t2[:, 0:512], in_=t2[:, 0:512],
                                        func=AF.Exp, scale=-1.0),
                 reads=["s_t2"], writes=["s_t2"])
            onec1 = ctx.constcol(1.0, 128)
            B.op("scalar",
                 lambda e, onec1=onec1: e.activation(out=t2[:, 0:512],
                                                     in_=t2[:, 0:512],
                                                     func=AF.Ln, bias=onec1[:, :]),
                 reads=["s_t2", onec1.name], writes=["s_t2"])
            B.op("vector",
                 lambda e, c0=c0: e.tensor_add(out=delta[:, c0:c0 + 512],
                                               in0=t1[:, 0:512], in1=t2[:, 0:512]),
                 reads=["s_t1", "s_t2"], writes=["s_delta"])
        B.dma("sync", xg[:, :], io["xsl"].ap()[g * 128:(g + 1) * 128, :],
              reads=["xsl"], writes=["s_xg"])
        B.op("vector", lambda e: e.tensor_mul(out=dx[:, :], in0=delta[:, :],
                                              in1=xg[:, :]),
             reads=["s_delta", "s_xg"], writes=["s_dx"])
        first = True
        for n in range(16):
            B.dma("gpsimd", brow[:, :], bcs[n:n + 1, :], reads=["s_bc"],
                  writes=["s_brow"])
            B.dma("gpsimd", crow[:, :], bcs[16 + n:17 + n, :], reads=["s_bc"],
                  writes=["s_crow"])
            for tc in range(8):
                c0 = tc * 512
                B.op("tensor",
                     lambda e, c0=c0: e.matmul(
                         pp[:, 0:512], onesr[:, :], brow[:, c0:c0 + 512],
                         start=True, stop=True),
                     reads=["s_brow", "c_ones_row32"], writes=["psA"])
                B.op("vector",
                     lambda e, c0=c0: e.tensor_copy(out=brep[:, c0:c0 + 512],
                                                    in_=pp[:, 0:512]),
                     reads=["psA"], writes=["s_brep"])
                B.op("tensor",
                     lambda e, c0=c0: e.matmul(
                         pp[:, 0:512], onesr[:, :], crow[:, c0:c0 + 512],
                         start=True, stop=True),
                     reads=["s_crow", "c_ones_row32"], writes=["psA"])
                B.op("vector",
                     lambda e, c0=c0: e.tensor_copy(out=crep[:, c0:c0 + 512],
                                                    in_=pp[:, 0:512]),
                     reads=["psA"], writes=["s_crep"])
            B.op("vector",
                 lambda e, g=g, n=n: e.tensor_scalar(
                     out=dA[:, :], in0=delta[:, :],
                     scalar1=an[:, g * 16 + n:g * 16 + n + 1], scalar2=None,
                     op0=ALU.mult),
                 reads=["s_delta", "s_an"], writes=["s_dA"])
            B.op("scalar", lambda e: e.activation(out=dA[:, :], in_=dA[:, :],
                                                  func=AF.Exp),
                 reads=["s_dA"], writes=["s_dA"])
            B.op("vector", lambda e: e.tensor_mul(out=dBx[:, :], in0=dx[:, :],
                                                  in1=brep[:, :]),
                 reads=["s_dx", "s_brep"], writes=["s_dBx"])
            B.op("vector",
                 lambda e: e.tensor_tensor_scan(out=hh[:, :], data0=dA[:, :],
                                                data1=dBx[:, :], initial=0.0,
                                                op0=ALU.mult, op1=ALU.add),
                 reads=["s_dA", "s_dBx"], writes=["s_h"])
            if first:
                B.op("vector", lambda e: e.tensor_mul(out=yg[:, :], in0=hh[:, :],
                                                      in1=crep[:, :]),
                     reads=["s_h", "s_crep"], writes=["s_y"])
                first = False
            else:
                for hf in range(2):
                    B.op("vector",
                         lambda e, hf=hf: e.tensor_mul(
                             out=hc[:, :], in0=hh[:, hf * 2048:(hf + 1) * 2048],
                             in1=crep[:, hf * 2048:(hf + 1) * 2048]),
                         reads=["s_h", "s_crep"], writes=["s_hc"])
                    B.op("vector",
                         lambda e, hf=hf: e.tensor_add(
                             out=yg[:, hf * 2048:(hf + 1) * 2048],
                             in0=yg[:, hf * 2048:(hf + 1) * 2048], in1=hc[:, :]),
                         reads=["s_hc"], writes=["s_y"])
        B.op("vector",
             lambda e, g=g: e.scalar_tensor_tensor(
                 out=yg[:, :], in0=xg[:, :], scalar=dc[:, g:g + 1], in1=yg[:, :],
                 op0=ALU.mult, op1=ALU.add),
             reads=["s_xg", "s_dc", "s_y"], writes=["s_y"])
        B.dma("scalar", y_out.ap()[g * 128:(g + 1) * 128, :], yg[:, :],
              reads=["s_y"], writes=["y"])
    return B


def build_a4c(B: Builder, io=None):
    ctx = Ctx(B)
    if io is None:
        io = {}
        io["x_in"] = B.dram("x_in", [D_MODEL, TOK_PER_CORE], F32,
                            kind="ExternalInput")
        io["y_in"] = B.dram("y_in", [D_MODEL, TOK_PER_CORE], F32,
                            kind="ExternalInput")
        io["x_out"] = B.dram("x_out", [D_MODEL, TOK_PER_CORE], F32,
                             kind="ExternalOutput")
    x_in, y_in, x_out = io["x_in"], io["y_in"], io["x_out"]
    x_sb = ctx.sb("c_x", [128, 16 * 512], F32)
    y_sb = ctx.sb("c_y", [128, 16 * 512], F32)
    xv = x_in.ap().rearrange("(a p) t -> p a t", p=128)
    yv = y_in.ap().rearrange("(a p) t -> p a t", p=128)
    xov = x_out.ap().rearrange("(a p) t -> p a t", p=128)
    for chunk in range(2):
        t0 = chunk * 512
        B.dma("sync", x_sb[:].rearrange("p (a t) -> p a t", a=16),
              xv[:, :, t0:t0 + 512], reads=["x_in"], writes=["c_x"])
        B.dma("scalar", y_sb[:].rearrange("p (a t) -> p a t", a=16),
              yv[:, :, t0:t0 + 512], reads=["y_in"], writes=["c_y"])
        for a in range(16):
            B.op("vector",
                 lambda e, a=a: e.tensor_add(out=x_sb[:, a * 512:(a + 1) * 512],
                                             in0=x_sb[:, a * 512:(a + 1) * 512],
                                             in1=y_sb[:, a * 512:(a + 1) * 512]),
                 reads=["c_y"], writes=["c_x"])
            B.dma("gpsimd", xov[:, a, t0:t0 + 512], x_sb[:, a * 512:(a + 1) * 512],
                  reads=["c_x"], writes=["x_out"])
    return B


def host_a4b_inputs(inputs, xln_shards, xdbl_shards, core):
    b = core // 4
    ch0 = (core % 4) * 512
    cols = {}
    xln_full = np.empty((2048, T_SEQ), dtype=xln_shards[0].dtype)
    xdbl_full = np.empty((160, T_SEQ), dtype=np.float32)
    for j in range(4):
        c = b * 4 + j
        for i, blk in enumerate(core_blocks(c)):
            xln_full[:, blk * BLK:(blk + 1) * BLK] = \
                xln_shards[c][:, i * BLK:(i + 1) * BLK]
            xdbl_full[:, blk * BLK:(blk + 1) * BLK] = \
                xdbl_shards[c][:, i * BLK:(i + 1) * BLK]
    alog = np.asarray(inputs["L1_attn_b_A_log"], np.float32)[ch0:ch0 + 512]
    dpar = np.asarray(inputs["L1_attn_b_D_param"], np.float32)[ch0:ch0 + 512]
    dtw = np.asarray(inputs["L1_attn_b_dt_proj_w"], np.float32)[ch0:ch0 + 512]
    dtb = np.asarray(inputs["L1_attn_b_dt_proj_b"], np.float32)[ch0:ch0 + 512]
    return {
        "xsl": np.ascontiguousarray(xln_full[ch0:ch0 + 512]),
        "dt": np.ascontiguousarray(xdbl_full[0:128]),
        "bc": np.ascontiguousarray(xdbl_full[128:160]),
        "dtpT": np.ascontiguousarray(dtw.T),
        "dtb": col_layout(dtb),
        "alog": np.ascontiguousarray(
            alog.reshape(4, 128, 16).transpose(1, 0, 2).reshape(128, 64)),
        "dcol": col_layout(dpar),
    }


def host_a4_reshard_y(y_slices, core):
    b = core // 4
    out = np.empty((2048, TOK_PER_CORE), np.float32)
    for j in range(4):
        src = y_slices[b * 4 + j]
        for i, blk in enumerate(core_blocks(core)):
            out[j * 512:(j + 1) * 512, i * BLK:(i + 1) * BLK] = \
                src[:, blk * BLK:(blk + 1) * BLK]
    return out



def build_a1a(B: Builder, io=None):
    ctx = Ctx(B)
    if io is None:
        io = {}
        io["x_in"] = B.dram("x_in", [D_MODEL, TOK_PER_CORE], F32,
                            kind="ExternalInput")
        io["wqT"] = B.dram("wqT", [2048, 2048], F32, kind="ExternalInput")
        io["wkvT"] = B.dram("wkvT", [2048, 1024], F32, kind="ExternalInput")
        io["bq"] = B.dram("bq", [128, 16], F32, kind="ExternalInput")
        io["bkv"] = B.dram("bkv", [128, 8], F32, kind="ExternalInput")
        io["peT"] = B.dram("peT", [128, TOK_PER_CORE], F32, kind="ExternalInput")
        io["qp"] = B.dram("qp", [2048, TOK_PER_CORE], BF16, kind="ExternalOutput")
        io["k_fm_loc"] = B.dram("k_fm", [512, TOK_PER_CORE], BF16,
                                kind="ExternalOutput")
        io["v_tok_loc"] = B.dram("v_tok", [TOK_PER_CORE, 512], BF16,
                                 kind="ExternalOutput")
        io["kmax_loc"] = B.dram("kmax", [1, 4], F32, kind="ExternalOutput")
    qp_out = io["qp"]
    kf_out = io["k_fm_loc"]
    vt_out = io["v_tok_loc"]
    km_out = io["kmax_loc"]
    ctx.io = io
    bq = ctx.sb("a1_bq", [128, 16], F32)
    B.dma("sync", bq[:, :], io["bq"].ap(), reads=[], writes=["a1_bq"])
    bkv = ctx.sb("a1_bkv", [128, 8], F32)
    B.dma("sync", bkv[:, :], io["bkv"].ap(), reads=[], writes=["a1_bkv"])
    pe = ctx.sb("a1_pe", [128, TOK_PER_CORE], F32)
    B.dma("sync", pe[:, :], io["peT"].ap(), reads=[], writes=["a1_pe"])
    if "identbf" not in io:
        io["identbf"] = B.dram("identbf_d", [128, 128], F32, kind="ExternalInput")
    idbf = ctx.sb("a1_idbf", [128, 128], BF16)
    B.dma("gpsimd", idbf[:, :], io["identbf"].ap(), reads=[], writes=["a1_idbf"])
    vt = ctx.sb("a1_vt", [128, 8 * 512], BF16)
    psB_bf = ctx.ps("psB", dtype=BF16)
    wsl = ctx.sb("a1_wsl", [128, 16 * 512], BF16)
    x_sb = ctx.sb("a1_x", [128, 16 * 512], F32)
    xn = ctx.sb("a1_xn", [128, 16 * 512], BF16)
    pq = ctx.ps("psA")
    t32 = ctx.sb("a1_t32", [128, 512], F32)
    tbf = ctx.sb("a1_tbf", [128, 512], BF16)
    ktil = ctx.sb("a1_k", [128, 4 * 1024], BF16)
    wq_v = io["wqT"].ap().rearrange("(a p) n -> p a n", p=128)
    wkv_v = io["wkvT"].ap().rearrange("(a p) n -> p a n", p=128)
    qv = qp_out.ap().rearrange("(a p) t -> p a t", p=128)
    inv_s = 1.0 / math.sqrt(128.0)
    for chunk in range(2):
        t0 = chunk * 512
        xv = io["x_in"].ap().rearrange("(a p) t -> p a t", p=128)
        B.dma("sync", x_sb[:].rearrange("p (a t) -> p a t", a=16),
              xv[:, :, t0:t0 + 512], reads=["x_in"], writes=["a1_x"])
        build_ln(ctx, x_sb, "a1_x", xn, "a1_xn", 512)
        for s in range(4):
            B.dma("gpsimd", wsl[:].rearrange("p (a n) -> p a n", a=16),
                  wq_v[:, :, s * 512:(s + 1) * 512], reads=["wqT"], writes=["a1_wsl"])
            for dc in range(4):
                d = s * 4 + dc
                for a in range(16):
                    B.op("tensor",
                         lambda e, s=s, dc=dc, a=a: e.matmul(
                             pq[:, 0:512],
                             wsl[:, a * 512 + dc * 128:a * 512 + (dc + 1) * 128],
                             xn[:, a * 512:(a + 1) * 512],
                             start=(a == 0), stop=(a == 15)),
                         reads=["a1_wsl", "a1_xn"], writes=["psA"])
                B.op("vector",
                     lambda e, d=d: e.tensor_scalar(out=t32[:, 0:512],
                                                    in0=pq[:, 0:512],
                                                    scalar1=bq[:, d:d + 1],
                                                    scalar2=None, op0=ALU.add),
                     reads=["psA", "a1_bq"], writes=["a1_t32"])
                B.op("vector",
                     lambda e, t0=t0: e.tensor_add(out=t32[:, 0:512],
                                                   in0=t32[:, 0:512],
                                                   in1=pe[:, t0:t0 + 512]),
                     reads=["a1_pe", "a1_t32"], writes=["a1_t32"])
                B.op("vector",
                     lambda e: e.tensor_scalar(out=tbf[:, 0:512], in0=t32[:, 0:512],
                                               scalar1=inv_s, scalar2=None,
                                               op0=ALU.mult),
                     reads=["a1_t32"], writes=["a1_tbf"])
                B.dma("scalar", qv[:, d, t0:t0 + 512], tbf[:, 0:512],
                      reads=["a1_tbf"], writes=["qp"])
        for s in range(2):
            B.dma("gpsimd", wsl[:].rearrange("p (a n) -> p a n", a=16),
                  wkv_v[:, :, s * 512:(s + 1) * 512], reads=["wkvT"],
                  writes=["a1_wsl"])
            for dc in range(4):
                d = s * 4 + dc
                g = d // 2
                is_k = (d % 2 == 0)
                for a in range(16):
                    B.op("tensor",
                         lambda e, s=s, dc=dc, a=a: e.matmul(
                             pq[:, 0:512],
                             wsl[:, a * 512 + dc * 128:a * 512 + (dc + 1) * 128],
                             xn[:, a * 512:(a + 1) * 512],
                             start=(a == 0), stop=(a == 15)),
                         reads=["a1_wsl", "a1_xn"], writes=["psA"])
                B.op("vector",
                     lambda e, d=d: e.tensor_scalar(out=t32[:, 0:512],
                                                    in0=pq[:, 0:512],
                                                    scalar1=bkv[:, d:d + 1],
                                                    scalar2=None, op0=ALU.add),
                     reads=["psA", "a1_bkv"], writes=["a1_t32"])
                if is_k:
                    B.op("vector",
                         lambda e, g=g, t0=t0: e.tensor_add(
                             out=ktil[:, g * 1024 + t0:g * 1024 + t0 + 512],
                             in0=t32[:, 0:512], in1=pe[:, t0:t0 + 512]),
                         reads=["a1_pe", "a1_t32"], writes=["a1_k"])
                else:
                    B.op("vector",
                         lambda e: e.tensor_copy(out=tbf[:, 0:512],
                                                 in_=t32[:, 0:512]),
                         reads=["a1_t32"], writes=["a1_tbf"])
                    for ts in range(4):
                        B.op("tensor",
                             lambda e, ts=ts: e.transpose(
                                 psB_bf[:, 0:128], tbf[:, ts * 128:(ts + 1) * 128],
                                 idbf[:, :]),
                             reads=["a1_tbf", "a1_idbf"], writes=["psB"])
                        col = 0
                        B.op("vector",
                             lambda e, g=g, ts=ts, chunk=chunk: e.tensor_copy(
                                 out=vt[:, (chunk * 4 + ts) * 512 + g * 128:
                                        (chunk * 4 + ts) * 512 + g * 128 + 128],
                                 in_=psB_bf[:, 0:128]),
                             reads=["psB"], writes=["a1_vt"])
    ksq = ctx.sb("a1_ksq", [128, 512], F32)
    kmx = ctx.sb("a1_kmx", [1, 8], F32)
    pk = ctx.ps("psC")
    onec = ctx.onescol_f32()
    for g in range(4):
        for hc2 in range(2):
            B.op("scalar",
                 lambda e, g=g, hc2=hc2: e.activation(
                     out=ksq[:, 0:512],
                     in_=ktil[:, g * 1024 + hc2 * 512:g * 1024 + hc2 * 512 + 512],
                     func=AF.Square),
                 reads=["a1_k"], writes=["a1_ksq"])
            B.op("tensor",
                 lambda e: e.matmul(pk[0:1, 0:512], onec[:, :], ksq[:, 0:512],
                                    start=True, stop=True),
                 reads=["a1_ksq", "c_ones_col32"], writes=["psC"])
            B.op("vector",
                 lambda e, g=g, hc2=hc2: e.reduce_max(
                     out=kmx[0:1, g * 2 + hc2:g * 2 + hc2 + 1],
                     in_=pk[0:1, 0:512], axis=AX.X),
                 reads=["psC"], writes=["a1_kmx"])
    kmo = ctx.sb("a1_kmo", [1, 4], F32)
    for g in range(4):
        B.op("vector",
             lambda e, g=g: e.tensor_max(out=kmo[0:1, g:g + 1],
                                         in0=kmx[0:1, g * 2:g * 2 + 1],
                                         in1=kmx[0:1, g * 2 + 1:g * 2 + 2]),
             reads=["a1_kmx"], writes=["a1_kmo"])
    B.dma("scalar", km_out.ap(), kmo[:, :], reads=["a1_kmo"], writes=["kmax"])
    kfv = kf_out.ap().rearrange("(g p) t -> p g t", p=128)
    for g in range(4):
        B.dma("scalar", kfv[:, g, :], ktil[:, g * 1024:(g + 1) * 1024],
              reads=["a1_k"], writes=["k_fm"])
    vtv = vt_out.ap().rearrange("(s p) d -> p s d", p=128)
    B.dma("scalar", vtv[:, :, :], vt[:].rearrange("p (s d) -> p s d", s=8),
          reads=["a1_vt"], writes=["v_tok"])
    return B


def build_a1b(B: Builder, io=None):
    ctx = Ctx(B)
    if io is None:
        io = {}
        io["x_in"] = B.dram("x_in", [D_MODEL, TOK_PER_CORE], F32,
                            kind="ExternalInput")
        io["x_out"] = B.dram("x_out", [D_MODEL, TOK_PER_CORE], F32,
                             kind="ExternalOutput")
        io["qp"] = B.dram("qp", [2048, TOK_PER_CORE], BF16, kind="ExternalInput")
        io["k_fm"] = B.dram("k_fm", [512, T_SEQ], BF16, kind="ExternalInput")
        io["v_tok"] = B.dram("v_tok", [T_SEQ, 512], BF16, kind="ExternalInput")
        io["woT"] = B.dram("woT", [2048, 2048], F32, kind="ExternalInput")
        io["ab"] = B.dram("ab", [128, 32], F32, kind="ExternalInput")
        io["diag"] = B.dram("diag", [128, 4 * 512], F32, kind="ExternalInput")
        for i in range(4):
            io[f"kmax{i}"] = B.dram(f"kmax{i}", [1, 4], F32,
                                    kind="ExternalInput").ap()
        io["identbf"] = B.dram("identbf_d", [128, 128], F32,
                               kind="ExternalInput")
    ctx.io = io
    NH, NKV, DV = 16, 4, 128
    idbf = ctx.sb("f_idbf", [128, 128], BF16)
    B.dma("gpsimd", idbf[:, :], io["identbf"].ap(), reads=[], writes=["f_idbf"])
    ab = ctx.sb("f_ab", [128, 32], F32)
    B.dma("sync", ab[:, :], io["ab"].ap(), reads=[], writes=["f_ab"])
    diag = ctx.sb("f_diag", [128, 4 * 512], BF16)
    B.dma("gpsimd", diag[:].rearrange("p (k t) -> p k t", k=4),
          io["diag"].ap().rearrange("p (k t) -> p k t", k=4), reads=[],
          writes=["f_diag"])
    km = ctx.sb("f_km", [1, 4], F32)
    kmt = ctx.sb("f_kmt", [1, 4], F32)
    B.dma("sync", km[:, :], io["kmax0"], reads=["d_kv_ag"], writes=["f_km"])
    for i in range(1, 4):
        B.dma("sync", kmt[:, :], io[f"kmax{i}"], reads=["d_kv_ag"],
              writes=["f_kmt"])
        B.op("vector", lambda e: e.tensor_max(out=km[:, :], in0=km[:, :],
                                              in1=kmt[:, :]),
             reads=["f_kmt"], writes=["f_km"])
    qsb = ctx.sb("f_q", [128, 16 * 1024], BF16)
    B.dma("sync", qsb[:].rearrange("p (a t) -> p a t", a=16),
          io["qp"].ap().rearrange("(a p) t -> p a t", p=128), reads=["d_qp"],
          writes=["f_q"])
    attn = ctx.sb("f_attn", [128, 16 * 1024], BF16)
    kch = ctx.sb("f_kch", [128, 4 * 512], BF16)
    vch = ctx.sb("f_vch", [128, 4 * 512], BF16)
    vaug = ctx.sb("f_vaug", [128, 16 * 129], BF16)
    for gs in range(16):
        B.op("vector", lambda e, gs=gs: e.memset(
                 vaug[:, gs * 129 + 128:gs * 129 + 129], 1.0),
             writes=["f_vaug"])
    outsb = ctx.sb("f_out", [128, 64 * 129], F32)
    qsq = ctx.sb("f_qsq", [128, 512], F32)
    qmx = ctx.sb("f_qmx", [1, 16], F32)
    bsq = ctx.sb("f_bsq", [1, 16], F32)
    nbc = ctx.sb("f_nbc", [128, 16], F32)
    pt = ctx.sb("f_pt", [128, 512], BF16)
    msk = ctx.sb("f_msk", [128, 4 * 512], BF16)
    stg = ctx.sb("f_stg", [128, 128], BF16)
    psS = ctx.ps("psA")
    psC = ctx.ps("psC")
    psCb = ctx.ps("psCb", dtype=BF16)
    psO = [ctx.ps(f"psO{i}", (128, 512)) for i in range(4)]
    onec = ctx.onescol_f32()
    onesr = ctx.onesrow_f32()
    kv_v = io["k_fm"].ap().rearrange("(g p) t -> p g t", p=128)
    vt_v = io["v_tok"].ap().rearrange("(s p) d -> p s d", p=128)
    for qb in range(2):
        q0 = qb * 512
        for a in range(NH):
            for hc2 in range(1):
                B.op("scalar",
                     lambda e, a=a, q0=q0: e.activation(
                         out=qsq[:, 0:512], in_=qsb[:, a * 1024 + q0:a * 1024 + q0 + 512],
                         func=AF.Square),
                     reads=["f_q"], writes=["f_qsq"])
                B.op("tensor",
                     lambda e: e.matmul(psC[0:1, 0:512], onec[:, :], qsq[:, 0:512],
                                        start=True, stop=True),
                     reads=["f_qsq", "c_ones_col32"], writes=["psC"])
                B.op("vector",
                     lambda e, a=a: e.reduce_max(out=qmx[0:1, a:a + 1],
                                                 in_=psC[0:1, 0:512], axis=AX.X),
                     reads=["psC"], writes=["f_qmx"])
            B.op("vector",
                 lambda e, a=a: e.tensor_mul(out=bsq[0:1, a:a + 1],
                                             in0=qmx[0:1, a:a + 1],
                                             in1=km[0:1, a // 4:a // 4 + 1]),
                 reads=["f_qmx", "f_km"], writes=["f_bsq"])
        B.op("scalar", lambda e: e.activation(out=bsq[0:1, 0:16], in_=bsq[0:1, 0:16],
                                              func=AF.Ln),
             reads=["f_bsq"], writes=["f_bsq"])
        B.op("scalar", lambda e: e.activation(out=bsq[0:1, 0:16], in_=bsq[0:1, 0:16],
                                              func=AF.Exp, scale=0.5),
             reads=["f_bsq"], writes=["f_bsq"])
        B.op("scalar", lambda e: e.activation(out=bsq[0:1, 0:16], in_=bsq[0:1, 0:16],
                                              func=AF.Copy, scale=-1.0),
             reads=["f_bsq"], writes=["f_bsq"])
        B.op("tensor", lambda e: e.matmul(psC[:, 0:16], onesr[:, :], bsq[0:1, 0:16],
                                          start=True, stop=True),
             reads=["f_bsq", "c_ones_row32"], writes=["psC"])
        B.op("vector", lambda e: e.tensor_copy(out=nbc[:, 0:16], in_=psC[:, 0:16]),
             reads=["psC"], writes=["f_nbc"])
        first = True
        for c in range(8):
            B.dma("sync", kch[:].rearrange("p (g t) -> p g t", g=4),
                  kv_v[:, :, c * 512:(c + 1) * 512], reads=["d_kv_ag"],
                  writes=["f_kch"])
            B.dma("scalar", vch[:].rearrange("p (s d) -> p s d", s=4),
                  vt_v[:, c * 4:c * 4 + 4, :], reads=["d_kv_ag"],
                  writes=["f_vch"])
            for g in range(4):
                for ks in range(4):
                    B.op("vector",
                         lambda e, g=g, ks=ks: e.tensor_copy(
                             out=vaug[:, (g * 4 + ks) * 129:(g * 4 + ks) * 129 + 128],
                             in_=vch[:, ks * 512 + g * 128:ks * 512 + g * 128 + 128]),
                         reads=["f_vch"], writes=["f_vaug"])
            abidx = qb * 16 + c * 2
            for ks in range(4):
                B.op("vector",
                     lambda e, ks=ks, abidx=abidx: e.tensor_scalar(
                         out=msk[:, ks * 512:(ks + 1) * 512],
                         in0=diag[:, ks * 512:(ks + 1) * 512],
                         scalar1=ab[:, abidx + 1:abidx + 2],
                         scalar2=ab[:, abidx:abidx + 1],
                         op0=ALU.mult, op1=ALU.add),
                     reads=["f_diag", "f_ab"], writes=["f_msk"])
            for a in range(NH):
                g = a // 4
                for ks in range(4):
                    B.op("tensor",
                         lambda e, a=a, g=g, ks=ks, q0=q0: e.matmul(
                             psS[:, 0:512],
                             kch[:, g * 512 + ks * 128:g * 512 + (ks + 1) * 128],
                             qsb[:, a * 1024 + q0:a * 1024 + q0 + 512],
                             start=True, stop=True),
                         reads=["f_kch", "f_q"], writes=["psA"])
                    B.op("scalar",
                         lambda e, a=a: e.activation(
                             out=pt[:, 0:512], in_=psS[:, 0:512], func=AF.Exp,
                             bias=nbc[:, a:a + 1]),
                         reads=["psA", "f_nbc"], writes=["f_pt"])
                    B.op("vector",
                         lambda e, ks=ks: e.tensor_mul(
                             out=pt[:, 0:512], in0=pt[:, 0:512],
                             in1=msk[:, ks * 512:(ks + 1) * 512]),
                         reads=["f_msk"], writes=["f_pt"])
                    for qs in range(4):
                        B.op("tensor",
                             lambda e, g=g, ks=ks, qs=qs: e.matmul(
                                 psO[qs][:, 0:129],
                                 pt[:, qs * 128:(qs + 1) * 128],
                                 vaug[:, (g * 4 + ks) * 129:(g * 4 + ks + 1) * 129],
                                 start=(ks == 0), stop=(ks == 3)),
                             reads=["f_pt", "f_vaug"], writes=[f"psO{qs}"])
                for qs in range(4):
                    oidx = (a * 4 + qs) * 129
                    if first:
                        B.op("vector",
                             lambda e, qs=qs, oidx=oidx: e.tensor_copy(
                                 out=outsb[:, oidx:oidx + 129],
                                 in_=psO[qs][:, 0:129]),
                             reads=[f"psO{qs}"], writes=["f_out"])
                    else:
                        B.op("vector",
                             lambda e, qs=qs, oidx=oidx: e.tensor_add(
                                 out=outsb[:, oidx:oidx + 129],
                                 in0=outsb[:, oidx:oidx + 129],
                                 in1=psO[qs][:, 0:129]),
                             reads=[f"psO{qs}"], writes=["f_out"])
            first = False
        rcol = ctx.sb("f_rcol", [128, 1], F32)
        for a in range(NH):
            for qs in range(4):
                oidx = (a * 4 + qs) * 129
                B.op("vector",
                     lambda e, oidx=oidx: e.reciprocal(
                         out=rcol[:, :], in_=outsb[:, oidx + 128:oidx + 129]),
                     reads=["f_out"], writes=["f_rcol"])
                B.op("vector",
                     lambda e, oidx=oidx: e.tensor_scalar(
                         out=stg[:, 0:128], in0=outsb[:, oidx:oidx + 128],
                         scalar1=rcol[:, :], scalar2=None, op0=ALU.mult),
                     reads=["f_out", "f_rcol"], writes=["f_stg"])
                B.op("tensor",
                     lambda e: e.transpose(psCb[:, 0:128], stg[:, 0:128],
                                           ctx.pool["f_idbf"][:, :]),
                     reads=["f_stg", "f_idbf"], writes=["psCb"])
                B.op("vector",
                     lambda e, a=a, qs=qs, q0=q0: e.tensor_copy(
                         out=attn[:, a * 1024 + q0 + qs * 128:
                                  a * 1024 + q0 + (qs + 1) * 128],
                         in_=psCb[:, 0:128]),
                     reads=["psCb"], writes=["f_attn"])
    x_sb = ctx.sb("f_x", [128, 16 * 512], F32)
    wsl = ctx.sb("f_wsl", [128, 16 * 512], BF16)
    wo_v = io["woT"].ap().rearrange("(a p) n -> p a n", p=128)
    xv = io["x_in"].ap().rearrange("(a p) t -> p a t", p=128)
    xov = io["x_out"].ap().rearrange("(a p) t -> p a t", p=128)
    psW = ctx.ps("psD")
    for chunk in range(2):
        t0 = chunk * 512
        B.dma("sync", x_sb[:].rearrange("p (a t) -> p a t", a=16),
              xv[:, :, t0:t0 + 512], reads=["x_in"], writes=["f_x"])
        for s in range(4):
            B.dma("gpsimd", wsl[:].rearrange("p (a n) -> p a n", a=16),
                  wo_v[:, :, s * 512:(s + 1) * 512], reads=["woT"], writes=["f_wsl"])
            for dc in range(4):
                d = s * 4 + dc
                for a in range(16):
                    B.op("tensor",
                         lambda e, s=s, dc=dc, a=a, t0=t0: e.matmul(
                             psW[:, 0:512],
                             wsl[:, a * 512 + dc * 128:a * 512 + (dc + 1) * 128],
                             attn[:, a * 1024 + t0:a * 1024 + t0 + 512],
                             start=(a == 0), stop=(a == 15)),
                         reads=["f_wsl", "f_attn"], writes=["psD"])
                B.op("vector",
                     lambda e, d=d: e.tensor_add(out=x_sb[:, d * 512:(d + 1) * 512],
                                                 in0=x_sb[:, d * 512:(d + 1) * 512],
                                                 in1=psW[:, 0:512]),
                     reads=["psD"], writes=["f_x"])
                B.dma("scalar", xov[:, d, t0:t0 + 512], x_sb[:, d * 512:(d + 1) * 512],
                      reads=["f_x"], writes=["x_out"])
    return B


def host_a1_inputs(inputs, core):
    wq = np.asarray(inputs["L0_attn_a_wq"], np.float32)
    wkv = np.asarray(inputs["L0_attn_a_wkv"], np.float32)
    wo = np.asarray(inputs["L0_attn_a_wo"], np.float32)
    pe = sinusoidal_pe(T_SEQ, 128)
    blks = core_blocks(core)
    pe_core = np.concatenate([pe[b * BLK:(b + 1) * BLK] for b in blks], 0)
    ab = np.zeros(32, np.float32)
    for qi, qpos in enumerate(blks):
        for c in range(8):
            al = 1.0 if c < qpos else 0.0
            be = 1.0 if c == qpos else 0.0
            ab[qi * 16 + c * 2] = al
            ab[qi * 16 + c * 2 + 1] = be
    ab_rep = np.ascontiguousarray(np.broadcast_to(ab[None, :], (128, 32)))
    dg = np.zeros((128, 4 * 512), np.float32)
    for ks in range(4):
        kk = np.arange(128)[:, None] + ks * 128
        qq = np.arange(512)[None, :]
        dg[:, ks * 512:(ks + 1) * 512] = (kk <= qq).astype(np.float32)
    return {
        "wqT": wq,
        "wkvT": wkv,
        "woT": wo,
        "bq": col_layout(inputs["L0_attn_a_bq"]),
        "bkv": col_layout(inputs["L0_attn_a_bkv"]),
        "peT": np.ascontiguousarray(pe_core.T),
        "identbf_d": np.eye(128, dtype=np.float32),
        "ab": ab_rep,
        "diag": dg,
    }



def build_a3a(B: Builder, io=None):
    ctx = Ctx(B)
    if io is None:
        io = {}
        io["x_in"] = B.dram("x_in", [D_MODEL, TOK_PER_CORE], F32,
                            kind="ExternalInput")
        io["wqT"] = B.dram("wqTp", [2048, 3072], F32, kind="ExternalInput")
        io["wkvaT"] = B.dram("wkvaT", [2048, 576], F32, kind="ExternalInput")
        io["wkvbTk"] = B.dram("wkvbTk", [512, 2048], F32, kind="ExternalInput")
        io["wkvbTv"] = B.dram("wkvbTv", [512, 2048], F32, kind="ExternalInput")
        io["gam"] = B.dram("gam", [128, 4], F32, kind="ExternalInput")
        io["cos2"] = B.dram("cos2", [128, 2], F32, kind="ExternalInput")
        io["csraw"] = B.dram("csraw", [64, 2], F32, kind="ExternalInput")
        io["qn"] = B.dram("qn", [2048, TOK_PER_CORE], BF16, kind="ExternalOutput")
        io["qpe"] = B.dram("qpe", [64, 16 * TOK_PER_CORE], BF16,
                           kind="ExternalOutput")
        io["knope_loc"] = B.dram("knope", [2048, TOK_PER_CORE], BF16,
                                 kind="ExternalOutput")
        io["kpe_loc"] = B.dram("kpe", [64, TOK_PER_CORE], BF16,
                               kind="ExternalOutput")
        io["v_tok_loc"] = B.dram("v_tok", [TOK_PER_CORE, 2048], BF16,
                                 kind="ExternalOutput")
        io["kmax_loc"] = B.dram("kmax", [1, 16], F32, kind="ExternalOutput")
    qn_out = io["qn"]
    qpe_out = io["qpe"]
    kn_out = io["knope_loc"]
    kpe_out = io["kpe_loc"]
    vt_out = io["v_tok_loc"]
    km_out = io["kmax_loc"]
    ctx.io = io
    SC = 192.0 ** (-0.5)
    cs = ctx.sb("a3_cs", [128, 2], F32)
    B.dma("sync", cs[:, :], io["cos2"].ap(), reads=[], writes=["a3_cs"])
    B.op("scalar", lambda e: e.activation(out=cs[:, :], in_=cs[:, :], func=AF.Copy,
                                          scale=SC),
         reads=["a3_cs"], writes=["a3_cs"])
    csr = ctx.sb("a3_csraw", [64, 2], F32)
    B.dma("sync", csr[:, :], io["csraw"].ap(), reads=[], writes=["a3_csraw"])
    io["csraw_s"] = csr
    gam = ctx.sb("a3_gam", [128, 4], F32)
    B.dma("sync", gam[:, :], io["gam"].ap(), reads=[], writes=["a3_gam"])
    wsl = ctx.sb("a3_wsl", [128, 16 * 512], BF16)
    x_sb = ctx.sb("a3_x", [128, 16 * 512], F32)
    xn = ctx.sb("a3_xn", [128, 16 * 512], BF16)
    pq = ctx.ps("psA")
    t32 = ctx.sb("a3_t32", [128, 512], F32)
    rot = ctx.sb("a3_rot", [128, 512], F32)
    tbf = ctx.sb("a3_tbf", [128, 512], BF16)
    ckvn = ctx.sb("a3_ckvn", [128, 4 * 1024], BF16)
    kpe_sb = ctx.sb("a3_kpe", [64, 1024], BF16)
    wq_v = io["wqT"].ap().rearrange("(a p) n -> p a n", p=128)
    wkva_v = io["wkvaT"].ap().rearrange("(a p) n -> p a n", p=128)
    qv = qn_out.ap().rearrange("(a p) t -> p a t", p=128)
    for chunk in range(2):
        t0 = chunk * 512
        xv = io["x_in"].ap().rearrange("(a p) t -> p a t", p=128)
        B.dma("sync", x_sb[:].rearrange("p (a t) -> p a t", a=16),
              xv[:, :, t0:t0 + 512], reads=["x_in"], writes=["a3_x"])
        build_ln(ctx, x_sb, "a3_x", xn, "a3_xn", 512)
        for s in range(6):
            B.dma("gpsimd", wsl[:].rearrange("p (a n) -> p a n", a=16),
                  wq_v[:, :, s * 512:(s + 1) * 512], reads=["wqTp"],
                  writes=["a3_wsl"])
            for dc in range(4):
                d = s * 4 + dc
                for a in range(16):
                    B.op("tensor",
                         lambda e, s=s, dc=dc, a=a: e.matmul(
                             pq[:, 0:512],
                             wsl[:, a * 512 + dc * 128:a * 512 + (dc + 1) * 128],
                             xn[:, a * 512:(a + 1) * 512],
                             start=(a == 0), stop=(a == 15)),
                         reads=["a3_wsl", "a3_xn"], writes=["psA"])
                if d < 16:
                    B.op("scalar",
                         lambda e: e.activation(out=tbf[:, 0:512], in_=pq[:, 0:512],
                                                func=AF.Copy, scale=SC),
                         reads=["psA"], writes=["a3_tbf"])
                    B.dma("scalar", qv[:, d, t0:t0 + 512], tbf[:, 0:512],
                          reads=["a3_tbf"], writes=["qn"])
                else:
                    B.op("vector",
                         lambda e: e.tensor_copy(out=t32[:, 0:512], in_=pq[:, 0:512]),
                         reads=["psA"], writes=["a3_t32"])
                    for (o_, i_, sg) in ((0, 32, -1.0), (32, 0, 1.0),
                                         (64, 96, -1.0), (96, 64, 1.0)):
                        B.op("vector",
                             lambda e, o_=o_, i_=i_, sg=sg: e.tensor_scalar(
                                 out=rot[o_:o_ + 32, 0:512],
                                 in0=t32[i_:i_ + 32, 0:512], scalar1=sg,
                                 scalar2=None, op0=ALU.mult),
                             reads=["a3_t32"], writes=["a3_rot"])
                    B.op("vector",
                         lambda e: e.tensor_scalar(out=t32[:, 0:512],
                                                   in0=t32[:, 0:512],
                                                   scalar1=cs[:, 0:1], scalar2=None,
                                                   op0=ALU.mult),
                         reads=["a3_t32", "a3_cs"], writes=["a3_t32"])
                    B.op("vector",
                         lambda e: e.tensor_scalar(out=rot[:, 0:512],
                                                   in0=rot[:, 0:512],
                                                   scalar1=cs[:, 1:2], scalar2=None,
                                                   op0=ALU.mult),
                         reads=["a3_rot", "a3_cs"], writes=["a3_rot"])
                    B.op("vector",
                         lambda e: e.tensor_add(out=tbf[:, 0:512], in0=t32[:, 0:512],
                                                in1=rot[:, 0:512]),
                         reads=["a3_t32", "a3_rot"], writes=["a3_tbf"])
                    h2 = (d - 16) * 2
                    B.dma("scalar",
                          qpe_out.ap()[:, h2 * TOK_PER_CORE + t0:
                                       h2 * TOK_PER_CORE + t0 + 512],
                          tbf[0:64, 0:512], reads=["a3_tbf"], writes=["qpe"])
                    B.dma("scalar",
                          qpe_out.ap()[:, (h2 + 1) * TOK_PER_CORE + t0:
                                       (h2 + 1) * TOK_PER_CORE + t0 + 512],
                          tbf[64:128, 0:512], reads=["a3_tbf"], writes=["qpe"])
        B.dma("gpsimd", wsl[:].rearrange("p (a n) -> p a n", a=16),
              wkva_v[:, :, 0:512], reads=["wkvaT"], writes=["a3_wsl"])
        ck32 = ctx.sb("a3_ck32", [128, 4 * 512], F32)
        for dc in range(4):
            for a in range(16):
                B.op("tensor",
                     lambda e, dc=dc, a=a: e.matmul(
                         pq[:, 0:512],
                         wsl[:, a * 512 + dc * 128:a * 512 + (dc + 1) * 128],
                         xn[:, a * 512:(a + 1) * 512],
                         start=(a == 0), stop=(a == 15)),
                     reads=["a3_wsl", "a3_xn"], writes=["psA"])
            B.op("vector",
                 lambda e, dc=dc: e.tensor_copy(out=ck32[:, dc * 512:(dc + 1) * 512],
                                                in_=pq[:, 0:512]),
                 reads=["psA"], writes=["a3_ck32"])
        wsl2 = ctx.sb("a3_wsl2", [128, 16 * 64], BF16)
        B.dma("gpsimd", wsl2[:].rearrange("p (a n) -> p a n", a=16),
              wkva_v[:, :, 512:576], reads=["wkvaT"], writes=["a3_wsl2"])
        for a in range(16):
            B.op("tensor",
                 lambda e, a=a: e.matmul(
                     pq[0:64, 0:512], wsl2[:, a * 64:(a + 1) * 64],
                     xn[:, a * 512:(a + 1) * 512],
                     start=(a == 0), stop=(a == 15)),
                 reads=["a3_wsl2", "a3_xn"], writes=["psA"])
        B.op("vector", lambda e: e.tensor_copy(out=t32[0:64, 0:512],
                                               in_=pq[0:64, 0:512]),
             reads=["psA"], writes=["a3_t32"])
        for (o_, i_, sg) in ((0, 32, -1.0), (32, 0, 1.0)):
            B.op("vector",
                 lambda e, o_=o_, i_=i_, sg=sg: e.tensor_scalar(
                     out=rot[o_:o_ + 32, 0:512], in0=t32[i_:i_ + 32, 0:512],
                     scalar1=sg, scalar2=None, op0=ALU.mult),
                 reads=["a3_t32"], writes=["a3_rot"])
        B.op("vector",
             lambda e: e.tensor_scalar(out=t32[0:64, 0:512], in0=t32[0:64, 0:512],
                                       scalar1=io["csraw_s"][:, 0:1], scalar2=None,
                                       op0=ALU.mult),
             reads=["a3_t32", "a3_csraw"], writes=["a3_t32"])
        B.op("vector",
             lambda e: e.tensor_scalar(out=rot[0:64, 0:512], in0=rot[0:64, 0:512],
                                       scalar1=io["csraw_s"][:, 1:2], scalar2=None,
                                       op0=ALU.mult),
             reads=["a3_rot", "a3_csraw"], writes=["a3_rot"])
        B.op("vector",
             lambda e, t0=t0: e.tensor_add(out=kpe_sb[:, t0:t0 + 512],
                                           in0=t32[0:64, 0:512],
                                           in1=rot[0:64, 0:512]),
             reads=["a3_t32", "a3_rot"], writes=["a3_kpe"])
        sq = ctx.sb("ln_sq4", [128, 4 * 512], F32)
        st_q = ctx.ps("psD")
        onec = ctx.onescol_f32()
        for dc in range(4):
            B.op("scalar",
                 lambda e, dc=dc: e.activation(out=sq[:, dc * 512:(dc + 1) * 512],
                                               in_=ck32[:, dc * 512:(dc + 1) * 512],
                                               func=AF.Square),
                 reads=["a3_ck32"], writes=["ln_sq4"])
        for dc in range(4):
            B.op("tensor",
                 lambda e, dc=dc: e.matmul(st_q[0:1, 0:512], onec[:, :],
                                           sq[:, dc * 512:(dc + 1) * 512],
                                           start=(dc == 0), stop=(dc == 3)),
                 reads=["ln_sq4", "c_ones_col32"], writes=["psD"])
        ms = ctx.sb("a3_ms", [1, 512], F32)
        B.op("scalar", lambda e: e.activation(out=ms[:, 0:512], in_=st_q[0:1, 0:512],
                                              func=AF.Copy, scale=1.0 / 512.0),
             reads=["psD"], writes=["a3_ms"])
        eps6 = ctx.constcol(1e-6)
        B.op("scalar", lambda e, eps6=eps6: e.activation(out=ms[:, 0:512],
                                                         in_=ms[:, 0:512],
                                                         func=AF.Ln,
                                                         bias=eps6[:, :]),
             reads=["a3_ms", eps6.name], writes=["a3_ms"])
        B.op("scalar", lambda e: e.activation(out=ms[:, 0:512], in_=ms[:, 0:512],
                                              func=AF.Exp, scale=-0.5),
             reads=["a3_ms"], writes=["a3_ms"])
        psb = ctx.ps("psC")
        onesr = ctx.onesrow_f32()
        rsb = ctx.sb("a3_rsb", [128, 512], F32)
        B.op("tensor", lambda e: e.matmul(psb[:, 0:512], onesr[:, :], ms[:, 0:512],
                                          start=True, stop=True),
             reads=["a3_ms", "c_ones_row32"], writes=["psC"])
        B.op("vector", lambda e: e.tensor_copy(out=rsb[:, 0:512], in_=psb[:, 0:512]),
             reads=["psC"], writes=["a3_rsb"])
        for dc in range(4):
            B.op("vector",
                 lambda e, dc=dc: e.tensor_mul(out=ck32[:, dc * 512:(dc + 1) * 512],
                                               in0=ck32[:, dc * 512:(dc + 1) * 512],
                                               in1=rsb[:, 0:512]),
                 reads=["a3_rsb"], writes=["a3_ck32"])
            B.op("vector",
                 lambda e, dc=dc, t0=t0: e.tensor_scalar(
                     out=ckvn[:, dc * 1024 + t0:dc * 1024 + t0 + 512],
                     in0=ck32[:, dc * 512:(dc + 1) * 512],
                     scalar1=gam[:, dc:dc + 1], scalar2=None, op0=ALU.mult),
                 reads=["a3_ck32", "a3_gam"], writes=["a3_ckvn"])
    B.dma("scalar", kpe_out.ap(), kpe_sb[:, :], reads=["a3_kpe"], writes=["kpe"])
    kbsl = ctx.sb("a3_kbsl", [128, 4 * 2048], BF16)
    B.dma("gpsimd", kbsl[:].rearrange("p (a n) -> p a n", a=4),
          io["wkvbTk"].ap().rearrange("(a p) n -> p a n", p=128), reads=[],
          writes=["a3_kbsl"])
    knv = kn_out.ap().rearrange("(a p) t -> p a t", p=128)
    kn_sb = ctx.sb("a3_kn", [128, 16 * 1024], BF16)
    for d in range(16):
        for tc in range(2):
            t0 = tc * 512
            for a in range(4):
                B.op("tensor",
                     lambda e, d=d, a=a, t0=t0: e.matmul(
                         pq[:, 0:512], kbsl[:, a * 2048 + d * 128:a * 2048 + (d + 1) * 128],
                         ckvn[:, a * 1024 + t0:a * 1024 + t0 + 512],
                         start=(a == 0), stop=(a == 3)),
                     reads=["a3_kbsl", "a3_ckvn"], writes=["psA"])
            B.op("vector",
                 lambda e, d=d, t0=t0: e.tensor_copy(
                     out=kn_sb[:, d * 1024 + t0:d * 1024 + t0 + 512],
                     in_=pq[:, 0:512]),
                 reads=["psA"], writes=["a3_kn"])
    for d in range(16):
        B.dma("scalar", knv[:, d, :], kn_sb[:, d * 1024:(d + 1) * 1024],
              reads=["a3_kn"], writes=["knope"])
    B.dma("gpsimd", kbsl[:].rearrange("p (a n) -> p a n", a=4),
          io["wkvbTv"].ap().rearrange("(a p) n -> p a n", p=128),
          reads=["a3_kn"], writes=["a3_kbsl"])
    vtv = vt_out.ap().rearrange("(s p) d -> p s d", p=128)
    tv = ctx.sb("a3_tv", [128, 512], BF16)
    for ts in range(8):
        tc, sub = ts // 4, ts % 4
        for vs in range(4):
            for a in range(4):
                B.op("tensor",
                     lambda e, ts=ts, tc=tc, sub=sub, vs=vs, a=a: e.matmul(
                         pq[:, 0:512],
                         ckvn[:, a * 1024 + tc * 512 + sub * 128:
                              a * 1024 + tc * 512 + sub * 128 + 128],
                         kbsl[:, a * 2048 + vs * 512:a * 2048 + (vs + 1) * 512],
                         start=(a == 0), stop=(a == 3)),
                     reads=["a3_kbsl", "a3_ckvn"], writes=["psA"])
            B.op("vector", lambda e: e.tensor_copy(out=tv[:, 0:512],
                                                   in_=pq[:, 0:512]),
                 reads=["psA"], writes=["a3_tv"])
            B.dma("scalar", vtv[:, ts, vs * 512:(vs + 1) * 512], tv[:, 0:512],
                  reads=["a3_tv"], writes=["v_tok"])
    ksq = ctx.sb("a3_ksq", [128, 512], F32)
    kpsq = ctx.sb("a3_kpsq", [1, 1024], F32)
    kmx = ctx.sb("a3_kmx", [1, 32], F32)
    kmo = ctx.sb("a3_kmo", [1, 16], F32)
    pk = ctx.ps("psC")
    onec = ctx.onescol_f32()
    for tc in range(2):
        B.op("scalar",
             lambda e, tc=tc: e.activation(out=ksq[0:64, 0:512],
                                           in_=kpe_sb[:, tc * 512:(tc + 1) * 512],
                                           func=AF.Square),
             reads=["a3_kpe"], writes=["a3_ksq"])
        B.op("tensor",
             lambda e: e.matmul(pk[0:1, 0:512], onec[0:64, :], ksq[0:64, 0:512],
                                start=True, stop=True),
             reads=["a3_ksq", "c_ones_col32"], writes=["psC"])
        B.op("vector",
             lambda e, tc=tc: e.tensor_copy(out=kpsq[:, tc * 512:(tc + 1) * 512],
                                            in_=pk[0:1, 0:512]),
             reads=["psC"], writes=["a3_kpsq"])
    for h in range(16):
        for tc in range(2):
            B.op("scalar",
                 lambda e, h=h, tc=tc: e.activation(
                     out=ksq[:, 0:512],
                     in_=kn_sb[:, h * 1024 + tc * 512:h * 1024 + tc * 512 + 512],
                     func=AF.Square),
                 reads=["a3_kn"], writes=["a3_ksq"])
            B.op("tensor",
                 lambda e: e.matmul(pk[0:1, 0:512], onec[:, :], ksq[:, 0:512],
                                    start=True, stop=True),
                 reads=["a3_ksq", "c_ones_col32"], writes=["psC"])
            kk = ctx.sb("a3_kk", [1, 512], F32)
            B.op("vector",
                 lambda e, tc=tc: e.tensor_add(out=kk[:, 0:512],
                                               in0=pk[0:1, 0:512],
                                               in1=kpsq[:, tc * 512:(tc + 1) * 512]),
                 reads=["psC", "a3_kpsq"], writes=["a3_kk"])
            B.op("vector",
                 lambda e, h=h, tc=tc: e.reduce_max(
                     out=kmx[0:1, h * 2 + tc:h * 2 + tc + 1], in_=kk[:, 0:512],
                     axis=AX.X),
                 reads=["a3_kk"], writes=["a3_kmx"])
        B.op("vector",
             lambda e, h=h: e.tensor_max(out=kmo[0:1, h:h + 1],
                                         in0=kmx[0:1, h * 2:h * 2 + 1],
                                         in1=kmx[0:1, h * 2 + 1:h * 2 + 2]),
             reads=["a3_kmx"], writes=["a3_kmo"])
    B.dma("scalar", km_out.ap(), kmo[:, :], reads=["a3_kmo"], writes=["kmax"])
    return B


def build_a3b(B: Builder, io=None):
    ctx = Ctx(B)
    if io is None:
        io = {}
        io["x_in"] = B.dram("x_in", [D_MODEL, TOK_PER_CORE], F32,
                            kind="ExternalInput")
        io["x_out"] = B.dram("x_out", [D_MODEL, TOK_PER_CORE], F32,
                             kind="ExternalOutput")
        io["qn"] = B.dram("qn", [2048, TOK_PER_CORE], BF16, kind="ExternalInput")
        io["qpe"] = B.dram("qpe", [64, 16 * TOK_PER_CORE], BF16,
                           kind="ExternalInput")
        io["knope"] = B.dram("knope", [2048, T_SEQ], BF16, kind="ExternalInput")
        io["kpe"] = B.dram("kpe", [64, T_SEQ], BF16, kind="ExternalInput")
        io["v_tok"] = B.dram("v_tok", [T_SEQ, 2048], BF16, kind="ExternalInput")
        io["woT"] = B.dram("woT", [2048, 2048], F32, kind="ExternalInput")
        io["ab"] = B.dram("ab", [128, 32], F32, kind="ExternalInput")
        io["diag"] = B.dram("diag", [128, 4 * 512], F32, kind="ExternalInput")
        io["identbf"] = B.dram("identbf_d", [128, 128], F32,
                               kind="ExternalInput")
        for i in range(4):
            io[f"kmax{i}"] = B.dram(f"kmax{i}", [1, 16], F32,
                                    kind="ExternalInput").ap()
    ctx.io = io
    NH = 16
    idbf = ctx.sb("f_idbf", [128, 128], BF16)
    B.dma("gpsimd", idbf[:, :], io["identbf"].ap(), reads=[], writes=["f_idbf"])
    ab = ctx.sb("f_ab", [128, 32], F32)
    B.dma("sync", ab[:, :], io["ab"].ap(), reads=[], writes=["f_ab"])
    diag = ctx.sb("f_diag", [128, 4 * 512], BF16)
    B.dma("gpsimd", diag[:].rearrange("p (k t) -> p k t", k=4),
          io["diag"].ap().rearrange("p (k t) -> p k t", k=4), reads=[],
          writes=["f_diag"])
    km = ctx.sb("f_km", [1, 16], F32)
    kmt = ctx.sb("f_kmt", [1, 16], F32)
    B.dma("sync", km[:, :], io["kmax0"], reads=["d_kv3_ag"], writes=["f_km"])
    for i in range(1, 4):
        B.dma("sync", kmt[:, :], io[f"kmax{i}"], reads=["d_kv3_ag"],
              writes=["f_kmt"])
        B.op("vector", lambda e: e.tensor_max(out=km[:, :], in0=km[:, :],
                                              in1=kmt[:, :]),
             reads=["f_kmt"], writes=["f_km"])
    qsb = ctx.sb("f_q", [128, 16 * 512], BF16)
    qpe_sb = ctx.sb("f_qpe", [64, 16 * 512], BF16)
    attn = ctx.sb("f_attn", [128, 16 * 1024], BF16)
    kch = ctx.sb("f_kch", [128, 16 * 512], BF16)
    kpch = ctx.sb("f_kpch", [64, 512], BF16)
    vch = ctx.sb("f_vch", [128, 4 * 2048], BF16)
    vaug = ctx.sb("f_vaug", [128, 4 * 129], BF16)
    for ks in range(4):
        B.op("vector", lambda e, ks=ks: e.memset(
                 vaug[:, ks * 129 + 128:ks * 129 + 129], 1.0),
             writes=["f_vaug"])
    outsb = ctx.sb("f_out", [128, 64 * 129], F32)
    qsq = ctx.sb("f_qsq", [128, 512], F32)
    qmx = ctx.sb("f_qmx", [1, 16], F32)
    bsq = ctx.sb("f_bsq", [1, 16], F32)
    nbc = ctx.sb("f_nbc", [128, 16], F32)
    pt = ctx.sb("f_pt", [128, 512], BF16)
    msk = ctx.sb("f_msk", [128, 4 * 512], BF16)
    stg = ctx.sb("f_stg", [128, 128], BF16)
    psS = ctx.ps("psA")
    psC = ctx.ps("psC")
    psCb = ctx.ps("psCb", dtype=BF16)
    psO = [ctx.ps(f"psO{i}", (128, 512)) for i in range(4)]
    onec = ctx.onescol_f32()
    onesr = ctx.onesrow_f32()
    kn_v = io["knope"].ap().rearrange("(a p) t -> p a t", p=128)
    vt_v = io["v_tok"].ap().rearrange("(s p) d -> p s d", p=128)
    for qb in range(2):
        q0 = qb * 512
        B.dma("sync", qsb[:].rearrange("p (a t) -> p a t", a=16),
              io["qn"].ap().rearrange("(a p) t -> p a t", p=128)[:, :, q0:q0 + 512],
              reads=["d_qn"], writes=["f_q"])
        B.dma("sync", qpe_sb[:].rearrange("p (a t) -> p a t", a=16),
              io["qpe"].ap().rearrange("p (a t) -> p a t", a=16)[:, :, q0:q0 + 512],
              reads=["d_qpe"], writes=["f_qpe"])
        for a in range(NH):
            B.op("scalar",
                 lambda e, a=a: e.activation(
                     out=qsq[:, 0:512], in_=qsb[:, a * 512:(a + 1) * 512],
                     func=AF.Square),
                 reads=["f_q"], writes=["f_qsq"])
            B.op("tensor",
                 lambda e: e.matmul(psC[0:1, 0:512], onec[:, :], qsq[:, 0:512],
                                    start=True, stop=False),
                 reads=["f_qsq", "c_ones_col32"], writes=["psC"])
            B.op("scalar",
                 lambda e, a=a: e.activation(
                     out=qsq[0:64, 0:512],
                     in_=qpe_sb[:, a * 512:(a + 1) * 512],
                     func=AF.Square),
                 reads=["f_qpe"], writes=["f_qsq"])
            B.op("tensor",
                 lambda e: e.matmul(psC[0:1, 0:512], onec[0:64, :], qsq[0:64, 0:512],
                                    start=False, stop=True),
                 reads=["f_qsq", "c_ones_col32"], writes=["psC"])
            B.op("vector",
                 lambda e, a=a: e.reduce_max(out=qmx[0:1, a:a + 1],
                                             in_=psC[0:1, 0:512], axis=AX.X),
                 reads=["psC"], writes=["f_qmx"])
            B.op("vector",
                 lambda e, a=a: e.tensor_mul(out=bsq[0:1, a:a + 1],
                                             in0=qmx[0:1, a:a + 1],
                                             in1=km[0:1, a:a + 1]),
                 reads=["f_qmx", "f_km"], writes=["f_bsq"])
        B.op("scalar", lambda e: e.activation(out=bsq[0:1, 0:16], in_=bsq[0:1, 0:16],
                                              func=AF.Ln),
             reads=["f_bsq"], writes=["f_bsq"])
        B.op("scalar", lambda e: e.activation(out=bsq[0:1, 0:16], in_=bsq[0:1, 0:16],
                                              func=AF.Exp, scale=0.5),
             reads=["f_bsq"], writes=["f_bsq"])
        B.op("scalar", lambda e: e.activation(out=bsq[0:1, 0:16], in_=bsq[0:1, 0:16],
                                              func=AF.Copy, scale=-1.0),
             reads=["f_bsq"], writes=["f_bsq"])
        B.op("tensor", lambda e: e.matmul(psC[:, 0:16], onesr[:, :], bsq[0:1, 0:16],
                                          start=True, stop=True),
             reads=["f_bsq", "c_ones_row32"], writes=["psC"])
        B.op("vector", lambda e: e.tensor_copy(out=nbc[:, 0:16], in_=psC[:, 0:16]),
             reads=["psC"], writes=["f_nbc"])
        first = True
        for c in range(8):
            B.dma("sync", kch[:].rearrange("p (a t) -> p a t", a=16),
                  kn_v[:, :, c * 512:(c + 1) * 512], reads=["d_kv3_ag"],
                  writes=["f_kch"])
            B.dma("sync", kpch[:, :], io["kpe"].ap()[:, c * 512:(c + 1) * 512],
                  reads=["d_kv3_ag"], writes=["f_kpch"])
            B.dma("scalar", vch[:].rearrange("p (s d) -> p s d", s=4),
                  vt_v[:, c * 4:c * 4 + 4, :], reads=["d_kv3_ag"],
                  writes=["f_vch"])
            abidx = qb * 16 + c * 2
            for ks in range(4):
                B.op("vector",
                     lambda e, ks=ks, abidx=abidx: e.tensor_scalar(
                         out=msk[:, ks * 512:(ks + 1) * 512],
                         in0=diag[:, ks * 512:(ks + 1) * 512],
                         scalar1=ab[:, abidx + 1:abidx + 2],
                         scalar2=ab[:, abidx:abidx + 1],
                         op0=ALU.mult, op1=ALU.add),
                     reads=["f_diag", "f_ab"], writes=["f_msk"])
            for a in range(NH):
                for ks in range(4):
                    B.op("tensor",
                         lambda e, a=a, ks=ks: e.matmul(
                             psS[:, 0:512],
                             kch[:, a * 512 + ks * 128:a * 512 + ks * 128 + 128],
                             qsb[:, a * 512:(a + 1) * 512],
                             start=True, stop=False),
                         reads=["f_kch", "f_q"], writes=["psA"])
                    B.op("tensor",
                         lambda e, a=a, ks=ks: e.matmul(
                             psS[:, 0:512],
                             kpch[:, ks * 128:(ks + 1) * 128],
                             qpe_sb[:, a * 512:(a + 1) * 512],
                             start=False, stop=True),
                         reads=["f_kpch", "f_qpe"], writes=["psA"])
                    B.op("scalar",
                         lambda e, a=a: e.activation(
                             out=pt[:, 0:512], in_=psS[:, 0:512], func=AF.Exp,
                             bias=nbc[:, a:a + 1]),
                         reads=["psA", "f_nbc"], writes=["f_pt"])
                    B.op("vector",
                         lambda e, ks=ks: e.tensor_mul(
                             out=pt[:, 0:512], in0=pt[:, 0:512],
                             in1=msk[:, ks * 512:(ks + 1) * 512]),
                         reads=["f_msk"], writes=["f_pt"])
                    B.op("vector",
                         lambda e, a=a, ks=ks: e.tensor_copy(
                             out=vaug[:, ks * 129:ks * 129 + 128],
                             in_=vch[:, ks * 512 + 0:ks * 512 + 0]
                             if False else
                             vch[:].rearrange("p (s d) -> p s d", s=4)[:, ks,
                                                                      a * 128:
                                                                      (a + 1) * 128]),
                         reads=["f_vch"], writes=["f_vaug"])
                    for qs in range(4):
                        B.op("tensor",
                             lambda e, ks=ks, qs=qs: e.matmul(
                                 psO[qs][:, 0:129],
                                 pt[:, qs * 128:(qs + 1) * 128],
                                 vaug[:, ks * 129:(ks + 1) * 129],
                                 start=(ks == 0), stop=(ks == 3)),
                             reads=["f_pt", "f_vaug"], writes=[f"psO{qs}"])
                for qs in range(4):
                    oidx = (a * 4 + qs) * 129
                    if first:
                        B.op("vector",
                             lambda e, qs=qs, oidx=oidx: e.tensor_copy(
                                 out=outsb[:, oidx:oidx + 129],
                                 in_=psO[qs][:, 0:129]),
                             reads=[f"psO{qs}"], writes=["f_out"])
                    else:
                        B.op("vector",
                             lambda e, qs=qs, oidx=oidx: e.tensor_add(
                                 out=outsb[:, oidx:oidx + 129],
                                 in0=outsb[:, oidx:oidx + 129],
                                 in1=psO[qs][:, 0:129]),
                             reads=[f"psO{qs}"], writes=["f_out"])
            first = False
        rcol = ctx.sb("f_rcol", [128, 1], F32)
        for a in range(NH):
            for qs in range(4):
                oidx = (a * 4 + qs) * 129
                B.op("vector",
                     lambda e, oidx=oidx: e.reciprocal(
                         out=rcol[:, :], in_=outsb[:, oidx + 128:oidx + 129]),
                     reads=["f_out"], writes=["f_rcol"])
                B.op("vector",
                     lambda e, oidx=oidx: e.tensor_scalar(
                         out=stg[:, 0:128], in0=outsb[:, oidx:oidx + 128],
                         scalar1=rcol[:, :], scalar2=None, op0=ALU.mult),
                     reads=["f_out", "f_rcol"], writes=["f_stg"])
                B.op("tensor",
                     lambda e: e.transpose(psCb[:, 0:128], stg[:, 0:128],
                                           idbf[:, :]),
                     reads=["f_stg", "f_idbf"], writes=["psCb"])
                B.op("vector",
                     lambda e, a=a, qs=qs, q0=q0: e.tensor_copy(
                         out=attn[:, a * 1024 + q0 + qs * 128:
                                  a * 1024 + q0 + (qs + 1) * 128],
                         in_=psCb[:, 0:128]),
                     reads=["psCb"], writes=["f_attn"])
    x_sb = ctx.sb("f_x", [128, 16 * 512], F32)
    wsl = ctx.sb("f_wsl", [128, 16 * 512], BF16)
    wo_v = io["woT"].ap().rearrange("(a p) n -> p a n", p=128)
    xv = io["x_in"].ap().rearrange("(a p) t -> p a t", p=128)
    xov = io["x_out"].ap().rearrange("(a p) t -> p a t", p=128)
    psW = ctx.ps("psD")
    for chunk in range(2):
        t0 = chunk * 512
        B.dma("sync", x_sb[:].rearrange("p (a t) -> p a t", a=16),
              xv[:, :, t0:t0 + 512], reads=["x_in"], writes=["f_x"])
        for s in range(4):
            B.dma("gpsimd", wsl[:].rearrange("p (a n) -> p a n", a=16),
                  wo_v[:, :, s * 512:(s + 1) * 512], reads=["woT"], writes=["f_wsl"])
            for dc in range(4):
                d = s * 4 + dc
                for a in range(16):
                    B.op("tensor",
                         lambda e, s=s, dc=dc, a=a, t0=t0: e.matmul(
                             psW[:, 0:512],
                             wsl[:, a * 512 + dc * 128:a * 512 + (dc + 1) * 128],
                             attn[:, a * 1024 + t0:a * 1024 + t0 + 512],
                             start=(a == 0), stop=(a == 15)),
                         reads=["f_wsl", "f_attn"], writes=["psD"])
                B.op("vector",
                     lambda e, d=d: e.tensor_add(out=x_sb[:, d * 512:(d + 1) * 512],
                                                 in0=x_sb[:, d * 512:(d + 1) * 512],
                                                 in1=psW[:, 0:512]),
                     reads=["psD"], writes=["f_x"])
                B.dma("scalar", xov[:, d, t0:t0 + 512], x_sb[:, d * 512:(d + 1) * 512],
                      reads=["f_x"], writes=["x_out"])
    return B


def host_a3_inputs(inputs, core):
    Wq = np.asarray(inputs["L1_attn_a_Wq"], np.float32)
    NH, DQ, DN, DR = 16, 192, 128, 64
    perm = np.empty(3072, np.int64)
    for h in range(NH):
        perm[h * DN:(h + 1) * DN] = np.arange(h * DQ, h * DQ + DN)
        perm[2048 + h * DR:2048 + (h + 1) * DR] = np.arange(h * DQ + DN,
                                                            h * DQ + DQ)
    WqP = Wq[perm]
    Wkva = np.asarray(inputs["L1_attn_a_Wkva"], np.float32)
    Wkvb = np.asarray(inputs["L1_attn_a_Wkvb"], np.float32)
    kperm = np.empty(2048, np.int64)
    vperm = np.empty(2048, np.int64)
    for h in range(NH):
        kperm[h * 128:(h + 1) * 128] = np.arange(h * 256, h * 256 + 128)
        vperm[h * 128:(h + 1) * 128] = np.arange(h * 256 + 128, h * 256 + 256)
    Wo = np.asarray(inputs["L1_attn_a_Wo"], np.float32)
    cos = np.asarray(inputs["L1_attn_a_cos"], np.float32).reshape(64)
    sin = np.asarray(inputs["L1_attn_a_sin"], np.float32).reshape(64)
    cos2 = np.stack([np.concatenate([cos, cos]),
                     np.concatenate([sin, sin])], axis=1)
    csraw = np.stack([cos, sin], axis=1)
    blks = core_blocks(core)
    ab = np.zeros(32, np.float32)
    for qi, qpos in enumerate(blks):
        for c in range(8):
            ab[qi * 16 + c * 2] = 1.0 if c < qpos else 0.0
            ab[qi * 16 + c * 2 + 1] = 1.0 if c == qpos else 0.0
    ab_rep = np.ascontiguousarray(np.broadcast_to(ab[None, :], (128, 32)))
    dg = np.zeros((128, 4 * 512), np.float32)
    for ks in range(4):
        kk = np.arange(128)[:, None] + ks * 128
        qq = np.arange(512)[None, :]
        dg[:, ks * 512:(ks + 1) * 512] = (kk <= qq).astype(np.float32)
    return {
        "wqTp": np.ascontiguousarray(WqP.T),
        "wkvaT": np.ascontiguousarray(Wkva.T),
        "wkvbTk": np.ascontiguousarray(Wkvb[kperm].T),
        "wkvbTv": np.ascontiguousarray(Wkvb[vperm].T),
        "gam": col_layout(inputs["L1_attn_a_kv_a_norm_weight"]),
        "cos2": np.ascontiguousarray(cos2),
        "csraw": np.ascontiguousarray(csraw),
        "woT": np.ascontiguousarray(Wo.T),
        "ab": ab_rep,
        "diag": dg,
        "identbf_d": np.eye(128, dtype=np.float32),
    }



def _run(build_fn, maps, cores=None):
    nc = bass.Bass("TRN2", target_bir_lowering=False)
    Bld = Builder(nc)
    build_fn(Bld)
    Bld.finalize()
    return Bld.run(maps, core_ids=cores or list(range(N_CORES)))


def kernel_unfused(**inputs):
    x0 = np.asarray(inputs["L0_attn_a_x"], np.float32)
    shards = shard_tokens(x0)
    x_fm = [np.ascontiguousarray(s.T) for s in shards]

    wres = _run(build_wprep, [host_big_weights(inputs)], cores=[0])
    wbf = {k: wres.results[0][k] for k in
           ("w1T_bf", "w3T_bf", "w2T_bf", "b1T_bf", "b3T_bf", "a2T_bf",
            "wupT_bf", "wgateT_bf", "wdownT_bf")}

    h_a1 = [host_a1_inputs(inputs, c) for c in range(8)]
    h_a2 = host_a2_inputs(inputs)
    h_a3 = [host_a3_inputs(inputs, c) for c in range(8)]

    ka = ("wqT", "wkvT", "bq", "bkv", "peT", "identbf_d")
    r = _run(build_a1a, [dict({"x_in": x_fm[c]}, **{k: h_a1[c][k] for k in ka})
                         for c in range(8)])
    kfull, vfull = {}, {}
    for b in range(2):
        kf = np.empty((512, T_SEQ), dtype=r.results[0]["k_fm"].dtype)
        vf = np.empty((T_SEQ, 512), dtype=r.results[0]["v_tok"].dtype)
        for j in range(4):
            c = b * 4 + j
            for i, blk in enumerate(core_blocks(c)):
                kf[:, blk * 512:(blk + 1) * 512] = \
                    r.results[c]["k_fm"][:, i * 512:(i + 1) * 512]
                vf[blk * 512:(blk + 1) * 512] = \
                    r.results[c]["v_tok"][i * 512:(i + 1) * 512]
        kfull[b], vfull[b] = np.ascontiguousarray(kf), np.ascontiguousarray(vf)
    maps = []
    for c in range(8):
        b = c // 4
        m = {"x_in": x_fm[c], "qp": r.results[c]["qp"], "k_fm": kfull[b],
             "v_tok": vfull[b], "woT": h_a1[c]["woT"], "ab": h_a1[c]["ab"],
             "diag": h_a1[c]["diag"], "identbf_d": h_a1[c]["identbf_d"]}
        for i in range(4):
            m[f"kmax{i}"] = r.results[b * 4 + i]["kmax"]
        maps.append(m)
    r = _run(build_a1b, maps)
    x_fm = [r.results[c]["x_out"] for c in range(8)]

    r = _run(build_f0, [host_f0_inputs(inputs, wbf, x_fm[c]) for c in range(8)])
    x_fm = [r.results[c]["x_out"] for c in range(8)]

    r = _run(build_a2a, [{"x_in": x_fm[c], "wqT": h_a2["wqT"],
                          "wkvT": h_a2["wkvT"]} for c in range(8)])
    maps = []
    for c in range(8):
        grp = (c // 4) * 4
        m = {"x_in": x_fm[c], "q_fm": r.results[c]["q_fm"],
             "wprojT": h_a2["wprojT"], "bproj": h_a2["bproj"]}
        for i in range(4):
            m[f"vkk{i}"] = r.results[grp + i]["vkk"]
        maps.append(m)
    r = _run(build_a2b, maps)
    x_fm = [r.results[c]["x_out"] for c in range(8)]

    r = _run(build_f0, [host_f0_inputs(inputs, wbf, x_fm[c]) for c in range(8)])
    x_fm = [r.results[c]["x_out"] for c in range(8)]

    ka3 = ("wqTp", "wkvaT", "wkvbTk", "wkvbTv", "gam", "cos2", "csraw")
    r = _run(build_a3a, [dict({"x_in": x_fm[c]}, **{k: h_a3[c][k] for k in ka3})
                         for c in range(8)])
    knf, kpf, vtf = {}, {}, {}
    for b in range(2):
        kn = np.empty((2048, T_SEQ), dtype=r.results[0]["knope"].dtype)
        kp = np.empty((64, T_SEQ), dtype=r.results[0]["kpe"].dtype)
        vt = np.empty((T_SEQ, 2048), dtype=r.results[0]["v_tok"].dtype)
        for j in range(4):
            c = b * 4 + j
            for i, blk in enumerate(core_blocks(c)):
                kn[:, blk * 512:(blk + 1) * 512] = \
                    r.results[c]["knope"][:, i * 512:(i + 1) * 512]
                kp[:, blk * 512:(blk + 1) * 512] = \
                    r.results[c]["kpe"][:, i * 512:(i + 1) * 512]
                vt[blk * 512:(blk + 1) * 512] = \
                    r.results[c]["v_tok"][i * 512:(i + 1) * 512]
        knf[b], kpf[b], vtf[b] = map(np.ascontiguousarray, (kn, kp, vt))
    maps = []
    for c in range(8):
        b = c // 4
        m = {"x_in": x_fm[c], "qn": r.results[c]["qn"], "qpe": r.results[c]["qpe"],
             "knope": knf[b], "kpe": kpf[b], "v_tok": vtf[b],
             "woT": h_a3[c]["woT"], "ab": h_a3[c]["ab"], "diag": h_a3[c]["diag"],
             "identbf_d": h_a3[c]["identbf_d"]}
        for i in range(4):
            m[f"kmax{i}"] = r.results[b * 4 + i]["kmax"]
        maps.append(m)
    r = _run(build_a3b, maps)
    x_fm = [r.results[c]["x_out"] for c in range(8)]

    r = _run(build_f1, [host_f1_inputs(inputs, wbf, x_fm[c]) for c in range(8)])
    x_fm = [r.results[c]["x_out"] for c in range(8)]

    xpT = np.ascontiguousarray(np.asarray(inputs["L1_attn_b_x_proj_w"],
                                          np.float32).T)
    r = _run(build_a4a, [{"x_in": x_fm[c], "xprojT": xpT} for c in range(8)])
    xln_shards = [r.results[c]["xln"] for c in range(8)]
    xdbl_shards = [r.results[c]["xdbl"] for c in range(8)]
    r = _run(build_a4b, [host_a4b_inputs(inputs, xln_shards, xdbl_shards, c)
                         for c in range(8)])
    y_slices = [r.results[c]["y"] for c in range(8)]
    r = _run(build_a4c, [{"x_in": x_fm[c],
                          "y_in": host_a4_reshard_y(y_slices, c)}
                         for c in range(8)])
    x_fm = [r.results[c]["x_out"] for c in range(8)]

    r = _run(build_f1, [host_f1_inputs(inputs, wbf, x_fm[c]) for c in range(8)])
    x_fm = [r.results[c]["x_out"] for c in range(8)]

    out = unshard_tokens([np.ascontiguousarray(np.asarray(x).T) for x in x_fm])
    return np.ascontiguousarray(out, dtype=np.float32)



I32 = mybir.dt.int32


def build_a4y(B: Builder, io):
    ctx = Ctx(B)
    yv = io["y_loc"].ap()
    ytv = io["y_T"].ap().rearrange("(c p) d -> p c d", p=128)
    ysb = ctx.sb("y_ysb", [128, T_SEQ], BF16)
    stage = ctx.sb("y_stage", [128, 32 * 128], BF16)
    idbf = ctx.sb("y_idbf", [128, 128], BF16)
    B.dma("gpsimd", idbf[:, :], io["identbf"].ap(), reads=[], writes=["y_idbf"])
    pst = ctx.ps("psA", dtype=BF16)
    for g in range(4):
        B.dma("gpsimd", ysb[:, :], yv[g * 128:(g + 1) * 128, :], reads=[],
              writes=["y_ysb"])
        for cchunk in range(32):
            B.op("tensor",
                 lambda e, cchunk=cchunk: e.transpose(
                     pst[:, 0:128], ysb[:, cchunk * 128:(cchunk + 1) * 128],
                     idbf[:, :]),
                 reads=["y_ysb", "y_idbf"], writes=["psA"])
            B.op("vector",
                 lambda e, cchunk=cchunk: e.tensor_copy(
                     out=stage[:, cchunk * 128:(cchunk + 1) * 128],
                     in_=pst[:, 0:128]),
                 reads=["psA"], writes=["y_stage"])
        B.dma("scalar", ytv[:, :, g * 128:(g + 1) * 128],
              stage[:].rearrange("p (c d) -> p c d", c=32),
              reads=["y_stage"], writes=["y_T"])


def build_a4gather_x(B: Builder, io):
    ctx = Ctx(B)
    gidx = ctx.sb("gx_idx", [128, 16], I32)
    B.dma("gpsimd", gidx[:, :], io["a4_gidx"].ap(), reads=[], writes=["gx_idx"])
    tile = ctx.sb("gx_tile", [128, 1024], BF16)
    agv = io["xlnag"].ap()
    dst = io["xsl_pos"].ap()
    for r in range(4):
        blks = core_blocks(r)
        for sub in range(4):
            B.op("gpsimd",
                 lambda e, r=r, sub=sub: e.indirect_dma_start(
                     tile[:, :], None, agv,
                     IndirectOffsetOnAxis(gidx[:, r * 4 + sub:r * 4 + sub + 1],
                                          0)),
                 reads=["gx_idx", "cc_out"], writes=["gx_tile"], kind="dma")
            for slot in range(2):
                b = blks[slot]
                B.dma("sync", dst[sub * 128:(sub + 1) * 128,
                                  b * 512:(b + 1) * 512],
                      tile[:, slot * 512:(slot + 1) * 512],
                      reads=["gx_tile"], writes=["gx_tile"])
    for i in range(4):
        lo, hi = core_blocks(i)
        B.dma("scalar", io["dt_pos"].ap()[:, lo * 512:(lo + 1) * 512],
              io["xdblag"].ap()[i * 160:i * 160 + 128, 0:512],
              reads=["cc_out"], writes=["dtp"])
        B.dma("scalar", io["dt_pos"].ap()[:, hi * 512:(hi + 1) * 512],
              io["xdblag"].ap()[i * 160:i * 160 + 128, 512:1024],
              reads=["cc_out"], writes=["dtp"])
        B.dma("gpsimd", io["bc_pos"].ap()[:, lo * 512:(lo + 1) * 512],
              io["xdblag"].ap()[i * 160 + 128:i * 160 + 160, 0:512],
              reads=["cc_out"], writes=["bcp"])
        B.dma("gpsimd", io["bc_pos"].ap()[:, hi * 512:(hi + 1) * 512],
              io["xdblag"].ap()[i * 160 + 128:i * 160 + 160, 512:1024],
              reads=["cc_out"], writes=["bcp"])


def build_a4gather_y(B: Builder, io):
    ctx = Ctx(B)
    yidx = ctx.sb("gy_idx", [128, 32], I32)
    B.dma("gpsimd", yidx[:, :], io["a4_yidx"].ap(), reads=[], writes=["gy_idx"])
    tile = ctx.sb("gy_tile", [128, 512], BF16)
    agv = io["yTag"].ap()
    dst = io["y_finT"].ap()
    for r in range(4):
        for sub in range(8):
            B.op("gpsimd",
                 lambda e, r=r, sub=sub: e.indirect_dma_start(
                     tile[:, 0:512], None, agv,
                     IndirectOffsetOnAxis(yidx[:, r * 8 + sub:r * 8 + sub + 1], 0)),
                 reads=["gy_idx", "cc_out"], writes=["gy_tile"], kind="dma")
            B.dma("sync", dst[sub * 128:(sub + 1) * 128, r * 512:(r + 1) * 512],
                  tile[:, 0:512], reads=["gy_tile"], writes=["gy_tile"])


def build_a4c_fused(B: Builder, io):
    ctx = Ctx(B)
    x_sb = ctx.sb("c_x", [128, 16 * 512], F32)
    y_sb = ctx.sb("c_y", [128, 16 * 128], BF16)
    idbf = ctx.sb("c_idbf", [128, 128], BF16)
    B.dma("gpsimd", idbf[:, :], io["identbf"].ap(), reads=[], writes=["c_idbf"])
    pst = ctx.ps("psA", dtype=BF16)
    xv = io["x_in"].ap().rearrange("(a p) t -> p a t", p=128)
    ytv = io["y_finT"].ap().rearrange("(s p) d -> p s d", p=128)
    xov = io["x_out"].ap().rearrange("(a p) t -> p a t", p=128)
    for chunk in range(2):
        t0 = chunk * 512
        B.dma("sync", x_sb[:].rearrange("p (a t) -> p a t", a=16),
              xv[:, :, t0:t0 + 512], reads=["cfx_in"], writes=["c_x"])
        for sub in range(4):
            gs = chunk * 4 + sub
            B.dma("scalar", y_sb[:].rearrange("p (a d) -> p a d", a=16),
                  ytv[:, gs, :].rearrange("p (a d) -> p a d", a=16),
                  reads=["cfy"], writes=["c_y"])
            for a in range(16):
                B.op("tensor",
                     lambda e, a=a: e.transpose(pst[:, 0:128],
                                                y_sb[:, a * 128:(a + 1) * 128],
                                                idbf[:, :]),
                     reads=["c_y", "c_idbf"], writes=["psA"])
                B.op("vector",
                     lambda e, a=a, sub=sub: e.tensor_add(
                         out=x_sb[:, a * 512 + sub * 128:a * 512 + (sub + 1) * 128],
                         in0=x_sb[:, a * 512 + sub * 128:a * 512 + (sub + 1) * 128],
                         in1=pst[:, 0:128]),
                     reads=["psA"], writes=["c_x"])
        for a in range(16):
            B.dma("gpsimd", xov[:, a, t0:t0 + 512], x_sb[:, a * 512:(a + 1) * 512],
                  reads=["c_x"], writes=["cfx_out"])



def rank_of_block(b):
    return b if b < 4 else 7 - b


def slot_of_block(b):
    return 0 if b < 4 else 1


def build_fused(B: Builder, upto=99):
    nc = B.nc
    dr = B.dram
    io = {}
    x0 = dr("x0", [D_MODEL, TOK_PER_CORE], F32, kind="ExternalInput")
    xout = dr("x_final", [D_MODEL, TOK_PER_CORE], F32, kind="ExternalOutput")
    wio = {}
    for name, (din, dout) in BIG_WEIGHTS.items():
        wio[name] = dr(name, [din, dout], F32, kind="ExternalInput")
        wio[name + "_bf"] = dr("i_" + name + "_bf", [din, dout], BF16)
    ext = {}
    for nm, shp in (
        ("gateT_d", (2048, 8)), ("a1l_d", (2048, 128)), ("a3l_d", (2048, 128)),
        ("b2l_d", (16, 8 * 2048)), ("ident32_d", (128, 128)),
        ("sel32_d", (8, 8 * 128)),
        ("p_su_ug_d", (128, 16)), ("p_su_dp_d", (128, 64)),
        ("p_wsu_up_d", (128, 64)), ("p_wsu_gate_d", (128, 64)),
        ("p_wsdp_d", (128, 16)), ("c_h128a_d", (128, 128)),
        ("c_h128b_d", (128, 128)),
        ("a1_wqT", (2048, 2048)), ("a1_wkvT", (2048, 1024)),
        ("a1_bq", (128, 16)), ("a1_bkv", (128, 8)),
        ("a1_peT", (128, TOK_PER_CORE)), ("a1_woT", (2048, 2048)),
        ("ab_d", (128, 32)), ("diag_d", (128, 4 * 512)),
        ("identbf_d", (128, 128)),
        ("a2_wqT", (2048, 2048)), ("a2_wkvT", (2048, 4096)),
        ("a2_wprojT", (2048, 2048)), ("a2_bproj", (128, 16)),
        ("a3_wqTp", (2048, 3072)), ("a3_wkvaT", (2048, 576)),
        ("a3_wkvbTk", (512, 2048)), ("a3_wkvbTv", (512, 2048)),
        ("a3_gam", (128, 4)), ("a3_cos2", (128, 2)), ("a3_csraw", (64, 2)),
        ("a3_woT", (2048, 2048)),
        ("a4_xprojT", (2048, 160)), ("a4_dtpT", (128, 512)),
        ("a4_dtb", (128, 4)), ("a4_alog", (128, 64)), ("a4_dcol", (128, 4)),
    ):
        ext[nm] = dr(nm, list(shp), F32, kind="ExternalInput")
    xa = dr("i_xa", [D_MODEL, TOK_PER_CORE], F32)
    xb = dr("i_xb", [D_MODEL, TOK_PER_CORE], F32)
    qp = dr("i_qp", [2048, TOK_PER_CORE], BF16)
    a1_kfl = dr("i_a1_kfl", [512, TOK_PER_CORE], BF16)
    a1_vtl = dr("i_a1_vtl", [TOK_PER_CORE, 512], BF16)
    a1_kml = dr("i_a1_kml", [1, 4], F32)
    a1_kag = dr("i_a1_kag", [4 * 512, TOK_PER_CORE], BF16)
    a1_vag = dr("i_a1_vag", [4 * TOK_PER_CORE, 512], BF16)
    a1_kmag = dr("i_a1_kmag", [4, 4], F32)
    a1_kpos = dr("i_a1_kpos", [512, T_SEQ], BF16)
    a1_vpos = dr("i_a1_vpos", [T_SEQ, 512], BF16)
    qfm = dr("i_qfm", [2048, TOK_PER_CORE], BF16)
    vkkl = dr("i_vkkl", [32, 64 * 33], F32)
    vkkag = dr("i_vkkag", [128, 64 * 33], F32)
    qn = dr("i_qn", [2048, TOK_PER_CORE], BF16)
    qpe = dr("i_qpe", [64, 16 * TOK_PER_CORE], BF16)
    knl = dr("i_knl", [2048, TOK_PER_CORE], BF16)
    kpel = dr("i_kpel", [64, TOK_PER_CORE], BF16)
    vt3l = dr("i_vt3l", [TOK_PER_CORE, 2048], BF16)
    km3l = dr("i_km3l", [1, 16], F32)
    knag = dr("i_knag", [4 * 2048, TOK_PER_CORE], BF16)
    kpeag = dr("i_kpeag", [4 * 64, TOK_PER_CORE], BF16)
    vt3ag = dr("i_vt3ag", [4 * TOK_PER_CORE, 2048], BF16)
    km3ag = dr("i_km3ag", [4, 16], F32)
    knpos = dr("i_knpos", [2048, T_SEQ], BF16)
    kpepos = dr("i_kpepos", [64, T_SEQ], BF16)
    vt3pos = dr("i_vt3pos", [T_SEQ, 2048], BF16)
    xln = dr("i_xln", [D_MODEL, TOK_PER_CORE], BF16)
    xdbl = dr("i_xdbl", [160, TOK_PER_CORE], F32)
    xlnag = dr("i_xlnag", [4 * D_MODEL, TOK_PER_CORE], BF16)
    xsl_pos = dr("i_xsl_pos", [512, T_SEQ], BF16)
    xdblag = dr("i_xdblag", [4 * 160, TOK_PER_CORE], F32)
    dt_pos = dr("i_dt_pos", [128, T_SEQ], F32)
    bc_pos = dr("i_bc_pos", [32, T_SEQ], F32)
    y_loc = dr("i_y_loc", [512, T_SEQ], F32)
    y_T = dr("i_y_T", [T_SEQ, 512], BF16)
    yTag = dr("i_yTag", [4 * T_SEQ, 512], BF16)
    y_finT = dr("i_y_finT", [TOK_PER_CORE, 2048], BF16)
    ext_i = {}
    ext_i["a4_gidx"] = dr("a4_gidx", [128, 16], I32, kind="ExternalInput")
    ext_i["a4_yidx"] = dr("a4_yidx", [128, 32], I32, kind="ExternalInput")

    def phase(fn, pio):
        B.begin_phase()
        fn(B, pio)
        B.end_phase()

    def cc_phase(ops, dmas=()):
        B.begin_phase()
        for kind, op, in_ap, out_ap in ops:
            B.cc(kind, op, GROUPS4, [in_ap.opt()], [out_ap.opt()],
                 reads=[], writes=["cc_out"])
        for q, dst, srcp in dmas:
            B.dma(q, dst, srcp, reads=["cc_out"], writes=["cc_out"])
        B.end_phase()

    def finish_early(xcur):
        B.begin_phase()
        B.dma("sync", xout.ap(), xcur.ap(), reads=[], writes=["fin"])
        B.end_phase()

    if upto >= 1:
        pio = dict(wio)
        phase(build_wprep, pio)
    if upto == 1:
        finish_early(x0)
        return B

    pio = {"x_in": x0, "wqT": ext["a1_wqT"], "wkvT": ext["a1_wkvT"],
           "bq": ext["a1_bq"], "bkv": ext["a1_bkv"], "peT": ext["a1_peT"],
           "qp": qp, "k_fm_loc": a1_kfl, "v_tok_loc": a1_vtl,
           "kmax_loc": a1_kml, "identbf": ext["identbf_d"]}
    phase(build_a1a, pio)
    dmas = []
    for b in range(8):
        r, s = rank_of_block(b), slot_of_block(b)
        dmas.append(("sync",
                     a1_kpos.ap()[:, b * 512:(b + 1) * 512],
                     a1_kag.ap()[r * 512:(r + 1) * 512, s * 512:(s + 1) * 512]))
        dmas.append(("scalar",
                     a1_vpos.ap()[b * 512:(b + 1) * 512, :],
                     a1_vag.ap()[r * 1024 + s * 512:r * 1024 + s * 512 + 512, :]))
    cc_phase([("AllGather", ALU.bypass, a1_kfl.ap(), a1_kag.ap()),
              ("AllGather", ALU.bypass, a1_vtl.ap(), a1_vag.ap()),
              ("AllGather", ALU.bypass, a1_kml.ap(), a1_kmag.ap())], dmas)
    pio = {"x_in": x0, "x_out": xa, "qp": qp, "k_fm": a1_kpos, "v_tok": a1_vpos,
           "woT": ext["a1_woT"], "ab": ext["ab_d"], "diag": ext["diag_d"],
           "identbf": ext["identbf_d"]}
    for i in range(4):
        pio[f"kmax{i}"] = a1_kmag.ap()[i:i + 1, :]
    phase(build_a1b, pio)
    if upto == 2:
        finish_early(xa)
        return B

    pio = {"x_in": xa, "x_out": xb}
    for nm in ("w1T_bf", "w3T_bf", "w2T_bf", "b1T_bf", "b3T_bf", "a2T_bf"):
        pio[nm] = wio[nm]
    for nm in ("gateT_d", "a1l_d", "a3l_d", "b2l_d", "ident32_d", "sel32_d"):
        pio[nm] = ext[nm]
    phase(build_f0, pio)
    if upto == 3:
        finish_early(xb)
        return B

    pio = {"x_in": xb, "wqT": ext["a2_wqT"], "wkvT": ext["a2_wkvT"],
           "q_fm": qfm, "vkk": vkkl}
    phase(build_a2a, pio)
    cc_phase([("AllGather", ALU.bypass, vkkl.ap(), vkkag.ap())])
    pio = {"x_in": xb, "x_out": xa, "q_fm": qfm, "wprojT": ext["a2_wprojT"],
           "bproj": ext["a2_bproj"]}
    for i in range(4):
        pio[f"vkk{i}"] = vkkag.ap()[i * 32:(i + 1) * 32, :]
    phase(build_a2b, pio)
    if upto == 4:
        finish_early(xa)
        return B

    pio = {"x_in": xa, "x_out": xb}
    for nm in ("w1T_bf", "w3T_bf", "w2T_bf", "b1T_bf", "b3T_bf", "a2T_bf"):
        pio[nm] = wio[nm]
    for nm in ("gateT_d", "a1l_d", "a3l_d", "b2l_d", "ident32_d", "sel32_d"):
        pio[nm] = ext[nm]
    phase(build_f0, pio)
    if upto == 5:
        finish_early(xb)
        return B

    pio = {"x_in": xb, "wqT": ext["a3_wqTp"], "wkvaT": ext["a3_wkvaT"],
           "wkvbTk": ext["a3_wkvbTk"], "wkvbTv": ext["a3_wkvbTv"],
           "gam": ext["a3_gam"], "cos2": ext["a3_cos2"], "csraw": ext["a3_csraw"],
           "qn": qn, "qpe": qpe, "knope_loc": knl, "kpe_loc": kpel,
           "v_tok_loc": vt3l, "kmax_loc": km3l}
    phase(build_a3a, pio)
    dmas = []
    for b in range(8):
        r, s = rank_of_block(b), slot_of_block(b)
        dmas.append(("sync",
                     knpos.ap()[:, b * 512:(b + 1) * 512],
                     knag.ap()[r * 2048:(r + 1) * 2048, s * 512:(s + 1) * 512]))
        dmas.append(("scalar",
                     kpepos.ap()[:, b * 512:(b + 1) * 512],
                     kpeag.ap()[r * 64:(r + 1) * 64, s * 512:(s + 1) * 512]))
        dmas.append(("gpsimd",
                     vt3pos.ap()[b * 512:(b + 1) * 512, :],
                     vt3ag.ap()[r * 1024 + s * 512:r * 1024 + s * 512 + 512, :]))
    cc_phase([("AllGather", ALU.bypass, knl.ap(), knag.ap()),
              ("AllGather", ALU.bypass, kpel.ap(), kpeag.ap()),
              ("AllGather", ALU.bypass, vt3l.ap(), vt3ag.ap()),
              ("AllGather", ALU.bypass, km3l.ap(), km3ag.ap())], dmas)
    pio = {"x_in": xb, "x_out": xa, "qn": qn, "qpe": qpe, "knope": knpos,
           "kpe": kpepos, "v_tok": vt3pos, "woT": ext["a3_woT"],
           "ab": ext["ab_d"], "diag": ext["diag_d"], "identbf": ext["identbf_d"]}
    for i in range(4):
        pio[f"kmax{i}"] = km3ag.ap()[i:i + 1, :]
    phase(build_a3b, pio)
    if upto == 6:
        finish_early(xa)
        return B

    pio = {"x_in": xa, "x_out": xb, "wupT_bf": wio["wupT_bf"],
           "wgateT_bf": wio["wgateT_bf"], "wdownT_bf": wio["wdownT_bf"]}
    for nm in ("p_su_ug_d", "p_su_dp_d", "p_wsu_up_d", "p_wsu_gate_d",
               "p_wsdp_d", "c_h128a_d", "c_h128b_d"):
        pio[nm] = ext[nm]
    phase(build_f1, pio)
    if upto == 7:
        finish_early(xb)
        return B

    pio = {"x_in": xb, "xln": xln, "xdbl": xdbl, "xprojT": ext["a4_xprojT"]}
    phase(build_a4a, pio)
    cc_phase([("AllGather", ALU.bypass, xln.ap(), xlnag.ap()),
              ("AllGather", ALU.bypass, xdbl.ap(), xdblag.ap())])
    pio = {"a4_gidx": ext_i["a4_gidx"], "xlnag": xlnag, "xsl_pos": xsl_pos,
           "xdblag": xdblag, "dt_pos": dt_pos, "bc_pos": bc_pos}
    phase(build_a4gather_x, pio)
    pio = {"xsl": xsl_pos, "dt": dt_pos, "bc": bc_pos, "dtpT": ext["a4_dtpT"],
           "dtb": ext["a4_dtb"], "alog": ext["a4_alog"], "dcol": ext["a4_dcol"],
           "y": y_loc}
    phase(build_a4b, pio)
    pio = {"y_loc": y_loc, "y_T": y_T, "identbf": ext["identbf_d"]}
    phase(build_a4y, pio)
    cc_phase([("AllGather", ALU.bypass, y_T.ap(), yTag.ap())])
    pio = {"a4_yidx": ext_i["a4_yidx"], "yTag": yTag, "y_finT": y_finT}
    phase(build_a4gather_y, pio)
    pio = {"x_in": xb, "y_finT": y_finT, "x_out": xa, "identbf": ext["identbf_d"]}
    phase(build_a4c_fused, pio)
    if upto == 8:
        finish_early(xa)
        return B

    pio = {"x_in": xa, "x_out": xout, "wupT_bf": wio["wupT_bf"],
           "wgateT_bf": wio["wgateT_bf"], "wdownT_bf": wio["wdownT_bf"]}
    for nm in ("p_su_ug_d", "p_su_dp_d", "p_wsu_up_d", "p_wsu_gate_d",
               "p_wsdp_d", "c_h128a_d", "c_h128b_d"):
        pio[nm] = ext[nm]
    phase(build_f1, pio)
    return B


def host_fused_inputs(inputs, core):
    m = {}
    m.update(host_big_weights(inputs))
    h0 = host_f0_inputs(inputs, {k: None for k in ()}, None) if False else None
    a1 = np.asarray(inputs["L0_ffn_a1"], np.float32)
    a3 = np.asarray(inputs["L0_ffn_a3"], np.float32)
    b2 = np.asarray(inputs["L0_ffn_b2"], np.float32)
    sel = np.zeros((8, 8 * 128), np.float32)
    for ee in range(8):
        sel[ee, ee * 128:(ee + 1) * 128] = 1.0
    m["gateT_d"] = np.ascontiguousarray(np.asarray(inputs["L0_ffn_gate_w"],
                                                   np.float32).T)
    m["a1l_d"] = np.ascontiguousarray(a1.transpose(2, 0, 1).reshape(2048, 128))
    m["a3l_d"] = np.ascontiguousarray(a3.transpose(2, 0, 1).reshape(2048, 128))
    m["b2l_d"] = np.ascontiguousarray(b2.transpose(2, 0, 1).reshape(16, 8 * 2048))
    m["ident32_d"] = np.eye(128, dtype=np.float32)
    m["sel32_d"] = sel
    I = 8192
    ws = np.asarray(inputs["L1_ffn_Wscale_ug"], np.float32)
    m["p_su_ug_d"] = col_layout(inputs["L1_ffn_SU_ug"])
    m["p_su_dp_d"] = col_layout(inputs["L1_ffn_SU_dp"])
    m["p_wsu_up_d"] = col_layout(ws[:I])
    m["p_wsu_gate_d"] = col_layout(ws[I:])
    m["p_wsdp_d"] = col_layout(inputs["L1_ffn_Wscale_dp"])
    m["c_h128a_d"] = hadamard(128) * np.float32(1.0 / (math.sqrt(2048) * 64.0))
    m["c_h128b_d"] = hadamard(128) * np.float32(1.0 / (math.sqrt(8192) * 64.0))
    h1 = host_a1_inputs(inputs, core)
    m["a1_wqT"] = h1["wqT"]
    m["a1_wkvT"] = h1["wkvT"]
    m["a1_bq"] = h1["bq"]
    m["a1_bkv"] = h1["bkv"]
    m["a1_peT"] = h1["peT"]
    m["a1_woT"] = h1["woT"]
    m["ab_d"] = h1["ab"]
    m["diag_d"] = h1["diag"]
    m["identbf_d"] = h1["identbf_d"]
    h2 = host_a2_inputs(inputs)
    m["a2_wqT"] = h2["wqT"]
    m["a2_wkvT"] = h2["wkvT"]
    m["a2_wprojT"] = h2["wprojT"]
    m["a2_bproj"] = h2["bproj"]
    h3 = host_a3_inputs(inputs, core)
    m["a3_wqTp"] = h3["wqTp"]
    m["a3_wkvaT"] = h3["wkvaT"]
    m["a3_wkvbTk"] = h3["wkvbTk"]
    m["a3_wkvbTv"] = h3["wkvbTv"]
    m["a3_gam"] = h3["gam"]
    m["a3_cos2"] = h3["cos2"]
    m["a3_csraw"] = h3["csraw"]
    m["a3_woT"] = h3["woT"]
    ch0 = (core % 4) * 512
    alog = np.asarray(inputs["L1_attn_b_A_log"], np.float32)[ch0:ch0 + 512]
    dpar = np.asarray(inputs["L1_attn_b_D_param"], np.float32)[ch0:ch0 + 512]
    dtw = np.asarray(inputs["L1_attn_b_dt_proj_w"], np.float32)[ch0:ch0 + 512]
    dtb = np.asarray(inputs["L1_attn_b_dt_proj_b"], np.float32)[ch0:ch0 + 512]
    m["a4_xprojT"] = np.ascontiguousarray(
        np.asarray(inputs["L1_attn_b_x_proj_w"], np.float32).T)
    m["a4_dtpT"] = np.ascontiguousarray(dtw.T)
    m["a4_dtb"] = col_layout(dtb)
    m["a4_alog"] = np.ascontiguousarray(
        alog.reshape(4, 128, 16).transpose(1, 0, 2).reshape(128, 64))
    m["a4_dcol"] = col_layout(dpar)
    gidx = np.empty((128, 16), np.int32)
    for r in range(4):
        for sub in range(4):
            gidx[:, r * 4 + sub] = 2048 * r + ch0 + sub * 128 + np.arange(128)
    m["a4_gidx"] = gidx
    yidx = np.empty((128, 32), np.int32)
    blks = core_blocks(core)
    for r in range(4):
        for sub in range(8):
            loc = sub * 128 + np.arange(128)
            pos = np.where(loc < 512, blks[0] * 512 + loc,
                           blks[1] * 512 + (loc - 512))
            yidx[:, r * 8 + sub] = 4096 * r + pos
    m["a4_yidx"] = yidx
    return m


def kernel_fused(upto=99, **inputs):
    x0 = np.asarray(inputs["L0_attn_a_x"], np.float32)
    shards = shard_tokens(x0)
    x_fm = [np.ascontiguousarray(s.T) for s in shards]
    nc = bass.Bass("TRN2", target_bir_lowering=False)
    Bld = Builder(nc)
    build_fused(Bld, upto=upto)
    Bld.finalize()
    maps = []
    for c in range(8):
        m = host_fused_inputs(inputs, c)
        m["x0"] = x_fm[c]
        maps.append(m)
    res = Bld.run(maps)
    outs = [np.ascontiguousarray(np.asarray(res.results[c]["x_final"]).T)
            for c in range(8)]
    return np.ascontiguousarray(unshard_tokens(outs), dtype=np.float32)


def kernel(**inputs):
    return kernel_fused(**inputs)


# version 03 (final)
# speedup vs baseline: 1.3101x; 1.3101x over previous
import contextlib
import math
import numpy as np
import ml_dtypes
import concourse.bass as bass
import concourse.mybir as mybir
from concourse.bass_utils import run_bass_kernel_spmd
from concourse.bass import IndirectOffsetOnAxis

F32 = mybir.dt.float32
BF16 = mybir.dt.bfloat16
AF = mybir.ActivationFunctionType
ALU = mybir.AluOpType
AX = mybir.AxisListType
BF = ml_dtypes.bfloat16

N_CORES = 8
B_SZ = 2
T_SEQ = 4096
D_MODEL = 2048
BLK = 512
N_BLK = T_SEQ // BLK
TOK_PER_CORE = 2 * BLK

ENGINES = ("sync", "tensor", "vector", "scalar", "gpsimd")
SAME_ENGINE_FIFO = False
GROUPS4 = [[0, 1, 2, 3], [4, 5, 6, 7]]


def core_blocks(core):
    j = core % 4
    return [j, 7 - j]



class Op:
    __slots__ = ("engine", "emit", "kind", "deps", "idx", "sem_val", "milestone",
                 "tag", "buf", "sem_key", "phase")

    def __init__(self, engine, emit, kind, tag=""):
        self.engine = engine
        self.emit = emit
        self.kind = kind
        self.deps = []
        self.idx = -1
        self.sem_val = None
        self.milestone = False
        self.tag = tag
        self.buf = None
        self.sem_key = None
        self.phase = 0


class Builder:

    def __init__(self, nc):
        self.nc = nc
        self.stack = contextlib.ExitStack()
        self.phase_stack = None
        self.phase_idx = 0
        self.eng_ops = {e: [] for e in ENGINES}
        self.last_write = {}
        self.readers = {}
        self.names = set()
        self._finalized = False

    def begin_phase(self):
        assert self.phase_stack is None
        self.phase_idx += 1
        self.phase_stack = contextlib.ExitStack()

    def end_phase(self):
        lasts = [ops[-1] for ops in self.eng_ops.values() if ops]
        for e in ENGINES:
            if not self.eng_ops[e]:
                continue
            o = Op(e, lambda eng: eng.nop(), "compute", tag="barrier")
            o.deps = [d for d in lasts if d.engine != e]
            o.idx = len(self.eng_ops[e])
            self.eng_ops[e].append(o)
        barrier_ops = {e: self.eng_ops[e][-1] for e in ENGINES if self.eng_ops[e]}
        for k in list(self.last_write):
            lw = self.last_write[k]
            b = barrier_ops.get(lw.engine)
            if b is not None and b is not lw:
                pass
        for e, b in barrier_ops.items():
            pass
        self.last_write = {"__barrier__": None}
        self.last_write.clear()
        for e, b in barrier_ops.items():
            self.last_write[f"__bar_{e}__"] = b
        self.readers = {}
        self._barrier_ops = list(barrier_ops.values())
        self.phase_stack.close()
        self.phase_stack = None

    def _uname(self, name):
        if self.phase_idx:
            name = f"{name}__p{self.phase_idx}"
        assert name not in self.names, f"duplicate tensor name {name}"
        self.names.add(name)
        return name

    def _stk(self):
        return self.phase_stack if self.phase_stack is not None else self.stack

    def sbuf(self, name, shape, dtype):
        name = self._uname(name)
        return self._stk().enter_context(
            self.nc.sbuf_tensor(name, list(shape), dtype))

    def psum(self, name, shape, dtype=F32):
        name = self._uname(name)
        return self._stk().enter_context(
            self.nc.psum_tensor(name, list(shape), dtype))

    def dram(self, name, shape, dtype, kind=None):
        self._uname(name)
        if kind is None:
            return self.nc.dram_tensor(name, list(shape), dtype)
        return self.nc.dram_tensor(name, list(shape), dtype, kind=kind)

    def op(self, engine, emit, reads=(), writes=(), kind="compute", tag=""):
        o = Op(engine, emit, kind, tag)
        deps = {}
        for bk in ("__bar_sync__", "__bar_tensor__", "__bar_vector__",
                   "__bar_scalar__", "__bar_gpsimd__"):
            w = self.last_write.get(bk)
            if w is not None:
                deps[id(w)] = w
        for k in reads:
            w = self.last_write.get(k)
            if w is not None:
                deps[id(w)] = w
        for k in writes:
            w = self.last_write.get(k)
            if w is not None:
                deps[id(w)] = w
            for r in self.readers.get(k, ()):
                deps[id(r)] = r
        def _drop(d):
            if d is o:
                return True
            if SAME_ENGINE_FIFO and d.engine == engine and d.kind == "compute"                     and kind == "compute":
                return True
            if (d.engine == engine == "tensor" and d.kind == "compute"
                    and kind == "compute"):
                return True
            return False
        o.deps = [d for d in deps.values() if not _drop(d)]
        o.idx = len(self.eng_ops[engine])
        o.phase = self.phase_idx
        self.eng_ops[engine].append(o)
        for k in reads:
            self.readers.setdefault(k, []).append(o)
        for k in writes:
            self.last_write[k] = o
            self.readers[k] = []
        return o

    def dma(self, queue, out, in_, reads, writes, tag="", **kw):
        assert len(writes) >= 1
        o = self.op(queue, lambda e: e.dma_start(out=out, in_=in_, **kw),
                    reads=reads, writes=writes, kind="dma", tag=tag)
        o.buf = writes[0]
        return o

    def dma_t(self, queue, out, in_, reads, writes, tag=""):
        o = self.op(queue, lambda e: e.dma_start(out=out, in_=in_, transpose=True),
                    reads=reads, writes=writes, kind="dma", tag=tag)
        o.buf = writes[0]
        return o

    def cc(self, kind, op, groups, ins, outs, reads, writes, tag="cc"):
        def emit(e):
            return e.collective_compute(kind, op, replica_groups=groups,
                                        ins=ins, outs=outs)
        return self.op("gpsimd", emit, reads=reads, writes=writes, kind="cc", tag=tag)

    def finalize(self):
        assert not self._finalized
        self._finalized = True
        NPOOL = 64
        phase_slots = {}
        slot_of = {}
        all_dmas = []
        for e in ENGINES:
            for o in self.eng_ops[e]:
                if o.kind == "dma":
                    pk = (o.phase, o.buf)
                    if pk not in slot_of:
                        n = phase_slots.setdefault(o.phase, 0)
                        assert n < NPOOL, f"too many dma buffers in phase {o.phase}"
                        slot_of[pk] = (o.phase * 17 + n) % NPOOL
                        phase_slots[o.phase] = n + 1
                    all_dmas.append(o)
        dma_count = {}
        per_pk = {}
        for e in ENGINES:
            for o in self.eng_ops[e]:
                if o.kind == "dma":
                    per_pk.setdefault((o.phase, o.buf), []).append(o)
        slot_total_before = {}
        slot_totals = {}
        for pk in sorted(per_pk, key=lambda t: t[0]):
            s = slot_of[pk]
            base = slot_totals.get(s, 0)
            cnt = 0
            for o in per_pk[pk]:
                cnt += 16
                o.sem_key = ("ds", s)
                o.sem_val = base + cnt
                o.milestone = True
            slot_totals[s] = base + cnt
        dma_count = {("ds", s): v for s, v in slot_totals.items()}
        for e in ENGINES:
            for o in self.eng_ops[e]:
                if o.kind == "cc":
                    o.sem_key = ("cc",)
                    o.milestone = True
                elif o.kind == "compute":
                    o.sem_key = ("c", e)
        for e in ENGINES:
            for o in self.eng_ops[e]:
                for d in o.deps:
                    d.milestone = True
        for e in ENGINES:
            c_comp = 0
            c_cc = 0
            for o in self.eng_ops[e]:
                if o.kind == "compute":
                    if o.milestone:
                        c_comp += 1
                    o.sem_val = c_comp
                elif o.kind == "cc":
                    c_cc += 1
                    o.sem_val = c_cc

        sems = {}
        for e in ENGINES:
            sems[("c", e)] = self.stack.enter_context(self.nc.semaphore(f"sem_{e}_c"))
        sems[("cc",)] = self.stack.enter_context(self.nc.semaphore("sem_cc"))
        for sk in dma_count:
            sems[sk] = self.stack.enter_context(
                self.nc.semaphore("sem_ds_" + str(sk[1])))

        tail = {}
        for e in ENGINES:
            for o in self.eng_ops[e]:
                if o.kind == "dma":
                    tail[o.sem_key] = (e, dma_count[o.sem_key])
        tail_by_engine = {}
        for sk, (e, v) in tail.items():
            tail_by_engine.setdefault(e, []).append((sk, v))

        eng_ops = self.eng_ops

        def make_section(ename):
            ops = eng_ops[ename]
            tails = tail_by_engine.get(ename, [])

            def section(eng):
                waited = {}
                for o in ops:
                    tgt = {}
                    for d in o.deps:
                        sk = d.sem_key
                        if tgt.get(sk, -1) < d.sem_val:
                            tgt[sk] = d.sem_val
                    for sk, v in sorted(tgt.items(), key=str):
                        if waited.get(sk, -1) >= v:
                            continue
                        waited[sk] = v
                        eng.wait_ge(sems[sk], v)
                    ins = o.emit(eng)
                    if o.milestone:
                        ins.then_inc(sems[o.sem_key],
                                     16 if o.kind == "dma" else 1)
                for sk, v in tails:
                    if waited.get(sk, -1) < v:
                        eng.wait_ge(sems[sk], v)
            return section

        with self.nc.Block() as block:
            for ename in ENGINES:
                if not eng_ops[ename]:
                    continue
                getattr(block, ename)(make_section(ename))
        return self

    def run(self, in_maps, core_ids=None, **kw):
        res = run_bass_kernel_spmd(self.nc, in_maps,
                                   core_ids=core_ids or list(range(N_CORES)), **kw)
        self.stack.close()
        return res


class Ctx:

    def __init__(self, B: Builder):
        self.B = B
        self.nc = B.nc
        self.pool = {}
        self.uid = 0

    def nm(self, base):
        self.uid += 1
        return f"{base}_{self.uid}"

    def sb(self, key, shape, dtype):
        if key not in self.pool:
            self.pool[key] = self.B.sbuf(key, shape, dtype)
        t = self.pool[key]
        return t

    def ps(self, key, shape=(128, 512), dtype=F32):
        if key not in self.pool:
            self.pool[key] = self.B.psum(key, shape, dtype)
        return self.pool[key]

    def onescol_f32(self):
        new = "c_ones_col32" not in self.pool
        t = self.sb("c_ones_col32", [128, 1], F32)
        if new:
            self.B.op("vector", lambda e: e.memset(t[:, :], 1.0),
                      writes=["c_ones_col32"])
        return t

    def onesrow_f32(self):
        new = "c_ones_row32" not in self.pool
        t = self.sb("c_ones_row32", [1, 128], F32)
        if new:
            self.B.op("vector", lambda e: e.memset(t[:, :], 1.0),
                      writes=["c_ones_row32"])
        return t

    def ones_col(self):
        new = "c_ones_col" not in self.pool
        t = self.sb("c_ones_col", [128, 1], BF16)
        if new:
            self.B.op("vector", lambda e: e.memset(t[:, :], 1.0),
                      writes=["c_ones_col"])
        return t

    def ones_row(self):
        new = "c_ones_row" not in self.pool
        t = self.sb("c_ones_row", [1, 128], BF16)
        if new:
            self.B.op("vector", lambda e: e.memset(t[:, :], 1.0),
                      writes=["c_ones_row"])
        return t

    def constcol(self, value, parts=1, dtype=F32):
        key = f"c_col_{parts}_{dtype}_{value}".replace(".", "p").replace("-", "m")
        new = key not in self.pool
        t = self.sb(key, [parts, 1], dtype)
        if new:
            self.B.op("vector", lambda e: e.memset(t[:, :], value), writes=[key])
        return t

    def bcast_row(self, row_ap, row_key, dst, dst_key, n, ps_key="psG"):
        B = self.B
        ps = self.ps(ps_key)
        ones = self.ones_row()
        B.op("tensor", lambda e: e.matmul(ps[:, 0:n], ones[:, :], row_ap,
                                          start=True, stop=True),
             reads=[row_key, "c_ones_row"], writes=[ps_key])
        B.op("vector", lambda e: e.tensor_copy(out=dst, in_=ps[:, 0:n]),
             reads=[ps_key], writes=[dst_key])



def shard_tokens(x_btd):
    out = []
    for c in range(N_CORES):
        b = c // 4
        blks = core_blocks(c)
        out.append(np.ascontiguousarray(np.concatenate(
            [x_btd[b, blk * BLK:(blk + 1) * BLK] for blk in blks], axis=0)))
    return out


def unshard_tokens(shards):
    D = shards[0].shape[-1]
    x = np.empty((B_SZ, T_SEQ, D), dtype=np.asarray(shards[0]).dtype)
    for c in range(N_CORES):
        b = c // 4
        blks = core_blocks(c)
        for i, blk in enumerate(blks):
            x[b, blk * BLK:(blk + 1) * BLK] = shards[c][i * BLK:(i + 1) * BLK]
    return x


def sinusoidal_pe(t, d):
    pos = np.arange(t, dtype=np.float64)[:, None]
    i = np.arange(0, d, 2, dtype=np.float64)
    inv_freq = np.exp(-(np.log(10000.0) * i / d))
    ang = pos * inv_freq
    pe = np.zeros((t, d), dtype=np.float64)
    pe[:, 0::2] = np.sin(ang)
    pe[:, 1::2] = np.cos(ang)
    return pe.astype(np.float32)


def hadamard(n):
    h = np.array([[1.0]], dtype=np.float64)
    while h.shape[0] < n:
        h = np.block([[h, h], [h, -h]])
    return h.astype(np.float32)



def build_ln(ctx, x_sb, x_key, xn_sb, xn_key, tok, d_model=D_MODEL):
    B = ctx.B
    ntile = d_model // 128
    onec = ctx.onescol_f32()
    st_s = ctx.ps("psC")
    st_q = ctx.ps("psD")
    for i in range(ntile):
        B.op("tensor",
             lambda e, i=i: e.matmul(st_s[0:1, 0:tok], onec[:, :],
                                     x_sb[:, i * tok:(i + 1) * tok],
                                     start=(i == 0), stop=(i == ntile - 1)),
             reads=[x_key, "c_ones_col32"], writes=["psC"])
    sq = ctx.sb("ln_sq4", [128, 4 * 512], F32)
    qtr = ntile // 4
    for g in range(4):
        for i in range(qtr):
            B.op("scalar",
                 lambda e, g=g, i=i: e.activation(
                     out=sq[:, i * tok:(i + 1) * tok],
                     in_=x_sb[:, (g * qtr + i) * tok:(g * qtr + i + 1) * tok],
                     func=AF.Square),
                 reads=[x_key], writes=["ln_sq4"])
        for i in range(qtr):
            B.op("tensor",
                 lambda e, g=g, i=i: e.matmul(st_q[0:1, 0:tok], onec[:, :],
                                              sq[:, i * tok:(i + 1) * tok],
                                              start=(g == 0 and i == 0),
                                              stop=(g == 3 and i == qtr - 1)),
                 reads=["ln_sq4", "c_ones_col32"], writes=["psD"])
    mean = ctx.sb("ln_mean", [1, 512], F32)
    msq = ctx.sb("ln_msq", [1, 512], F32)
    var = ctx.sb("ln_var", [1, 512], F32)
    rstd = ctx.sb("ln_rstd", [1, 512], F32)
    nmr = ctx.sb("ln_nmr", [1, 512], F32)
    inv_d = 1.0 / d_model
    B.op("scalar", lambda e: e.activation(out=mean[:, 0:tok], in_=st_s[0:1, 0:tok],
                                          func=AF.Copy, scale=inv_d),
         reads=["psC"], writes=["ln_mean"])
    B.op("scalar", lambda e: e.activation(out=msq[:, 0:tok], in_=st_q[0:1, 0:tok],
                                          func=AF.Copy, scale=inv_d),
         reads=["psD"], writes=["ln_msq"])
    B.op("vector", lambda e: e.tensor_mul(out=var[:, 0:tok], in0=mean[:, 0:tok],
                                          in1=mean[:, 0:tok]),
         reads=["ln_mean"], writes=["ln_var"])
    B.op("vector", lambda e: e.tensor_sub(out=var[:, 0:tok], in0=msq[:, 0:tok],
                                          in1=var[:, 0:tok]),
         reads=["ln_msq", "ln_var"], writes=["ln_var"])
    epsc = ctx.constcol(1e-5)
    B.op("scalar", lambda e: e.activation(out=var[:, 0:tok], in_=var[:, 0:tok],
                                          func=AF.Ln, bias=epsc[:, :]),
         reads=["ln_var", epsc.name], writes=["ln_var"])
    B.op("scalar", lambda e: e.activation(out=rstd[:, 0:tok], in_=var[:, 0:tok],
                                          func=AF.Exp, scale=-0.5),
         reads=["ln_var"], writes=["ln_rstd"])
    B.op("vector", lambda e: e.tensor_mul(out=nmr[:, 0:tok], in0=mean[:, 0:tok],
                                          in1=rstd[:, 0:tok]),
         reads=["ln_mean", "ln_rstd"], writes=["ln_nmr"])
    B.op("scalar", lambda e: e.activation(out=nmr[:, 0:tok], in_=nmr[:, 0:tok],
                                          func=AF.Copy, scale=-1.0),
         reads=["ln_nmr"], writes=["ln_nmr"])
    rstd_b = ctx.sb("ln_rstd_b", [128, 512], F32)
    nmr_b = ctx.sb("ln_nmr_b", [128, 512], F32)
    onerow = ctx.onesrow_f32()
    bc1 = ctx.ps("psG")
    bc2 = ctx.ps("psH")
    B.op("tensor", lambda e: e.matmul(bc1[:, 0:tok], onerow[:, :], rstd[:, 0:tok],
                                      start=True, stop=True),
         reads=["ln_rstd", "c_ones_row32"], writes=["psG"])
    B.op("vector", lambda e: e.tensor_copy(out=rstd_b[:, 0:tok], in_=bc1[:, 0:tok]),
         reads=["psG"], writes=["ln_rstd_b"])
    B.op("tensor", lambda e: e.matmul(bc2[:, 0:tok], onerow[:, :], nmr[:, 0:tok],
                                      start=True, stop=True),
         reads=["ln_nmr", "c_ones_row32"], writes=["psH"])
    B.op("vector", lambda e: e.tensor_copy(out=nmr_b[:, 0:tok], in_=bc2[:, 0:tok]),
         reads=["psH"], writes=["ln_nmr_b"])
    t32 = ctx.sb("ln_t32", [128, 512], F32)
    for i in range(ntile):
        B.op("vector",
             lambda e, i=i: e.tensor_mul(out=t32[:, 0:tok],
                                         in0=x_sb[:, i * tok:(i + 1) * tok],
                                         in1=rstd_b[:, 0:tok]),
             reads=[x_key, "ln_rstd_b"], writes=["ln_t32"])
        B.op("vector",
             lambda e, i=i: e.tensor_add(out=xn_sb[:, i * tok:(i + 1) * tok],
                                         in0=t32[:, 0:tok],
                                         in1=nmr_b[:, 0:tok]),
             reads=["ln_t32", "ln_nmr_b"], writes=[xn_key])



BIG_WEIGHTS = {
    "w1T": (2048, 5632),
    "w3T": (2048, 5632),
    "w2T": (5632, 2048),
    "wupT": (2048, 8192),
    "wgateT": (2048, 8192),
    "wdownT": (8192, 2048),
    "b1T": (16, 8 * 5632),
    "b3T": (16, 8 * 5632),
    "a2T": (5632, 128),
}


def build_wprep(B: Builder, io=None):
    if io is None:
        io = {}
        for name, (din, dout) in BIG_WEIGHTS.items():
            io[name] = B.dram(name, [din, dout], F32, kind="ExternalInput")
            io[name + "_bf"] = B.dram(name + "_bf", [din, dout], BF16,
                                      kind="ExternalOutput")
    bounce = [B.sbuf("wp_bounce0", [128, 8192], BF16),
              B.sbuf("wp_bounce1", [128, 8192], BF16)]
    it = 0
    for name, (din, dout) in BIG_WEIGHTS.items():
        srcd = io[name].ap()
        dst = io[name + "_bf"].ap()
        p = min(din, 128)
        c = min(dout, 8192)
        for a in range((din + 127) // 128):
            pr = min(128, din - a * 128)
            for j0 in range(0, dout, c):
                w = min(c, dout - j0)
                bb = bounce[it % 2]
                kb = f"wp_bounce{it % 2}"
                it += 1
                B.dma("gpsimd", bb[0:pr, 0:w],
                      srcd[a * 128:a * 128 + pr, j0:j0 + w],
                      reads=[name], writes=[kb])
                B.dma("sync", dst[a * 128:a * 128 + pr, j0:j0 + w], bb[0:pr, 0:w],
                      reads=[kb], writes=[name + "_bf"])
    return B


def host_big_weights(inputs):
    t = lambda a: np.ascontiguousarray(np.asarray(a).T, dtype=np.float32)
    b1 = np.asarray(inputs["L0_ffn_b1"], np.float32)
    b3 = np.asarray(inputs["L0_ffn_b3"], np.float32)
    a2 = np.asarray(inputs["L0_ffn_a2"], np.float32)
    return {
        "w1T": t(inputs["L0_ffn_w1"]),
        "w3T": t(inputs["L0_ffn_w3"]),
        "w2T": t(inputs["L0_ffn_w2"]),
        "wupT": t(inputs["L1_ffn_W_up"]),
        "wgateT": t(inputs["L1_ffn_W_gate"]),
        "wdownT": t(inputs["L1_ffn_W_down"]),
        "b1T": np.ascontiguousarray(b1.transpose(2, 0, 1).reshape(16, 8 * 5632)),
        "b3T": np.ascontiguousarray(b3.transpose(2, 0, 1).reshape(16, 8 * 5632)),
        "a2T": np.ascontiguousarray(a2.transpose(2, 0, 1).reshape(5632, 128)),
    }


def col_layout(v, n=None):
    v = np.asarray(v, dtype=np.float32).reshape(-1)
    a = v.shape[0] // 128
    return np.ascontiguousarray(v.reshape(a, 128).T)





def butterfly_group8(ctx, buf, kbuf, tile_idx, tok):
    B = ctx.B
    s0 = ctx.sb("bfly_s0", [128, 8 * 512], F32)
    s1 = ctx.sb("bfly_s1", [128, 8 * 512], F32)
    for j, ti in enumerate(tile_idx):
        B.op("vector",
             lambda e, j=j, ti=ti: e.tensor_copy(
                 out=s0[:, j * tok:(j + 1) * tok],
                 in_=buf[:, ti * tok:(ti + 1) * tok]),
             reads=[kbuf], writes=["bfly_s0"])
    cur, ck, alt, ak = s0, "bfly_s0", s1, "bfly_s1"
    for h in (1, 2, 4):
        for base in range(0, 8, 2 * h):
            for o in range(h):
                j0, j1 = base + o, base + o + h
                B.op("vector",
                     lambda e, j0=j0, j1=j1, cur=cur, alt=alt: e.tensor_add(
                         out=alt[:, j0 * tok:(j0 + 1) * tok],
                         in0=cur[:, j0 * tok:(j0 + 1) * tok],
                         in1=cur[:, j1 * tok:(j1 + 1) * tok]),
                     reads=[ck], writes=[ak])
                B.op("vector",
                     lambda e, j0=j0, j1=j1, cur=cur, alt=alt: e.tensor_sub(
                         out=alt[:, j1 * tok:(j1 + 1) * tok],
                         in0=cur[:, j0 * tok:(j0 + 1) * tok],
                         in1=cur[:, j1 * tok:(j1 + 1) * tok]),
                     reads=[ck], writes=[ak])
        cur, ck, alt, ak = alt, ak, cur, ck
    for j, ti in enumerate(tile_idx):
        B.op("vector",
             lambda e, j=j, ti=ti, cur=cur: e.tensor_copy(
                 out=buf[:, ti * tok:(ti + 1) * tok],
                 in_=cur[:, j * tok:(j + 1) * tok]),
             reads=[ck], writes=[kbuf])


def butterfly_inplace(ctx, buf, kbuf, ntiles, strides, tok):
    rem = list(strides)
    while rem:
        take = rem[:3]
        rem = rem[3:]
        base_stride = take[0]
        span = take[-1] * 2
        s = base_stride
        nblk = 8 * s if len(take) == 3 else (4 * s if len(take) == 2 else 2 * s)
        per = nblk // s
        for blk0 in range(0, ntiles, nblk):
            for o in range(s):
                idx = [blk0 + o + j * s for j in range(per)]
                if per == 8:
                    butterfly_group8(ctx, buf, kbuf, idx, tok)
                else:
                    butterfly_small(ctx, buf, kbuf, idx, tok)


def butterfly_small(ctx, buf, kbuf, tile_idx, tok):
    B = ctx.B
    n = len(tile_idx)
    s0 = ctx.sb("bfly_s0", [128, 8 * 512], F32)
    s1 = ctx.sb("bfly_s1", [128, 8 * 512], F32)
    for j, ti in enumerate(tile_idx):
        B.op("vector",
             lambda e, j=j, ti=ti: e.tensor_copy(
                 out=s0[:, j * tok:(j + 1) * tok],
                 in_=buf[:, ti * tok:(ti + 1) * tok]),
             reads=[kbuf], writes=["bfly_s0"])
    cur, ck, alt, ak = s0, "bfly_s0", s1, "bfly_s1"
    h = 1
    while h < n:
        for base in range(0, n, 2 * h):
            for o in range(h):
                j0, j1 = base + o, base + o + h
                B.op("vector",
                     lambda e, j0=j0, j1=j1, cur=cur, alt=alt: e.tensor_add(
                         out=alt[:, j0 * tok:(j0 + 1) * tok],
                         in0=cur[:, j0 * tok:(j0 + 1) * tok],
                         in1=cur[:, j1 * tok:(j1 + 1) * tok]),
                     reads=[ck], writes=[ak])
                B.op("vector",
                     lambda e, j0=j0, j1=j1, cur=cur, alt=alt: e.tensor_sub(
                         out=alt[:, j1 * tok:(j1 + 1) * tok],
                         in0=cur[:, j0 * tok:(j0 + 1) * tok],
                         in1=cur[:, j1 * tok:(j1 + 1) * tok]),
                     reads=[ck], writes=[ak])
        cur, ck, alt, ak = alt, ak, cur, ck
        h *= 2
    for j, ti in enumerate(tile_idx):
        B.op("vector",
             lambda e, j=j, ti=ti, cur=cur: e.tensor_copy(
                 out=buf[:, ti * tok:(ti + 1) * tok],
                 in_=cur[:, j * tok:(j + 1) * tok]),
             reads=[ck], writes=[kbuf])




def build_f1_chunk(ctx: Ctx, chunk, tok=512):
    B = ctx.B
    io = ctx.io
    t0 = chunk * tok
    INTER = 8192
    x_sb = ctx.sb("f1_x", [128, 16 * tok], F32)
    kx = "f1_x"
    xv = io["x_in"].ap().rearrange("(a p) t -> p a t", p=128)
    B.dma("sync", x_sb[:].rearrange("p (a t) -> p a t", a=16),
          xv[:, :, t0:t0 + tok], reads=["x_in"], writes=[kx])
    xn = ctx.sb("f1_xn", [128, 16 * tok], BF16)
    kxn = "f1_xn"
    build_ln(ctx, x_sb, kx, xn, kxn, tok)
    for i in range(16):
        B.op("vector",
             lambda e, i=i: e.tensor_scalar(
                 out=xn[:, i * tok:(i + 1) * tok], in0=xn[:, i * tok:(i + 1) * tok],
                 scalar1=io["p_su_ug"][:, i:i + 1], scalar2=None, op0=ALU.mult),
             reads=[kxn, "p_su_ug"], writes=[kxn])
    psh = ctx.ps("psA")
    for i in range(16):
        B.op("tensor",
             lambda e, i=i: e.matmul(psh[:, 0:tok], ctx.pool["c_h128a"][:, :],
                                     xn[:, i * tok:(i + 1) * tok],
                                     start=True, stop=True),
             reads=[kxn, "c_h128a"], writes=["psA"])
        B.op("vector",
             lambda e, i=i: e.tensor_copy(out=xn[:, i * tok:(i + 1) * tok],
                                          in_=psh[:, 0:tok]),
             reads=["psA"], writes=[kxn])
    butterfly_inplace(ctx, xn, kxn, 16, (1, 2, 4, 8), tok)
    fw, kfw = xn, kxn
    hbuf = ctx.sb("f1_hbuf", [128, 64 * tok], BF16)
    khb = "f1_hbuf"
    upsl = [ctx.sb("f1_upsl0", [128, 16 * 512], BF16)]
    gtsl = [ctx.sb("f1_gtsl0", [128, 16 * 512], BF16)]
    wupv = io["wupT_bf"].ap().rearrange("(a p) n -> p a n", p=128)
    wgtv = io["wgateT_bf"].ap().rearrange("(a p) n -> p a n", p=128)
    psu = ctx.ps("psB")
    psg = ctx.ps("psE")
    tup = ctx.sb("f1_tup", [128, 512], F32)
    tgt = ctx.sb("f1_tgt", [128, 512], F32)
    for s in range(16):
        sl = 0
        ku, kg = "f1_upsl0", "f1_gtsl0"
        B.dma("sync", upsl[sl][:].rearrange("p (a n) -> p a n", a=16),
              wupv[:, :, s * 512:(s + 1) * 512], reads=["wupT_bf"], writes=[ku])
        B.dma("scalar", gtsl[sl][:].rearrange("p (a n) -> p a n", a=16),
              wgtv[:, :, s * 512:(s + 1) * 512], reads=["wgateT_bf"], writes=[kg])
        for dc in range(4):
            dcg = s * 4 + dc
            for a in range(16):
                B.op("tensor",
                     lambda e, a=a, dc=dc, sl=sl: e.matmul(
                         psu[:, 0:tok], upsl[sl][:, a * 512 + dc * 128:a * 512 + (dc + 1) * 128],
                         fw[:, a * tok:(a + 1) * tok],
                         start=(a == 0), stop=(a == 15)),
                     reads=[ku, kfw], writes=["psB"])
            for a in range(16):
                B.op("tensor",
                     lambda e, a=a, dc=dc, sl=sl: e.matmul(
                         psg[:, 0:tok], gtsl[sl][:, a * 512 + dc * 128:a * 512 + (dc + 1) * 128],
                         fw[:, a * tok:(a + 1) * tok],
                         start=(a == 0), stop=(a == 15)),
                     reads=[kg, kfw], writes=["psE"])
            B.op("scalar",
                 lambda e, dcg=dcg: e.activation(
                     out=tup[:, 0:tok], in_=psu[:, 0:tok], func=AF.Copy,
                     scale=io["p_up_eff"][:, dcg:dcg + 1]),
                 reads=["psB", "p_up_eff"], writes=["f1_tup"])
            B.op("scalar",
                 lambda e, dcg=dcg: e.activation(
                     out=tgt[:, 0:tok], in_=psg[:, 0:tok], func=AF.Silu,
                     scale=io["p_gate_eff"][:, dcg:dcg + 1]),
                 reads=["psE", "p_gate_eff"], writes=["f1_tgt"])
            B.op("vector",
                 lambda e, dcg=dcg: e.tensor_mul(
                     out=hbuf[:, dcg * tok:(dcg + 1) * tok],
                     in0=tgt[:, 0:tok], in1=tup[:, 0:tok]),
                 reads=["f1_tup", "f1_tgt"], writes=[khb])
    for i in range(64):
        B.op("tensor",
             lambda e, i=i: e.matmul(psh[:, 0:tok], ctx.pool["c_h128b"][:, :],
                                     hbuf[:, i * tok:(i + 1) * tok],
                                     start=True, stop=True),
             reads=[khb, "c_h128b"], writes=["psA"])
        B.op("vector",
             lambda e, i=i: e.tensor_copy(out=hbuf[:, i * tok:(i + 1) * tok],
                                          in_=psh[:, 0:tok]),
             reads=["psA"], writes=[khb])
    butterfly_inplace(ctx, hbuf, khb, 64, (1, 2, 4, 8, 16, 32), tok)
    fw8, kfw8 = hbuf, khb
    dsl = [ctx.sb("f1_upsl0", [128, 16 * 512], BF16)]
    wdnv = io["wdownT_bf"].ap().rearrange("(a p) n -> p a n", p=128)
    psd = ctx.ps("psF")
    tdn = ctx.sb("f1_tup", [128, 512], F32)
    xov = io["x_out"].ap().rearrange("(a p) t -> p a t", p=128)
    for s in range(16):
        sl = 0
        kd = "f1_upsl0"
        B.dma("sync", dsl[sl][:].rearrange("p (a n) -> p a n", a=64),
              wdnv[:, :, s * 128:(s + 1) * 128], reads=["wdownT_bf"], writes=[kd])
        for a in range(64):
            B.op("tensor",
                 lambda e, a=a, sl=sl: e.matmul(
                     psd[:, 0:tok], dsl[sl][:, a * 128:(a + 1) * 128],
                     fw8[:, a * tok:(a + 1) * tok],
                     start=(a == 0), stop=(a == 63)),
                 reads=[kd, kfw8], writes=["psF"])
        B.op("scalar",
             lambda e, s=s: e.activation(out=tdn[:, 0:tok], in_=psd[:, 0:tok],
                                         func=AF.Copy,
                                         scale=io["p_dp_eff"][:, s:s + 1]),
             reads=["psF", "p_dp_eff"], writes=["f1_tup"])
        B.op("vector",
             lambda e, s=s: e.tensor_add(out=x_sb[:, s * tok:(s + 1) * tok],
                                         in0=x_sb[:, s * tok:(s + 1) * tok],
                                         in1=tdn[:, 0:tok]),
             reads=["f1_tup"], writes=[kx])
        B.dma("scalar", xov[:, s, t0:t0 + tok], x_sb[:, s * tok:(s + 1) * tok],
              reads=[kx], writes=["x_out"])


def build_f1(B: Builder, io=None):
    ctx = Ctx(B)
    if io is None:
        io = {}
        io["x_in"] = B.dram("x_in", [D_MODEL, TOK_PER_CORE], F32,
                            kind="ExternalInput")
        io["x_out"] = B.dram("x_out", [D_MODEL, TOK_PER_CORE], F32,
                             kind="ExternalOutput")
        io["wupT_bf"] = B.dram("wupT_bf", [2048, 8192], BF16, kind="ExternalInput")
        io["wgateT_bf"] = B.dram("wgateT_bf", [2048, 8192], BF16,
                                 kind="ExternalInput")
        io["wdownT_bf"] = B.dram("wdownT_bf", [8192, 2048], BF16,
                                 kind="ExternalInput")
    ctx.io = io
    par = {}
    for nm, cols in (("p_su_ug", 16), ("p_su_dp", 64), ("p_wsu_up", 64),
                     ("p_wsu_gate", 64), ("p_wsdp", 16)):
        d = io.get(nm + "_d") or B.dram(nm + "_d", [128, cols], F32,
                                        kind="ExternalInput")
        t = ctx.sb(nm if nm in ("p_su_ug",) else nm + "_s", [128, cols], F32)
        B.dma("gpsimd", t[:, :], d.ap(), reads=[], writes=[t.name])
        par[nm] = t
        io[nm] = t
    for nm in ("c_h128a", "c_h128b"):
        d = io.get(nm + "_d") or B.dram(nm + "_d", [128, 128], F32,
                                        kind="ExternalInput")
        t = ctx.sb(nm, [128, 128], BF16)
        B.dma("gpsimd", t[:, :], d.ap(), reads=[], writes=[nm])
    up_eff = ctx.sb("p_up_eff", [128, 64], F32)
    B.op("vector", lambda e: e.tensor_mul(out=up_eff[:, :], in0=par["p_wsu_up"][:, :],
                                          in1=par["p_su_dp"][:, :]),
         reads=[par["p_wsu_up"].name, par["p_su_dp"].name], writes=["p_up_eff"])
    B.op("scalar", lambda e: e.activation(out=up_eff[:, :], in_=up_eff[:, :],
                                          func=AF.Copy, scale=64.0),
         reads=["p_up_eff"], writes=["p_up_eff"])
    gt_eff = ctx.sb("p_gate_eff", [128, 64], F32)
    B.op("scalar", lambda e: e.activation(out=gt_eff[:, :],
                                          in_=par["p_wsu_gate"][:, :],
                                          func=AF.Copy, scale=64.0),
         reads=[par["p_wsu_gate"].name], writes=["p_gate_eff"])
    dp_eff = ctx.sb("p_dp_eff", [128, 16], F32)
    B.op("scalar", lambda e: e.activation(out=dp_eff[:, :], in_=par["p_wsdp"][:, :],
                                          func=AF.Copy, scale=64.0),
         reads=[par["p_wsdp"].name], writes=["p_dp_eff"])
    io["p_up_eff"] = up_eff
    io["p_gate_eff"] = gt_eff
    io["p_dp_eff"] = dp_eff
    for chunk in range(2):
        build_f1_chunk(ctx, chunk, tok=512)
    return B


def host_f1_inputs(inputs, wbf, x_shard_fm):
    I = 8192
    ws = np.asarray(inputs["L1_ffn_Wscale_ug"], np.float32)
    return {
        "x_in": x_shard_fm,
        "wupT_bf": wbf["wupT_bf"],
        "wgateT_bf": wbf["wgateT_bf"],
        "wdownT_bf": wbf["wdownT_bf"],
        "p_su_ug_d": col_layout(inputs["L1_ffn_SU_ug"]),
        "p_su_dp_d": col_layout(inputs["L1_ffn_SU_dp"]),
        "p_wsu_up_d": col_layout(ws[:I]),
        "p_wsu_gate_d": col_layout(ws[I:]),
        "p_wsdp_d": col_layout(inputs["L1_ffn_Wscale_dp"]),
        "c_h128a_d": hadamard(128) * np.float32(1.0 / (math.sqrt(2048) * 64.0)),
        "c_h128b_d": hadamard(128) * np.float32(1.0 / (math.sqrt(8192) * 64.0)),
    }



F_DIM = 5632
N_EXP = 8
N_FSL = 11


def build_f0_chunk(ctx: Ctx, chunk, tok=256):
    B = ctx.B
    io = ctx.io
    t0 = chunk * tok
    nfs = 4
    x_sb = ctx.sb("f0_x", [128, 16 * 256], F32)
    xv = io["x_in"].ap().rearrange("(a p) t -> p a t", p=128)
    B.dma("sync", x_sb[:].rearrange("p (a t) -> p a t", a=16),
          xv[:, :, t0:t0 + tok], reads=["x_in"], writes=["f0_x"])
    xn = ctx.sb("f0_xn", [128, 16 * 256], BF16)
    build_ln(ctx, x_sb, "f0_x", xn, "f0_xn", tok)
    psl = ctx.ps("psF")
    for a in range(16):
        B.op("tensor",
             lambda e, a=a: e.matmul(psl[0:8, 0:tok], io["gateT"][:, a * 8:(a + 1) * 8],
                                     xn[:, a * tok:(a + 1) * tok],
                                     start=(a == 0), stop=(a == 15)),
             reads=["f0_xn", "gateT"], writes=["psF"])
    lt_f = ctx.sb("f0_lt", [8, 256], F32)
    B.op("vector", lambda e: e.tensor_copy(out=lt_f[:, 0:tok], in_=psl[0:8, 0:tok]),
         reads=["psF"], writes=["f0_lt"])
    wfull = ctx.sb("f0_wfull", [128, 2 * 8], F32)
    pst = ctx.ps("psG")
    for s in range(tok // 128):
        B.op("tensor",
             lambda e, s=s: e.transpose(pst[:, 0:8], lt_f[0:8, s * 128:(s + 1) * 128],
                                        io["ident32"][0:8, 0:8]),
             reads=["f0_lt", "ident32"], writes=["psG"])
        lt = ctx.sb(f"f0_ltT{s}", [128, 8], F32)
        B.op("vector", lambda e, s=s: e.tensor_copy(out=lt[:, :], in_=pst[:, 0:8]),
             reads=["psG"], writes=[f"f0_ltT{s}"])
        m1 = ctx.sb(f"f0_m1{s}", [128, 1], F32)
        m2 = ctx.sb(f"f0_m2{s}", [128, 1], F32)
        eq1 = ctx.sb(f"f0_eq1{s}", [128, 8], F32)
        eq2 = ctx.sb(f"f0_eq2{s}", [128, 8], F32)
        l2 = ctx.sb(f"f0_l2{s}", [128, 8], F32)
        B.op("vector", lambda e, s=s: e.reduce_max(out=m1[:, :], in_=lt[:, :],
                                                   axis=AX.X),
             reads=[f"f0_ltT{s}"], writes=[f"f0_m1{s}"])
        B.op("vector", lambda e, s=s: e.tensor_scalar(out=eq1[:, :], in0=lt[:, :],
                                                      scalar1=m1[:, :], scalar2=None,
                                                      op0=ALU.is_ge),
             reads=[f"f0_ltT{s}", f"f0_m1{s}"], writes=[f"f0_eq1{s}"])
        B.op("vector", lambda e, s=s: e.scalar_tensor_tensor(
                 out=l2[:, :], in0=eq1[:, :], scalar=-1e30, in1=lt[:, :],
                 op0=ALU.mult, op1=ALU.add),
             reads=[f"f0_eq1{s}", f"f0_ltT{s}"], writes=[f"f0_l2{s}"])
        B.op("vector", lambda e, s=s: e.reduce_max(out=m2[:, :], in_=l2[:, :],
                                                   axis=AX.X),
             reads=[f"f0_l2{s}"], writes=[f"f0_m2{s}"])
        B.op("vector", lambda e, s=s: e.tensor_scalar(out=eq2[:, :], in0=l2[:, :],
                                                      scalar1=m2[:, :], scalar2=None,
                                                      op0=ALU.is_ge),
             reads=[f"f0_l2{s}", f"f0_m2{s}"], writes=[f"f0_eq2{s}"])
        dcol = ctx.sb(f"f0_d{s}", [128, 1], F32)
        B.op("vector", lambda e, s=s: e.tensor_sub(out=dcol[:, :], in0=m2[:, :],
                                                   in1=m1[:, :]),
             reads=[f"f0_m2{s}", f"f0_m1{s}"], writes=[f"f0_d{s}"])
        ec = ctx.sb(f"f0_e{s}", [128, 1], F32)
        onec1 = ctx.constcol(1.0, 128)
        B.op("scalar", lambda e, s=s: e.activation(out=ec[:, :], in_=dcol[:, :],
                                                   func=AF.Exp),
             reads=[f"f0_d{s}"], writes=[f"f0_e{s}"])
        w1c = ctx.sb(f"f0_w1{s}", [128, 1], F32)
        B.op("vector", lambda e, s=s: e.tensor_scalar(out=w1c[:, :], in0=ec[:, :],
                                                      scalar1=1.0, scalar2=None,
                                                      op0=ALU.add),
             reads=[f"f0_e{s}"], writes=[f"f0_w1{s}"])
        B.op("vector", lambda e, s=s: e.reciprocal(out=w1c[:, :], in_=w1c[:, :]),
             reads=[f"f0_w1{s}"], writes=[f"f0_w1{s}"])
        w2c = ctx.sb(f"f0_w2{s}", [128, 1], F32)
        B.op("vector", lambda e, s=s: e.tensor_mul(out=w2c[:, :], in0=ec[:, :],
                                                   in1=w1c[:, :]),
             reads=[f"f0_e{s}", f"f0_w1{s}"], writes=[f"f0_w2{s}"])
        B.op("vector", lambda e, s=s: e.tensor_scalar(
                 out=eq1[:, :], in0=eq1[:, :], scalar1=w1c[:, :], scalar2=None,
                 op0=ALU.mult),
             reads=[f"f0_eq1{s}", f"f0_w1{s}"], writes=[f"f0_eq1{s}"])
        B.op("vector", lambda e, s=s: e.tensor_scalar(
                 out=wfull[:, s * 8:(s + 1) * 8], in0=eq2[:, :],
                 scalar1=w2c[:, :], scalar2=None, op0=ALU.mult),
             reads=[f"f0_eq2{s}", f"f0_w2{s}"], writes=["f0_wfull"])
        B.op("vector", lambda e, s=s: e.tensor_add(
                 out=wfull[:, s * 8:(s + 1) * 8],
                 in0=wfull[:, s * 8:(s + 1) * 8], in1=eq1[:, :]),
             reads=[f"f0_eq1{s}"], writes=["f0_wfull"])
        B.op("tensor",
             lambda e, s=s: e.transpose(pst[0:8, 0:128],
                                        wfull[:, s * 8:(s + 1) * 8],
                                        io["ident32"][:, :]),
             reads=["f0_wfull", "ident32"], writes=["psG"])
        B.op("vector", lambda e, s=s: e.tensor_copy(out=lt_f[:, s * 128:(s + 1) * 128],
                                                    in_=pst[0:8, 0:128]),
             reads=["psG"], writes=["f0_lt"])
    webc = ctx.sb("f0_webc", [128, 8 * 256], F32)
    psb = ctx.ps("psH")
    for ee in range(N_EXP):
        B.op("tensor",
             lambda e, ee=ee: e.matmul(psb[:, 0:tok],
                                       io["sel32"][:, ee * 128:(ee + 1) * 128],
                                       lt_f[:, 0:tok], start=True, stop=True),
             reads=["f0_lt", "sel32"], writes=["psH"])
        B.op("vector",
             lambda e, ee=ee: e.tensor_copy(out=webc[:, ee * tok:(ee + 1) * tok],
                                            in_=psb[:, 0:tok]),
             reads=["psH"], writes=["f0_webc"])
    u1b = ctx.sb("f0_u1", [16, 8 * 256], BF16)
    u3b = ctx.sb("f0_u3", [16, 8 * 256], BF16)
    psu = ctx.ps("psC")
    psv = ctx.ps("psD")
    for ee in range(N_EXP):
        for a in range(16):
            B.op("tensor",
                 lambda e, ee=ee, a=a: e.matmul(
                     psu[0:16, 0:tok], io["a1l"][:, a * 128 + ee * 16:a * 128 + ee * 16 + 16],
                     xn[:, a * tok:(a + 1) * tok], start=(a == 0), stop=(a == 15)),
                 reads=["f0_xn", "a1l"], writes=["psC"])
        B.op("scalar",
             lambda e, ee=ee: e.activation(out=u1b[:, ee * tok:(ee + 1) * tok],
                                           in_=psu[0:16, 0:tok], func=AF.Copy,
                                           scale=2.0),
             reads=["psC"], writes=["f0_u1"])
        for a in range(16):
            B.op("tensor",
                 lambda e, ee=ee, a=a: e.matmul(
                     psv[0:16, 0:tok], io["a3l"][:, a * 128 + ee * 16:a * 128 + ee * 16 + 16],
                     xn[:, a * tok:(a + 1) * tok], start=(a == 0), stop=(a == 15)),
                 reads=["f0_xn", "a3l"], writes=["psD"])
        B.op("scalar",
             lambda e, ee=ee: e.activation(out=u3b[:, ee * tok:(ee + 1) * tok],
                                           in_=psv[0:16, 0:tok], func=AF.Copy,
                                           scale=2.0),
             reads=["psD"], writes=["f0_u3"])
    z_sb = ctx.sb("f0_z", [16, 8 * 256], F32)
    w1sl = ctx.sb("f0_w1sl", [128, 16 * 512], BF16)
    w3sl = ctx.sb("f0_w3sl", [128, 16 * 512], BF16)
    w2sl = ctx.sb("f0_w2sl", [128, 4 * 2048], BF16)
    b1sl = ctx.sb("f0_b1sl", [16, 8 * 512], BF16)
    b3sl = ctx.sb("f0_b3sl", [16, 8 * 512], BF16)
    a2sl = ctx.sb("f0_a2sl", [128, 4 * 128], BF16)
    base1 = ctx.sb("f0_base1", [128, 4 * 256], BF16)
    base3 = ctx.sb("f0_base3", [128, 4 * 256], BF16)
    acts = ctx.sb("f0_acts", [128, 4 * 256], F32)
    th1 = ctx.sb("f0_th1", [128, 256], F32)
    ts_ = ctx.sb("f0_ts", [128, 256], F32)
    th3 = ctx.sb("f0_th3", [128, 256], F32)
    tact = ctx.sb("f0_tact", [128, 256], F32)
    wact = ctx.sb("f0_wact", [128, 256], BF16)
    w1v = io["w1T_bf"].ap().rearrange("(a p) n -> p a n", p=128)
    w3v = io["w3T_bf"].ap().rearrange("(a p) n -> p a n", p=128)
    w2v = io["w2T_bf"].ap().rearrange("(a p) n -> p a n", p=128)
    b1v = io["b1T_bf"].ap().rearrange("r (ee f) -> r ee f", ee=8)
    b3v = io["b3T_bf"].ap().rearrange("r (ee f) -> r ee f", ee=8)
    a2v = io["a2T_bf"].ap().rearrange("(a p) c -> p a c", p=128)
    pb1 = ctx.ps("psA")
    pb3 = ctx.ps("psB")
    pz = ctx.ps("psE")
    for sl in range(N_FSL):
        f0 = sl * 512
        B.dma("sync", w1sl[:].rearrange("p (a n) -> p a n", a=16),
              w1v[:, :, f0:f0 + 512], reads=["w1T_bf"], writes=["f0_w1sl"])
        B.dma("scalar", w3sl[:].rearrange("p (a n) -> p a n", a=16),
              w3v[:, :, f0:f0 + 512], reads=["w3T_bf"], writes=["f0_w3sl"])
        B.dma("sync", w2sl[:].rearrange("p (a n) -> p a n", a=4),
              w2v[:, 4 * sl:4 * sl + 4, :], reads=["w2T_bf"], writes=["f0_w2sl"])
        B.dma("gpsimd", b1sl[:].rearrange("r (ee f) -> r ee f", ee=8),
              b1v[:, :, f0:f0 + 512], reads=["b1T_bf"], writes=["f0_b1sl"])
        B.dma("gpsimd", b3sl[:].rearrange("r (ee f) -> r ee f", ee=8),
              b3v[:, :, f0:f0 + 512], reads=["b3T_bf"], writes=["f0_b3sl"])
        B.dma("gpsimd", a2sl[:].rearrange("p (a c) -> p a c", a=4),
              a2v[:, 4 * sl:4 * sl + 4, :], reads=["a2T_bf"], writes=["f0_a2sl"])
        for j in range(nfs):
            for a in range(16):
                B.op("tensor",
                     lambda e, j=j, a=a: e.matmul(
                         pb1[:, 0:tok], w1sl[:, a * 512 + j * 128:a * 512 + (j + 1) * 128],
                         xn[:, a * tok:(a + 1) * tok], start=(a == 0), stop=(a == 15)),
                     reads=["f0_w1sl", "f0_xn"], writes=["psA"])
            B.op("scalar",
                 lambda e, j=j: e.activation(out=base1[:, j * tok:(j + 1) * tok],
                                             in_=pb1[:, 0:tok], func=AF.Copy),
                 reads=["psA"], writes=["f0_base1"])
            for a in range(16):
                B.op("tensor",
                     lambda e, j=j, a=a: e.matmul(
                         pb3[:, 0:tok], w3sl[:, a * 512 + j * 128:a * 512 + (j + 1) * 128],
                         xn[:, a * tok:(a + 1) * tok], start=(a == 0), stop=(a == 15)),
                     reads=["f0_w3sl", "f0_xn"], writes=["psB"])
            B.op("scalar",
                 lambda e, j=j: e.activation(out=base3[:, j * tok:(j + 1) * tok],
                                             in_=pb3[:, 0:tok], func=AF.Copy),
                 reads=["psB"], writes=["f0_base3"])
        for ee in range(N_EXP):
            for j in range(nfs):
                B.op("tensor",
                     lambda e, ee=ee, j=j: e.matmul(
                         pb1[:, 0:tok],
                         b1sl[:, ee * 512 + j * 128:ee * 512 + (j + 1) * 128],
                         u1b[:, ee * tok:(ee + 1) * tok], start=True, stop=True),
                     reads=["f0_b1sl", "f0_u1", "f0_base1"], writes=["psA"])
                B.op("tensor",
                     lambda e, ee=ee, j=j: e.matmul(
                         pb3[:, 0:tok],
                         b3sl[:, ee * 512 + j * 128:ee * 512 + (j + 1) * 128],
                         u3b[:, ee * tok:(ee + 1) * tok], start=True, stop=True),
                     reads=["f0_b3sl", "f0_u3", "f0_base3"], writes=["psB"])
                B.op("vector",
                     lambda e, j=j: e.tensor_add(out=th1[:, 0:tok], in0=pb1[:, 0:tok],
                                                 in1=base1[:, j * tok:(j + 1) * tok]),
                     reads=["psA", "f0_base1"], writes=["f0_th1"])
                B.op("scalar",
                     lambda e: e.activation(out=ts_[:, 0:tok], in_=th1[:, 0:tok],
                                            func=AF.Silu),
                     reads=["f0_th1"], writes=["f0_ts"])
                B.op("vector",
                     lambda e, j=j: e.tensor_add(out=th3[:, 0:tok], in0=pb3[:, 0:tok],
                                                 in1=base3[:, j * tok:(j + 1) * tok]),
                     reads=["psB", "f0_base3"], writes=["f0_th3"])
                B.op("vector",
                     lambda e: e.tensor_mul(out=tact[:, 0:tok], in0=ts_[:, 0:tok],
                                            in1=th3[:, 0:tok]),
                     reads=["f0_ts", "f0_th3"], writes=["f0_tact"])
                B.op("vector",
                     lambda e, ee=ee: e.tensor_mul(out=wact[:, 0:tok],
                                                   in0=tact[:, 0:tok],
                                                   in1=webc[:, ee * tok:(ee + 1) * tok]),
                     reads=["f0_tact", "f0_webc"], writes=["f0_wact"])
                if ee == 0:
                    B.op("vector",
                         lambda e, j=j: e.tensor_copy(out=acts[:, j * tok:(j + 1) * tok],
                                                      in_=wact[:, 0:tok]),
                         reads=["f0_wact"], writes=["f0_acts"])
                else:
                    B.op("vector",
                         lambda e, j=j: e.tensor_add(out=acts[:, j * tok:(j + 1) * tok],
                                                     in0=acts[:, j * tok:(j + 1) * tok],
                                                     in1=wact[:, 0:tok]),
                         reads=["f0_wact"], writes=["f0_acts"])
                B.op("tensor",
                     lambda e, ee=ee, j=j: e.matmul(
                         pz[0:16, 0:tok], a2sl[:, j * 128 + ee * 16:j * 128 + ee * 16 + 16],
                         wact[:, 0:tok], start=(j == 0), stop=(j == nfs - 1)),
                     reads=["f0_a2sl", "f0_wact"], writes=["psE"])
            if sl == 0:
                B.op("vector",
                     lambda e, ee=ee: e.tensor_copy(out=z_sb[:, ee * tok:(ee + 1) * tok],
                                                    in_=pz[0:16, 0:tok]),
                     reads=["psE"], writes=["f0_z"])
            else:
                B.op("vector",
                     lambda e, ee=ee: e.tensor_add(out=z_sb[:, ee * tok:(ee + 1) * tok],
                                                   in0=z_sb[:, ee * tok:(ee + 1) * tok],
                                                   in1=pz[0:16, 0:tok]),
                     reads=["psE"], writes=["f0_z"])
        actb = ctx.sb("f0_actb", [128, 4 * 256], BF16)
        for j in range(nfs):
            B.op("vector",
                 lambda e, j=j: e.tensor_copy(out=actb[:, j * tok:(j + 1) * tok],
                                              in_=acts[:, j * tok:(j + 1) * tok]),
                 reads=["f0_acts"], writes=["f0_actb"])
        for d in range(16):
            B.op("tensor",
                 lambda e, d=d, sl=sl: _mm_group(
                     e, pb1[:, 0:tok],
                     [(w2sl[:, j * 2048 + d * 128:j * 2048 + (d + 1) * 128],
                       actb[:, j * tok:(j + 1) * tok]) for j in range(nfs)]),
                 reads=["f0_w2sl", "f0_actb"], writes=["psA"])
            B.op("vector",
                 lambda e, d=d: e.tensor_add(out=x_sb[:, d * tok:(d + 1) * tok],
                                             in0=x_sb[:, d * tok:(d + 1) * tok],
                                             in1=pb1[:, 0:tok]),
                 reads=["psA"], writes=["f0_x"])
    zb = ctx.sb("f0_zb", [16, 8 * 256], BF16)
    for ee in range(N_EXP):
        B.op("scalar",
             lambda e, ee=ee: e.activation(out=zb[:, ee * tok:(ee + 1) * tok],
                                           in_=z_sb[:, ee * tok:(ee + 1) * tok],
                                           func=AF.Copy, scale=2.0),
             reads=["f0_z"], writes=["f0_zb"])
    xov = io["x_out"].ap().rearrange("(a p) t -> p a t", p=128)
    for d in range(16):
        for ee in range(N_EXP):
            B.op("tensor",
                 lambda e, d=d, ee=ee: e.matmul(
                     pb3[:, 0:tok],
                     io["b2l"][:, ee * 2048 + d * 128:ee * 2048 + (d + 1) * 128],
                     zb[:, ee * tok:(ee + 1) * tok], start=(ee == 0),
                     stop=(ee == N_EXP - 1)),
                 reads=["f0_zb", "b2l"], writes=["psB"])
        B.op("vector",
             lambda e, d=d: e.tensor_add(out=x_sb[:, d * tok:(d + 1) * tok],
                                         in0=x_sb[:, d * tok:(d + 1) * tok],
                                         in1=pb3[:, 0:tok]),
             reads=["psB"], writes=["f0_x"])
        B.dma("scalar", xov[:, d, t0:t0 + tok], x_sb[:, d * tok:(d + 1) * tok],
              reads=["f0_x"], writes=["x_out"])


def _mm_group(e, out_ap, pairs):
    ins = None
    n = len(pairs)
    for i, (lhsT, rhs) in enumerate(pairs):
        ins = e.matmul(out_ap, lhsT, rhs, start=(i == 0), stop=(i == n - 1))
    return ins


def build_f0(B: Builder, io=None):
    ctx = Ctx(B)
    if io is None:
        io = {}
        io["x_in"] = B.dram("x_in", [D_MODEL, TOK_PER_CORE], F32,
                            kind="ExternalInput")
        io["x_out"] = B.dram("x_out", [D_MODEL, TOK_PER_CORE], F32,
                             kind="ExternalOutput")
        for nm, shp in (("w1T_bf", (2048, 5632)), ("w3T_bf", (2048, 5632)),
                        ("w2T_bf", (5632, 2048)), ("b1T_bf", (16, 8 * 5632)),
                        ("b3T_bf", (16, 8 * 5632)), ("a2T_bf", (5632, 128))):
            io[nm] = B.dram(nm, list(shp), BF16, kind="ExternalInput")
    ctx.io = io
    gate_d = io.get("gateT_d") or B.dram("gateT_d", [2048, 8], F32,
                                          kind="ExternalInput")
    gate_s = ctx.sb("gateT", [128, 16 * 8], BF16)
    B.dma("gpsimd", gate_s[:].rearrange("p (a g) -> p a g", a=16),
          gate_d.ap().rearrange("(a p) g -> p a g", p=128), reads=[],
          writes=["gateT"])
    io["gateT"] = gate_s
    for nm, cols in (("a1l", 16 * 128), ("a3l", 16 * 128)):
        d = io.get(nm + "_d") or B.dram(nm + "_d", [2048, 128], F32,
                                        kind="ExternalInput")
        t = ctx.sb(nm, [128, cols], BF16)
        B.dma("gpsimd", t[:].rearrange("p (a c) -> p a c", a=16),
              d.ap().rearrange("(a p) c -> p a c", p=128), reads=[], writes=[nm])
        io[nm] = t
    b2_d = io.get("b2l_d") or B.dram("b2l_d", [16, 8 * 2048], F32,
                                      kind="ExternalInput")
    b2_s = ctx.sb("b2l", [16, 8 * 2048], BF16)
    B.dma("gpsimd", b2_s[:, :], b2_d.ap(), reads=[], writes=["b2l"])
    io["b2l"] = b2_s
    id_d = io.get("ident32_d") or B.dram("ident32_d", [128, 128], F32,
                                          kind="ExternalInput")
    id_s = ctx.sb("ident32", [128, 128], F32)
    B.dma("sync", id_s[:, :], id_d.ap(), reads=[], writes=["ident32"])
    io["ident32"] = id_s
    sel_d = io.get("sel32_d") or B.dram("sel32_d", [8, 8 * 128], F32,
                                         kind="ExternalInput")
    sel_s = ctx.sb("sel32", [8, 8 * 128], F32)
    B.dma("sync", sel_s[:, :], sel_d.ap(), reads=[], writes=["sel32"])
    io["sel32"] = sel_s
    for chunk in range(4):
        build_f0_chunk(ctx, chunk, tok=256)
    return B


def host_f0_inputs(inputs, wbf, x_shard_fm):
    a1 = np.asarray(inputs["L0_ffn_a1"], np.float32)
    a3 = np.asarray(inputs["L0_ffn_a3"], np.float32)
    b2 = np.asarray(inputs["L0_ffn_b2"], np.float32)
    sel = np.zeros((8, 8 * 128), np.float32)
    for ee in range(8):
        sel[ee, ee * 128:(ee + 1) * 128] = 1.0
    return {
        "x_in": x_shard_fm,
        "w1T_bf": wbf["w1T_bf"], "w3T_bf": wbf["w3T_bf"], "w2T_bf": wbf["w2T_bf"],
        "b1T_bf": wbf["b1T_bf"], "b3T_bf": wbf["b3T_bf"], "a2T_bf": wbf["a2T_bf"],
        "gateT_d": np.ascontiguousarray(np.asarray(inputs["L0_ffn_gate_w"],
                                                   np.float32).T),
        "a1l_d": np.ascontiguousarray(a1.transpose(2, 0, 1).reshape(2048, 128)),
        "a3l_d": np.ascontiguousarray(a3.transpose(2, 0, 1).reshape(2048, 128)),
        "b2l_d": np.ascontiguousarray(b2.transpose(2, 0, 1).reshape(16, 8 * 2048)),
        "ident32_d": np.eye(128, dtype=np.float32),
        "sel32_d": sel,
    }



def build_a2a(B: Builder, io=None):
    ctx = Ctx(B)
    if io is None:
        io = {}
        io["x_in"] = B.dram("x_in", [D_MODEL, TOK_PER_CORE], F32,
                            kind="ExternalInput")
        io["wqT"] = B.dram("wqT", [2048, 2048], F32, kind="ExternalInput")
        io["wkvT"] = B.dram("wkvT", [2048, 4096], F32, kind="ExternalInput")
        io["q_fm"] = B.dram("q_fm", [2048, TOK_PER_CORE], BF16,
                            kind="ExternalOutput")
        io["vkk"] = B.dram("vkk", [32, 64 * 33], F32, kind="ExternalOutput")
    q_out = io["q_fm"]
    vkk_out = io["vkk"]
    ctx.io = io
    wq_v = io["wqT"].ap().rearrange("(a p) n -> p a n", p=128)
    wkv_v = io["wkvT"].ap().rearrange("(a p) n -> p a n", p=128)
    qv = q_out.ap().rearrange("(a p) t -> p a t", p=128)
    wsl = ctx.sb("a2_wsl", [128, 16 * 512], BF16)
    ktok = ctx.sb("a2_ktok", [128, 8 * 2048], BF16)
    vtok = ctx.sb("a2_vtok", [128, 8 * 2048], BF16)
    pq = ctx.ps("psA")
    tq = ctx.sb("a2_tq", [128, 512], BF16)
    x_sb = ctx.sb("a2_x", [128, 16 * 512], F32)
    xn_tmp = ctx.sb("a2_xntmp", [128, 16 * 512], BF16)
    for chunk in range(2):
        t0 = chunk * 512
        xv = io["x_in"].ap().rearrange("(a p) t -> p a t", p=128)
        B.dma("sync", x_sb[:].rearrange("p (a t) -> p a t", a=16),
              xv[:, :, t0:t0 + 512], reads=["x_in"], writes=["a2_x"])
        build_ln(ctx, x_sb, "a2_x", xn_tmp, "a2_xntmp", 512)
        for s in range(4):
            B.dma("gpsimd", wsl[:].rearrange("p (a n) -> p a n", a=16),
                  wq_v[:, :, s * 512:(s + 1) * 512], reads=["wqT"], writes=["a2_wsl"])
            for dc in range(4):
                for a in range(16):
                    B.op("tensor",
                         lambda e, s=s, dc=dc, a=a: e.matmul(
                             pq[:, 0:512],
                             wsl[:, a * 512 + dc * 128:a * 512 + (dc + 1) * 128],
                             xn_tmp[:, a * 512:(a + 1) * 512],
                             start=(a == 0), stop=(a == 15)),
                         reads=["a2_wsl", "a2_xntmp"], writes=["psA"])
                B.op("scalar",
                     lambda e: e.activation(out=tq[:, 0:512], in_=pq[:, 0:512],
                                            func=AF.Relu),
                     reads=["psA"], writes=["a2_tq"])
                B.dma("scalar", qv[:, s * 4 + dc, t0:t0 + 512], tq[:, 0:512],
                      reads=["a2_tq"], writes=["q_fm"])
        for s in range(8):
            B.dma("gpsimd", wsl[:].rearrange("p (a n) -> p a n", a=16),
                  wkv_v[:, :, s * 512:(s + 1) * 512], reads=["wkvT"],
                  writes=["a2_wsl"])
            for ts in range(4):
                gsub = chunk * 4 + ts
                for a in range(16):
                    B.op("tensor",
                         lambda e, s=s, ts=ts, a=a: e.matmul(
                             pq[:, 0:512],
                             xn_tmp[:, a * 512 + ts * 128:a * 512 + ts * 128 + 128],
                             wsl[:, a * 512:(a + 1) * 512],
                             start=(a == 0), stop=(a == 15)),
                         reads=["a2_wsl", "a2_xntmp"], writes=["psA"])
                for half in range(4):
                    feat0 = s * 512 + half * 128
                    is_k = feat0 < 2048
                    col = gsub * 2048 + (feat0 % 2048)
                    dst, kdst = (ktok, "a2_ktok") if is_k else (vtok, "a2_vtok")
                    fn = AF.Relu if is_k else AF.Copy
                    B.op("scalar",
                         lambda e, half=half, col=col, dst=dst, fn=fn: e.activation(
                             out=dst[:, col:col + 128],
                             in_=pq[:, half * 128:(half + 1) * 128], func=fn),
                         reads=["psA"], writes=[kdst])
    vkk_sb = ctx.sb("a2_vkk", [32, 64 * 33], F32)
    pvk = ctx.ps("psB")
    onec = ctx.ones_col()
    for h in range(64):
        b = h // 4
        off = (h % 4) * 32
        for sub in range(8):
            B.op("tensor",
                 lambda e, b=b, off=off, sub=sub: e.matmul(
                     pvk[0:32, 0:32],
                     ktok[:, sub * 2048 + b * 128 + off:sub * 2048 + b * 128 + off + 32],
                     vtok[:, sub * 2048 + b * 128 + off:sub * 2048 + b * 128 + off + 32],
                     start=(sub == 0), stop=(sub == 7)),
                 reads=["a2_ktok", "a2_vtok"], writes=["psB"])
        for sub in range(8):
            B.op("tensor",
                 lambda e, b=b, off=off, sub=sub: e.matmul(
                     pvk[0:32, 32:33],
                     ktok[:, sub * 2048 + b * 128 + off:sub * 2048 + b * 128 + off + 32],
                     onec[:, :], start=(sub == 0), stop=(sub == 7)),
                 reads=["a2_ktok", "c_ones_col"], writes=["psB"])
        B.op("vector",
             lambda e, h=h: e.tensor_copy(out=vkk_sb[:, h * 33:(h + 1) * 33],
                                          in_=pvk[0:32, 0:33]),
             reads=["psB"], writes=["a2_vkk"])
    B.dma("scalar", vkk_out.ap(), vkk_sb[:, :], reads=["a2_vkk"], writes=["vkk"])
    return B


def build_a2b(B: Builder, io=None):
    ctx = Ctx(B)
    if io is None:
        io = {}
        io["x_in"] = B.dram("x_in", [D_MODEL, TOK_PER_CORE], F32,
                            kind="ExternalInput")
        io["x_out"] = B.dram("x_out", [D_MODEL, TOK_PER_CORE], F32,
                             kind="ExternalOutput")
        io["q_fm"] = B.dram("q_fm", [2048, TOK_PER_CORE], BF16,
                            kind="ExternalInput")
        io["wprojT"] = B.dram("wprojT", [2048, 2048], F32, kind="ExternalInput")
        io["bproj"] = B.dram("bproj", [128, 16], F32, kind="ExternalInput")
        for i in range(4):
            io[f"vkk{i}"] = B.dram(f"vkk{i}", [32, 64 * 33], F32,
                                   kind="ExternalInput").ap()
    ctx.io = io
    vkk = ctx.sb("b_vkk", [32, 64 * 33], F32)
    tmp = ctx.sb("b_vkktmp", [32, 64 * 33], F32)
    B.dma("sync", vkk[:, :], io["vkk0"], reads=["d_vkkag"], writes=["b_vkk"])
    for i in range(1, 4):
        B.dma("sync", tmp[:, :], io[f"vkk{i}"], reads=["d_vkkag"],
              writes=["b_vkktmp"])
        B.op("vector", lambda e: e.tensor_add(out=vkk[:, :], in0=vkk[:, :],
                                              in1=tmp[:, :]),
             reads=["b_vkktmp"], writes=["b_vkk"])
    vkb = ctx.sb("b_vkb", [32, 64 * 33], BF16)
    B.op("vector", lambda e: e.tensor_copy(out=vkb[:, :], in_=vkk[:, :]),
         reads=["b_vkk"], writes=["b_vkb"])
    bpr = ctx.sb("b_bpr", [128, 16], F32)
    B.dma("gpsimd", bpr[:, :], io["bproj"].ap(), reads=[], writes=["b_bpr"])
    qsb = ctx.sb("b_q", [128, 16 * 1024], BF16)
    B.dma("sync", qsb[:].rearrange("p (a t) -> p a t", a=16),
          io["q_fm"].ap().rearrange("(a p) t -> p a t", p=128), reads=[],
          writes=["b_q"])
    attn = ctx.sb("b_attn", [128, 16 * 1024], BF16)
    pn = ctx.ps("psA")
    pd = ctx.ps("psB")
    pbc = ctx.ps("psC")
    dn = ctx.sb("b_dn", [1, 512], F32)
    rb = ctx.sb("b_rb", [32, 512], BF16)
    onesr = ctx.onesrow_f32()
    qst = ctx.sb("b_qst", [32, 512], BF16)
    for h in range(64):
        b = h // 4
        off = (h % 4) * 32
        for tc in range(2):
            t0 = tc * 512
            B.op("vector",
                 lambda e, b=b, off=off, t0=t0: e.tensor_copy(
                     out=qst[:, 0:512],
                     in_=qsb[off:off + 32, b * 1024 + t0:b * 1024 + t0 + 512]),
                 reads=["b_q"], writes=["b_qst"])
            B.op("tensor",
                 lambda e, h=h: e.matmul(
                     pn[0:32, 0:512], vkb[:, h * 33:h * 33 + 32],
                     qst[:, 0:512], start=True, stop=True),
                 reads=["b_vkb", "b_qst"], writes=["psA"])
            B.op("tensor",
                 lambda e, h=h: e.matmul(
                     pd[0:1, 0:512], vkb[:, h * 33 + 32:h * 33 + 33],
                     qst[:, 0:512], start=True, stop=True),
                 reads=["b_vkb", "b_qst"], writes=["psB"])
            B.op("vector",
                 lambda e: e.tensor_scalar(out=dn[:, 0:512], in0=pd[0:1, 0:512],
                                           scalar1=1e-15, scalar2=None,
                                           op0=ALU.add),
                 reads=["psB"], writes=["b_dn"])
            B.op("vector", lambda e: e.reciprocal(out=dn[:, 0:512], in_=dn[:, 0:512]),
                 reads=["b_dn"], writes=["b_dn"])
            B.op("tensor",
                 lambda e: e.matmul(pbc[0:32, 0:512], onesr[:, 0:32], dn[:, 0:512],
                                    start=True, stop=True),
                 reads=["b_dn", "c_ones_row32"], writes=["psC"])
            B.op("vector", lambda e: e.tensor_copy(out=rb[:, 0:512],
                                                   in_=pbc[0:32, 0:512]),
                 reads=["psC"], writes=["b_rb"])
            B.op("vector",
                 lambda e, b=b, off=off, t0=t0: e.tensor_mul(
                     out=attn[off:off + 32, b * 1024 + t0:b * 1024 + t0 + 512],
                     in0=pn[0:32, 0:512], in1=rb[:, 0:512]),
                 reads=["psA", "b_rb"], writes=["b_attn"])
    x_sb = ctx.sb("b_x", [128, 16 * 512], F32)
    wsl = ctx.sb("b_wsl", [128, 16 * 512], BF16)
    wp_v = io["wprojT"].ap().rearrange("(a p) n -> p a n", p=128)
    xv = io["x_in"].ap().rearrange("(a p) t -> p a t", p=128)
    xov = io["x_out"].ap().rearrange("(a p) t -> p a t", p=128)
    po = ctx.ps("psD")
    to_ = ctx.sb("b_to", [128, 512], F32)
    for chunk in range(2):
        t0 = chunk * 512
        B.dma("sync", x_sb[:].rearrange("p (a t) -> p a t", a=16),
              xv[:, :, t0:t0 + 512], reads=["x_in"], writes=["b_x"])
        for s in range(4):
            B.dma("gpsimd", wsl[:].rearrange("p (a n) -> p a n", a=16),
                  wp_v[:, :, s * 512:(s + 1) * 512], reads=["wprojT"],
                  writes=["b_wsl"])
            for dc in range(4):
                d = s * 4 + dc
                for a in range(16):
                    B.op("tensor",
                         lambda e, s=s, dc=dc, a=a, t0=t0: e.matmul(
                             po[:, 0:512],
                             wsl[:, a * 512 + dc * 128:a * 512 + (dc + 1) * 128],
                             attn[:, a * 1024 + t0:a * 1024 + t0 + 512],
                             start=(a == 0), stop=(a == 15)),
                         reads=["b_wsl", "b_attn"], writes=["psD"])
                B.op("vector",
                     lambda e, d=d: e.tensor_add(out=x_sb[:, d * 512:(d + 1) * 512],
                                                 in0=x_sb[:, d * 512:(d + 1) * 512],
                                                 in1=po[:, 0:512]),
                     reads=["psD"], writes=["b_x"])
                B.op("vector",
                     lambda e, d=d: e.tensor_scalar(
                         out=x_sb[:, d * 512:(d + 1) * 512],
                         in0=x_sb[:, d * 512:(d + 1) * 512],
                         scalar1=bpr[:, d:d + 1], scalar2=None, op0=ALU.add),
                     reads=["b_bpr"], writes=["b_x"])
                B.dma("scalar", xov[:, d, t0:t0 + 512], x_sb[:, d * 512:(d + 1) * 512],
                      reads=["b_x"], writes=["x_out"])
    return B


def host_a2_inputs(inputs):
    wqkv = np.asarray(inputs["L0_attn_b_w_qkv"], np.float32)
    return {
        "wqT": np.ascontiguousarray(wqkv[0:2048].T),
        "wkvT": np.ascontiguousarray(wqkv[2048:6144].T),
        "wprojT": np.ascontiguousarray(
            np.asarray(inputs["L0_attn_b_w_proj"], np.float32).T),
        "bproj": col_layout(inputs["L0_attn_b_b_proj"]),
    }



def build_a4a(B: Builder, io=None):
    ctx = Ctx(B)
    if io is None:
        io = {}
        io["x_in"] = B.dram("x_in", [D_MODEL, TOK_PER_CORE], F32,
                            kind="ExternalInput")
        io["xln"] = B.dram("xln", [D_MODEL, TOK_PER_CORE], BF16,
                           kind="ExternalOutput")
        io["xdbl"] = B.dram("xdbl", [160, TOK_PER_CORE], F32,
                            kind="ExternalOutput")
        io["xprojT"] = B.dram("xprojT", [2048, 160], F32, kind="ExternalInput")
    xln_out = io["xln"]
    xdbl_out = io["xdbl"]
    ctx.io = io
    xp = ctx.sb("a4_xp", [128, 16 * 160], BF16)
    B.dma("gpsimd", xp[:].rearrange("p (a n) -> p a n", a=16),
          io["xprojT"].ap().rearrange("(a p) n -> p a n", p=128), reads=[],
          writes=["a4_xp"])
    x_sb = ctx.sb("a4_x", [128, 16 * 512], F32)
    xn = ctx.sb("a4_xn", [128, 16 * 512], BF16)
    pq = ctx.ps("psA")
    xlv = xln_out.ap().rearrange("(a p) t -> p a t", p=128)
    xdv = xdbl_out.ap()
    t32 = ctx.sb("a4_t32", [128, 512], F32)
    for chunk in range(2):
        t0 = chunk * 512
        xv = io["x_in"].ap().rearrange("(a p) t -> p a t", p=128)
        B.dma("sync", x_sb[:].rearrange("p (a t) -> p a t", a=16),
              xv[:, :, t0:t0 + 512], reads=["x_in"], writes=["a4_x"])
        build_ln(ctx, x_sb, "a4_x", xn, "a4_xn", 512)
        for a in range(16):
            B.dma("scalar", xlv[:, a, t0:t0 + 512], xn[:, a * 512:(a + 1) * 512],
                  reads=["a4_xn"], writes=["xln"])
        for a in range(16):
            B.op("tensor",
                 lambda e, a=a: e.matmul(pq[:, 0:512], xp[:, a * 160:a * 160 + 128],
                                         xn[:, a * 512:(a + 1) * 512],
                                         start=(a == 0), stop=(a == 15)),
                 reads=["a4_xp", "a4_xn"], writes=["psA"])
        B.op("vector", lambda e: e.tensor_copy(out=t32[:, 0:512], in_=pq[:, 0:512]),
             reads=["psA"], writes=["a4_t32"])
        B.dma("scalar", xdv[0:128, t0:t0 + 512], t32[:, 0:512],
              reads=["a4_t32"], writes=["xdbl"])
        for a in range(16):
            B.op("tensor",
                 lambda e, a=a: e.matmul(pq[0:32, 0:512],
                                         xp[:, a * 160 + 128:a * 160 + 160],
                                         xn[:, a * 512:(a + 1) * 512],
                                         start=(a == 0), stop=(a == 15)),
                 reads=["a4_xp", "a4_xn"], writes=["psA"])
        B.op("vector", lambda e: e.tensor_copy(out=t32[0:32, 0:512],
                                               in_=pq[0:32, 0:512]),
             reads=["psA"], writes=["a4_t32"])
        B.dma("scalar", xdv[128:160, t0:t0 + 512], t32[0:32, 0:512],
              reads=["a4_t32"], writes=["xdbl"])
    return B


def build_a4b(B: Builder, io=None):
    ctx = Ctx(B)
    if io is None:
        io = {}
        io["xsl"] = B.dram("xsl", [512, T_SEQ], BF16, kind="ExternalInput")
        io["dt"] = B.dram("dt", [128, T_SEQ], F32, kind="ExternalInput")
        io["bc"] = B.dram("bc", [32, T_SEQ], F32, kind="ExternalInput")
        io["dtpT"] = B.dram("dtpT", [128, 512], F32, kind="ExternalInput")
        io["dtb"] = B.dram("dtb", [128, 4], F32, kind="ExternalInput")
        io["alog"] = B.dram("alog", [128, 4 * 16], F32, kind="ExternalInput")
        io["dcol"] = B.dram("dcol", [128, 4], F32, kind="ExternalInput")
        io["y"] = B.dram("y", [512, T_SEQ], F32, kind="ExternalOutput")
    y_out = io["y"]
    ctx.io = io
    dtp = ctx.sb("s_dtp", [128, 512], BF16)
    B.dma("gpsimd", dtp[:, :], io["dtpT"].ap(), reads=[], writes=["s_dtp"])
    dtb = ctx.sb("s_dtb", [128, 4], F32)
    B.dma("sync", dtb[:, :], io["dtb"].ap(), reads=[], writes=["s_dtb"])
    an = ctx.sb("s_an", [128, 64], F32)
    B.dma("sync", an[:, :], io["alog"].ap(), reads=[], writes=["s_an"])
    B.op("scalar", lambda e: e.activation(out=an[:, :], in_=an[:, :], func=AF.Exp),
         reads=["s_an"], writes=["s_an"])
    B.op("scalar", lambda e: e.activation(out=an[:, :], in_=an[:, :], func=AF.Copy,
                                          scale=-1.0),
         reads=["s_an"], writes=["s_an"])
    dc = ctx.sb("s_dc", [128, 4], F32)
    B.dma("sync", dc[:, :], io["dcol"].ap(), reads=[], writes=["s_dc"])
    dtf = ctx.sb("s_dtf", [128, T_SEQ], BF16)
    B.dma("gpsimd", dtf[:, :], io["dt"].ap(), reads=[], writes=["s_dtf"])
    bcs = ctx.sb("s_bc", [32, T_SEQ], F32)
    B.dma("sync", bcs[:, :], io["bc"].ap(), reads=[], writes=["s_bc"])
    brow = ctx.sb("s_brow", [1, T_SEQ], F32)
    crow = ctx.sb("s_crow", [1, T_SEQ], F32)
    onesr = ctx.onesrow_f32()
    delta = ctx.sb("s_delta", [128, T_SEQ], F32)
    dx = ctx.sb("s_dx", [128, T_SEQ], BF16)
    xg = ctx.sb("s_xg", [128, T_SEQ], BF16)
    dA = ctx.sb("s_dA", [128, T_SEQ], F32)
    dBx = ctx.sb("s_dBx", [128, T_SEQ], BF16)
    hh = ctx.sb("s_h", [128, T_SEQ], F32)
    yg = ctx.sb("s_y", [128, T_SEQ], F32)
    brep = ctx.sb("s_brep", [128, T_SEQ], BF16)
    crep = ctx.sb("s_crep", [128, T_SEQ], BF16)
    hc = ctx.sb("s_hc", [128, 2048], F32)
    pp = ctx.ps("psA")
    t1 = ctx.sb("s_t1", [128, 512], F32)
    t2 = ctx.sb("s_t2", [128, 512], F32)
    for g in range(4):
        for tc in range(8):
            c0 = tc * 512
            B.op("tensor",
                 lambda e, g=g, c0=c0: e.matmul(pp[:, 0:512],
                                                dtp[:, g * 128:(g + 1) * 128],
                                                dtf[:, c0:c0 + 512],
                                                start=True, stop=True),
                 reads=["s_dtp", "s_dtf"], writes=["psA"])
            B.op("scalar",
                 lambda e, g=g: e.activation(out=t1[:, 0:512], in_=pp[:, 0:512],
                                             func=AF.Relu, bias=dtb[:, g:g + 1]),
                 reads=["psA", "s_dtb"], writes=["s_t1"])
            B.op("scalar",
                 lambda e, g=g: e.activation(out=t2[:, 0:512], in_=pp[:, 0:512],
                                             func=AF.Abs, bias=dtb[:, g:g + 1]),
                 reads=["psA", "s_dtb"], writes=["s_t2"])
            B.op("scalar",
                 lambda e: e.activation(out=t2[:, 0:512], in_=t2[:, 0:512],
                                        func=AF.Exp, scale=-1.0),
                 reads=["s_t2"], writes=["s_t2"])
            onec1 = ctx.constcol(1.0, 128)
            B.op("scalar",
                 lambda e, onec1=onec1: e.activation(out=t2[:, 0:512],
                                                     in_=t2[:, 0:512],
                                                     func=AF.Ln, bias=onec1[:, :]),
                 reads=["s_t2", onec1.name], writes=["s_t2"])
            B.op("vector",
                 lambda e, c0=c0: e.tensor_add(out=delta[:, c0:c0 + 512],
                                               in0=t1[:, 0:512], in1=t2[:, 0:512]),
                 reads=["s_t1", "s_t2"], writes=["s_delta"])
        B.dma("sync", xg[:, :], io["xsl"].ap()[g * 128:(g + 1) * 128, :],
              reads=["xsl"], writes=["s_xg"])
        B.op("vector", lambda e: e.tensor_mul(out=dx[:, :], in0=delta[:, :],
                                              in1=xg[:, :]),
             reads=["s_delta", "s_xg"], writes=["s_dx"])
        first = True
        for n in range(16):
            B.dma("gpsimd", brow[:, :], bcs[n:n + 1, :], reads=["s_bc"],
                  writes=["s_brow"])
            B.dma("gpsimd", crow[:, :], bcs[16 + n:17 + n, :], reads=["s_bc"],
                  writes=["s_crow"])
            for tc in range(8):
                c0 = tc * 512
                B.op("tensor",
                     lambda e, c0=c0: e.matmul(
                         pp[:, 0:512], onesr[:, :], brow[:, c0:c0 + 512],
                         start=True, stop=True),
                     reads=["s_brow", "c_ones_row32"], writes=["psA"])
                B.op("vector",
                     lambda e, c0=c0: e.tensor_copy(out=brep[:, c0:c0 + 512],
                                                    in_=pp[:, 0:512]),
                     reads=["psA"], writes=["s_brep"])
                B.op("tensor",
                     lambda e, c0=c0: e.matmul(
                         pp[:, 0:512], onesr[:, :], crow[:, c0:c0 + 512],
                         start=True, stop=True),
                     reads=["s_crow", "c_ones_row32"], writes=["psA"])
                B.op("vector",
                     lambda e, c0=c0: e.tensor_copy(out=crep[:, c0:c0 + 512],
                                                    in_=pp[:, 0:512]),
                     reads=["psA"], writes=["s_crep"])
            B.op("vector",
                 lambda e, g=g, n=n: e.tensor_scalar(
                     out=dA[:, :], in0=delta[:, :],
                     scalar1=an[:, g * 16 + n:g * 16 + n + 1], scalar2=None,
                     op0=ALU.mult),
                 reads=["s_delta", "s_an"], writes=["s_dA"])
            B.op("scalar", lambda e: e.activation(out=dA[:, :], in_=dA[:, :],
                                                  func=AF.Exp),
                 reads=["s_dA"], writes=["s_dA"])
            B.op("vector", lambda e: e.tensor_mul(out=dBx[:, :], in0=dx[:, :],
                                                  in1=brep[:, :]),
                 reads=["s_dx", "s_brep"], writes=["s_dBx"])
            B.op("vector",
                 lambda e: e.tensor_tensor_scan(out=hh[:, :], data0=dA[:, :],
                                                data1=dBx[:, :], initial=0.0,
                                                op0=ALU.mult, op1=ALU.add),
                 reads=["s_dA", "s_dBx"], writes=["s_h"])
            if first:
                B.op("vector", lambda e: e.tensor_mul(out=yg[:, :], in0=hh[:, :],
                                                      in1=crep[:, :]),
                     reads=["s_h", "s_crep"], writes=["s_y"])
                first = False
            else:
                for hf in range(2):
                    B.op("vector",
                         lambda e, hf=hf: e.tensor_mul(
                             out=hc[:, :], in0=hh[:, hf * 2048:(hf + 1) * 2048],
                             in1=crep[:, hf * 2048:(hf + 1) * 2048]),
                         reads=["s_h", "s_crep"], writes=["s_hc"])
                    B.op("vector",
                         lambda e, hf=hf: e.tensor_add(
                             out=yg[:, hf * 2048:(hf + 1) * 2048],
                             in0=yg[:, hf * 2048:(hf + 1) * 2048], in1=hc[:, :]),
                         reads=["s_hc"], writes=["s_y"])
        B.op("vector",
             lambda e, g=g: e.scalar_tensor_tensor(
                 out=yg[:, :], in0=xg[:, :], scalar=dc[:, g:g + 1], in1=yg[:, :],
                 op0=ALU.mult, op1=ALU.add),
             reads=["s_xg", "s_dc", "s_y"], writes=["s_y"])
        B.dma("scalar", y_out.ap()[g * 128:(g + 1) * 128, :], yg[:, :],
              reads=["s_y"], writes=["y"])
    return B


def build_a4c(B: Builder, io=None):
    ctx = Ctx(B)
    if io is None:
        io = {}
        io["x_in"] = B.dram("x_in", [D_MODEL, TOK_PER_CORE], F32,
                            kind="ExternalInput")
        io["y_in"] = B.dram("y_in", [D_MODEL, TOK_PER_CORE], F32,
                            kind="ExternalInput")
        io["x_out"] = B.dram("x_out", [D_MODEL, TOK_PER_CORE], F32,
                             kind="ExternalOutput")
    x_in, y_in, x_out = io["x_in"], io["y_in"], io["x_out"]
    x_sb = ctx.sb("c_x", [128, 16 * 512], F32)
    y_sb = ctx.sb("c_y", [128, 16 * 512], F32)
    xv = x_in.ap().rearrange("(a p) t -> p a t", p=128)
    yv = y_in.ap().rearrange("(a p) t -> p a t", p=128)
    xov = x_out.ap().rearrange("(a p) t -> p a t", p=128)
    for chunk in range(2):
        t0 = chunk * 512
        B.dma("sync", x_sb[:].rearrange("p (a t) -> p a t", a=16),
              xv[:, :, t0:t0 + 512], reads=["x_in"], writes=["c_x"])
        B.dma("scalar", y_sb[:].rearrange("p (a t) -> p a t", a=16),
              yv[:, :, t0:t0 + 512], reads=["y_in"], writes=["c_y"])
        for a in range(16):
            B.op("vector",
                 lambda e, a=a: e.tensor_add(out=x_sb[:, a * 512:(a + 1) * 512],
                                             in0=x_sb[:, a * 512:(a + 1) * 512],
                                             in1=y_sb[:, a * 512:(a + 1) * 512]),
                 reads=["c_y"], writes=["c_x"])
            B.dma("gpsimd", xov[:, a, t0:t0 + 512], x_sb[:, a * 512:(a + 1) * 512],
                  reads=["c_x"], writes=["x_out"])
    return B


def host_a4b_inputs(inputs, xln_shards, xdbl_shards, core):
    b = core // 4
    ch0 = (core % 4) * 512
    cols = {}
    xln_full = np.empty((2048, T_SEQ), dtype=xln_shards[0].dtype)
    xdbl_full = np.empty((160, T_SEQ), dtype=np.float32)
    for j in range(4):
        c = b * 4 + j
        for i, blk in enumerate(core_blocks(c)):
            xln_full[:, blk * BLK:(blk + 1) * BLK] = \
                xln_shards[c][:, i * BLK:(i + 1) * BLK]
            xdbl_full[:, blk * BLK:(blk + 1) * BLK] = \
                xdbl_shards[c][:, i * BLK:(i + 1) * BLK]
    alog = np.asarray(inputs["L1_attn_b_A_log"], np.float32)[ch0:ch0 + 512]
    dpar = np.asarray(inputs["L1_attn_b_D_param"], np.float32)[ch0:ch0 + 512]
    dtw = np.asarray(inputs["L1_attn_b_dt_proj_w"], np.float32)[ch0:ch0 + 512]
    dtb = np.asarray(inputs["L1_attn_b_dt_proj_b"], np.float32)[ch0:ch0 + 512]
    return {
        "xsl": np.ascontiguousarray(xln_full[ch0:ch0 + 512]),
        "dt": np.ascontiguousarray(xdbl_full[0:128]),
        "bc": np.ascontiguousarray(xdbl_full[128:160]),
        "dtpT": np.ascontiguousarray(dtw.T),
        "dtb": col_layout(dtb),
        "alog": np.ascontiguousarray(
            alog.reshape(4, 128, 16).transpose(1, 0, 2).reshape(128, 64)),
        "dcol": col_layout(dpar),
    }


def host_a4_reshard_y(y_slices, core):
    b = core // 4
    out = np.empty((2048, TOK_PER_CORE), np.float32)
    for j in range(4):
        src = y_slices[b * 4 + j]
        for i, blk in enumerate(core_blocks(core)):
            out[j * 512:(j + 1) * 512, i * BLK:(i + 1) * BLK] = \
                src[:, blk * BLK:(blk + 1) * BLK]
    return out



def build_a1a(B: Builder, io=None):
    ctx = Ctx(B)
    if io is None:
        io = {}
        io["x_in"] = B.dram("x_in", [D_MODEL, TOK_PER_CORE], F32,
                            kind="ExternalInput")
        io["wqT"] = B.dram("wqT", [2048, 2048], F32, kind="ExternalInput")
        io["wkvT"] = B.dram("wkvT", [2048, 1024], F32, kind="ExternalInput")
        io["bq"] = B.dram("bq", [128, 16], F32, kind="ExternalInput")
        io["bkv"] = B.dram("bkv", [128, 8], F32, kind="ExternalInput")
        io["peT"] = B.dram("peT", [128, TOK_PER_CORE], F32, kind="ExternalInput")
        io["qp"] = B.dram("qp", [2048, TOK_PER_CORE], BF16, kind="ExternalOutput")
        io["k_fm_loc"] = B.dram("k_fm", [512, TOK_PER_CORE], BF16,
                                kind="ExternalOutput")
        io["v_tok_loc"] = B.dram("v_tok", [TOK_PER_CORE, 512], BF16,
                                 kind="ExternalOutput")
        io["kmax_loc"] = B.dram("kmax", [1, 4], F32, kind="ExternalOutput")
    qp_out = io["qp"]
    kf_out = io["k_fm_loc"]
    vt_out = io["v_tok_loc"]
    km_out = io["kmax_loc"]
    ctx.io = io
    bq = ctx.sb("a1_bq", [128, 16], F32)
    B.dma("sync", bq[:, :], io["bq"].ap(), reads=[], writes=["a1_bq"])
    bkv = ctx.sb("a1_bkv", [128, 8], F32)
    B.dma("sync", bkv[:, :], io["bkv"].ap(), reads=[], writes=["a1_bkv"])
    pe = ctx.sb("a1_pe", [128, TOK_PER_CORE], F32)
    B.dma("sync", pe[:, :], io["peT"].ap(), reads=[], writes=["a1_pe"])
    if "identbf" not in io:
        io["identbf"] = B.dram("identbf_d", [128, 128], F32, kind="ExternalInput")
    idbf = ctx.sb("a1_idbf", [128, 128], BF16)
    B.dma("gpsimd", idbf[:, :], io["identbf"].ap(), reads=[], writes=["a1_idbf"])
    vt = ctx.sb("a1_vt", [128, 8 * 512], BF16)
    psB_bf = ctx.ps("psB", dtype=BF16)
    wsl = ctx.sb("a1_wsl", [128, 16 * 512], BF16)
    x_sb = ctx.sb("a1_x", [128, 16 * 512], F32)
    xn = ctx.sb("a1_xn", [128, 16 * 512], BF16)
    pq = ctx.ps("psA")
    t32 = ctx.sb("a1_t32", [128, 512], F32)
    tbf = ctx.sb("a1_tbf", [128, 512], BF16)
    ktil = ctx.sb("a1_k", [128, 4 * 1024], BF16)
    wq_v = io["wqT"].ap().rearrange("(a p) n -> p a n", p=128)
    wkv_v = io["wkvT"].ap().rearrange("(a p) n -> p a n", p=128)
    qv = qp_out.ap().rearrange("(a p) t -> p a t", p=128)
    inv_s = 1.0 / math.sqrt(128.0)
    for chunk in range(2):
        t0 = chunk * 512
        xv = io["x_in"].ap().rearrange("(a p) t -> p a t", p=128)
        B.dma("sync", x_sb[:].rearrange("p (a t) -> p a t", a=16),
              xv[:, :, t0:t0 + 512], reads=["x_in"], writes=["a1_x"])
        build_ln(ctx, x_sb, "a1_x", xn, "a1_xn", 512)
        for s in range(4):
            B.dma("gpsimd", wsl[:].rearrange("p (a n) -> p a n", a=16),
                  wq_v[:, :, s * 512:(s + 1) * 512], reads=["wqT"], writes=["a1_wsl"])
            for dc in range(4):
                d = s * 4 + dc
                for a in range(16):
                    B.op("tensor",
                         lambda e, s=s, dc=dc, a=a: e.matmul(
                             pq[:, 0:512],
                             wsl[:, a * 512 + dc * 128:a * 512 + (dc + 1) * 128],
                             xn[:, a * 512:(a + 1) * 512],
                             start=(a == 0), stop=(a == 15)),
                         reads=["a1_wsl", "a1_xn"], writes=["psA"])
                B.op("vector",
                     lambda e, d=d: e.tensor_scalar(out=t32[:, 0:512],
                                                    in0=pq[:, 0:512],
                                                    scalar1=bq[:, d:d + 1],
                                                    scalar2=None, op0=ALU.add),
                     reads=["psA", "a1_bq"], writes=["a1_t32"])
                B.op("vector",
                     lambda e, t0=t0: e.tensor_add(out=t32[:, 0:512],
                                                   in0=t32[:, 0:512],
                                                   in1=pe[:, t0:t0 + 512]),
                     reads=["a1_pe", "a1_t32"], writes=["a1_t32"])
                B.op("vector",
                     lambda e: e.tensor_scalar(out=tbf[:, 0:512], in0=t32[:, 0:512],
                                               scalar1=inv_s, scalar2=None,
                                               op0=ALU.mult),
                     reads=["a1_t32"], writes=["a1_tbf"])
                B.dma("scalar", qv[:, d, t0:t0 + 512], tbf[:, 0:512],
                      reads=["a1_tbf"], writes=["qp"])
        for s in range(2):
            B.dma("gpsimd", wsl[:].rearrange("p (a n) -> p a n", a=16),
                  wkv_v[:, :, s * 512:(s + 1) * 512], reads=["wkvT"],
                  writes=["a1_wsl"])
            for dc in range(4):
                d = s * 4 + dc
                g = d // 2
                is_k = (d % 2 == 0)
                for a in range(16):
                    B.op("tensor",
                         lambda e, s=s, dc=dc, a=a: e.matmul(
                             pq[:, 0:512],
                             wsl[:, a * 512 + dc * 128:a * 512 + (dc + 1) * 128],
                             xn[:, a * 512:(a + 1) * 512],
                             start=(a == 0), stop=(a == 15)),
                         reads=["a1_wsl", "a1_xn"], writes=["psA"])
                B.op("vector",
                     lambda e, d=d: e.tensor_scalar(out=t32[:, 0:512],
                                                    in0=pq[:, 0:512],
                                                    scalar1=bkv[:, d:d + 1],
                                                    scalar2=None, op0=ALU.add),
                     reads=["psA", "a1_bkv"], writes=["a1_t32"])
                if is_k:
                    B.op("vector",
                         lambda e, g=g, t0=t0: e.tensor_add(
                             out=ktil[:, g * 1024 + t0:g * 1024 + t0 + 512],
                             in0=t32[:, 0:512], in1=pe[:, t0:t0 + 512]),
                         reads=["a1_pe", "a1_t32"], writes=["a1_k"])
                else:
                    B.op("vector",
                         lambda e: e.tensor_copy(out=tbf[:, 0:512],
                                                 in_=t32[:, 0:512]),
                         reads=["a1_t32"], writes=["a1_tbf"])
                    for ts in range(4):
                        B.op("tensor",
                             lambda e, ts=ts: e.transpose(
                                 psB_bf[:, 0:128], tbf[:, ts * 128:(ts + 1) * 128],
                                 idbf[:, :]),
                             reads=["a1_tbf", "a1_idbf"], writes=["psB"])
                        col = 0
                        B.op("vector",
                             lambda e, g=g, ts=ts, chunk=chunk: e.tensor_copy(
                                 out=vt[:, (chunk * 4 + ts) * 512 + g * 128:
                                        (chunk * 4 + ts) * 512 + g * 128 + 128],
                                 in_=psB_bf[:, 0:128]),
                             reads=["psB"], writes=["a1_vt"])
    ksq = ctx.sb("a1_ksq", [128, 512], F32)
    kmx = ctx.sb("a1_kmx", [1, 8], F32)
    pk = ctx.ps("psC")
    onec = ctx.onescol_f32()
    for g in range(4):
        for hc2 in range(2):
            B.op("scalar",
                 lambda e, g=g, hc2=hc2: e.activation(
                     out=ksq[:, 0:512],
                     in_=ktil[:, g * 1024 + hc2 * 512:g * 1024 + hc2 * 512 + 512],
                     func=AF.Square),
                 reads=["a1_k"], writes=["a1_ksq"])
            B.op("tensor",
                 lambda e: e.matmul(pk[0:1, 0:512], onec[:, :], ksq[:, 0:512],
                                    start=True, stop=True),
                 reads=["a1_ksq", "c_ones_col32"], writes=["psC"])
            B.op("vector",
                 lambda e, g=g, hc2=hc2: e.reduce_max(
                     out=kmx[0:1, g * 2 + hc2:g * 2 + hc2 + 1],
                     in_=pk[0:1, 0:512], axis=AX.X),
                 reads=["psC"], writes=["a1_kmx"])
    kmo = ctx.sb("a1_kmo", [1, 4], F32)
    for g in range(4):
        B.op("vector",
             lambda e, g=g: e.tensor_max(out=kmo[0:1, g:g + 1],
                                         in0=kmx[0:1, g * 2:g * 2 + 1],
                                         in1=kmx[0:1, g * 2 + 1:g * 2 + 2]),
             reads=["a1_kmx"], writes=["a1_kmo"])
    B.dma("scalar", km_out.ap(), kmo[:, :], reads=["a1_kmo"], writes=["kmax"])
    kfv = kf_out.ap().rearrange("(g p) t -> p g t", p=128)
    for g in range(4):
        B.dma("scalar", kfv[:, g, :], ktil[:, g * 1024:(g + 1) * 1024],
              reads=["a1_k"], writes=["k_fm"])
    vtv = vt_out.ap().rearrange("(s p) d -> p s d", p=128)
    B.dma("scalar", vtv[:, :, :], vt[:].rearrange("p (s d) -> p s d", s=8),
          reads=["a1_vt"], writes=["v_tok"])
    return B


def build_a1b(B: Builder, io=None):
    ctx = Ctx(B)
    if io is None:
        io = {}
        io["x_in"] = B.dram("x_in", [D_MODEL, TOK_PER_CORE], F32,
                            kind="ExternalInput")
        io["x_out"] = B.dram("x_out", [D_MODEL, TOK_PER_CORE], F32,
                             kind="ExternalOutput")
        io["qp"] = B.dram("qp", [2048, TOK_PER_CORE], BF16, kind="ExternalInput")
        io["k_fm"] = B.dram("k_fm", [512, T_SEQ], BF16, kind="ExternalInput")
        io["v_tok"] = B.dram("v_tok", [T_SEQ, 512], BF16, kind="ExternalInput")
        io["woT"] = B.dram("woT", [2048, 2048], F32, kind="ExternalInput")
        io["ab"] = B.dram("ab", [128, 32], F32, kind="ExternalInput")
        io["diag"] = B.dram("diag", [128, 4 * 512], F32, kind="ExternalInput")
        for i in range(4):
            io[f"kmax{i}"] = B.dram(f"kmax{i}", [1, 4], F32,
                                    kind="ExternalInput").ap()
        io["identbf"] = B.dram("identbf_d", [128, 128], F32,
                               kind="ExternalInput")
    ctx.io = io
    NH, NKV, DV = 16, 4, 128
    idbf = ctx.sb("f_idbf", [128, 128], BF16)
    B.dma("gpsimd", idbf[:, :], io["identbf"].ap(), reads=[], writes=["f_idbf"])
    ab = ctx.sb("f_ab", [128, 32], F32)
    B.dma("sync", ab[:, :], io["ab"].ap(), reads=[], writes=["f_ab"])
    diag = ctx.sb("f_diag", [128, 4 * 512], BF16)
    B.dma("gpsimd", diag[:].rearrange("p (k t) -> p k t", k=4),
          io["diag"].ap().rearrange("p (k t) -> p k t", k=4), reads=[],
          writes=["f_diag"])
    km = ctx.sb("f_km", [1, 4], F32)
    kmt = ctx.sb("f_kmt", [1, 4], F32)
    B.dma("sync", km[:, :], io["kmax0"], reads=["d_kv_ag"], writes=["f_km"])
    for i in range(1, 4):
        B.dma("sync", kmt[:, :], io[f"kmax{i}"], reads=["d_kv_ag"],
              writes=["f_kmt"])
        B.op("vector", lambda e: e.tensor_max(out=km[:, :], in0=km[:, :],
                                              in1=kmt[:, :]),
             reads=["f_kmt"], writes=["f_km"])
    qsb = ctx.sb("f_q", [128, 16 * 1024], BF16)
    B.dma("sync", qsb[:].rearrange("p (a t) -> p a t", a=16),
          io["qp"].ap().rearrange("(a p) t -> p a t", p=128), reads=["d_qp"],
          writes=["f_q"])
    attn = ctx.sb("f_attn", [128, 16 * 1024], BF16)
    kch = ctx.sb("f_kch", [128, 4 * 512], BF16)
    vch = ctx.sb("f_vch", [128, 4 * 512], BF16)
    vaug = ctx.sb("f_vaug", [128, 16 * 129], BF16)
    for gs in range(16):
        B.op("vector", lambda e, gs=gs: e.memset(
                 vaug[:, gs * 129 + 128:gs * 129 + 129], 1.0),
             writes=["f_vaug"])
    outsb = ctx.sb("f_out", [128, 64 * 129], F32)
    qsq = ctx.sb("f_qsq", [128, 512], F32)
    qmx = ctx.sb("f_qmx", [1, 16], F32)
    bsq = ctx.sb("f_bsq", [1, 16], F32)
    nbc = ctx.sb("f_nbc", [128, 16], F32)
    pt = ctx.sb("f_pt", [128, 512], BF16)
    msk = ctx.sb("f_msk", [128, 4 * 512], BF16)
    stg = ctx.sb("f_stg", [128, 128], BF16)
    psS = ctx.ps("psA")
    psC = ctx.ps("psC")
    psCb = ctx.ps("psCb", dtype=BF16)
    psO = [ctx.ps(f"psO{i}", (128, 512)) for i in range(4)]
    onec = ctx.onescol_f32()
    onesr = ctx.onesrow_f32()
    kv_v = io["k_fm"].ap().rearrange("(g p) t -> p g t", p=128)
    vt_v = io["v_tok"].ap().rearrange("(s p) d -> p s d", p=128)
    for qb in range(2):
        q0 = qb * 512
        for a in range(NH):
            for hc2 in range(1):
                B.op("scalar",
                     lambda e, a=a, q0=q0: e.activation(
                         out=qsq[:, 0:512], in_=qsb[:, a * 1024 + q0:a * 1024 + q0 + 512],
                         func=AF.Square),
                     reads=["f_q"], writes=["f_qsq"])
                B.op("tensor",
                     lambda e: e.matmul(psC[0:1, 0:512], onec[:, :], qsq[:, 0:512],
                                        start=True, stop=True),
                     reads=["f_qsq", "c_ones_col32"], writes=["psC"])
                B.op("vector",
                     lambda e, a=a: e.reduce_max(out=qmx[0:1, a:a + 1],
                                                 in_=psC[0:1, 0:512], axis=AX.X),
                     reads=["psC"], writes=["f_qmx"])
            B.op("vector",
                 lambda e, a=a: e.tensor_mul(out=bsq[0:1, a:a + 1],
                                             in0=qmx[0:1, a:a + 1],
                                             in1=km[0:1, a // 4:a // 4 + 1]),
                 reads=["f_qmx", "f_km"], writes=["f_bsq"])
        B.op("scalar", lambda e: e.activation(out=bsq[0:1, 0:16], in_=bsq[0:1, 0:16],
                                              func=AF.Ln),
             reads=["f_bsq"], writes=["f_bsq"])
        B.op("scalar", lambda e: e.activation(out=bsq[0:1, 0:16], in_=bsq[0:1, 0:16],
                                              func=AF.Exp, scale=0.5),
             reads=["f_bsq"], writes=["f_bsq"])
        B.op("scalar", lambda e: e.activation(out=bsq[0:1, 0:16], in_=bsq[0:1, 0:16],
                                              func=AF.Copy, scale=-1.0),
             reads=["f_bsq"], writes=["f_bsq"])
        B.op("tensor", lambda e: e.matmul(psC[:, 0:16], onesr[:, :], bsq[0:1, 0:16],
                                          start=True, stop=True),
             reads=["f_bsq", "c_ones_row32"], writes=["psC"])
        B.op("vector", lambda e: e.tensor_copy(out=nbc[:, 0:16], in_=psC[:, 0:16]),
             reads=["psC"], writes=["f_nbc"])
        first = True
        for c in range(8):
            B.dma("sync", kch[:].rearrange("p (g t) -> p g t", g=4),
                  kv_v[:, :, c * 512:(c + 1) * 512], reads=["d_kv_ag"],
                  writes=["f_kch"])
            B.dma("scalar", vch[:].rearrange("p (s d) -> p s d", s=4),
                  vt_v[:, c * 4:c * 4 + 4, :], reads=["d_kv_ag"],
                  writes=["f_vch"])
            for g in range(4):
                for ks in range(4):
                    B.op("vector",
                         lambda e, g=g, ks=ks: e.tensor_copy(
                             out=vaug[:, (g * 4 + ks) * 129:(g * 4 + ks) * 129 + 128],
                             in_=vch[:, ks * 512 + g * 128:ks * 512 + g * 128 + 128]),
                         reads=["f_vch"], writes=["f_vaug"])
            abidx = qb * 16 + c * 2
            for ks in range(4):
                B.op("vector",
                     lambda e, ks=ks, abidx=abidx: e.tensor_scalar(
                         out=msk[:, ks * 512:(ks + 1) * 512],
                         in0=diag[:, ks * 512:(ks + 1) * 512],
                         scalar1=ab[:, abidx + 1:abidx + 2],
                         scalar2=ab[:, abidx:abidx + 1],
                         op0=ALU.mult, op1=ALU.add),
                     reads=["f_diag", "f_ab"], writes=["f_msk"])
            for a in range(NH):
                g = a // 4
                for ks in range(4):
                    B.op("tensor",
                         lambda e, a=a, g=g, ks=ks, q0=q0: e.matmul(
                             psS[:, 0:512],
                             kch[:, g * 512 + ks * 128:g * 512 + (ks + 1) * 128],
                             qsb[:, a * 1024 + q0:a * 1024 + q0 + 512],
                             start=True, stop=True),
                         reads=["f_kch", "f_q"], writes=["psA"])
                    B.op("scalar",
                         lambda e, a=a: e.activation(
                             out=pt[:, 0:512], in_=psS[:, 0:512], func=AF.Exp,
                             bias=nbc[:, a:a + 1]),
                         reads=["psA", "f_nbc"], writes=["f_pt"])
                    B.op("vector",
                         lambda e, ks=ks: e.tensor_mul(
                             out=pt[:, 0:512], in0=pt[:, 0:512],
                             in1=msk[:, ks * 512:(ks + 1) * 512]),
                         reads=["f_msk"], writes=["f_pt"])
                    for qs in range(4):
                        B.op("tensor",
                             lambda e, g=g, ks=ks, qs=qs: e.matmul(
                                 psO[qs][:, 0:129],
                                 pt[:, qs * 128:(qs + 1) * 128],
                                 vaug[:, (g * 4 + ks) * 129:(g * 4 + ks + 1) * 129],
                                 start=(ks == 0), stop=(ks == 3)),
                             reads=["f_pt", "f_vaug"], writes=[f"psO{qs}"])
                for qs in range(4):
                    oidx = (a * 4 + qs) * 129
                    if first:
                        B.op("vector",
                             lambda e, qs=qs, oidx=oidx: e.tensor_copy(
                                 out=outsb[:, oidx:oidx + 129],
                                 in_=psO[qs][:, 0:129]),
                             reads=[f"psO{qs}"], writes=["f_out"])
                    else:
                        B.op("vector",
                             lambda e, qs=qs, oidx=oidx: e.tensor_add(
                                 out=outsb[:, oidx:oidx + 129],
                                 in0=outsb[:, oidx:oidx + 129],
                                 in1=psO[qs][:, 0:129]),
                             reads=[f"psO{qs}"], writes=["f_out"])
            first = False
        rcol = ctx.sb("f_rcol", [128, 1], F32)
        for a in range(NH):
            for qs in range(4):
                oidx = (a * 4 + qs) * 129
                B.op("vector",
                     lambda e, oidx=oidx: e.reciprocal(
                         out=rcol[:, :], in_=outsb[:, oidx + 128:oidx + 129]),
                     reads=["f_out"], writes=["f_rcol"])
                B.op("vector",
                     lambda e, oidx=oidx: e.tensor_scalar(
                         out=stg[:, 0:128], in0=outsb[:, oidx:oidx + 128],
                         scalar1=rcol[:, :], scalar2=None, op0=ALU.mult),
                     reads=["f_out", "f_rcol"], writes=["f_stg"])
                B.op("tensor",
                     lambda e: e.transpose(psCb[:, 0:128], stg[:, 0:128],
                                           ctx.pool["f_idbf"][:, :]),
                     reads=["f_stg", "f_idbf"], writes=["psCb"])
                B.op("vector",
                     lambda e, a=a, qs=qs, q0=q0: e.tensor_copy(
                         out=attn[:, a * 1024 + q0 + qs * 128:
                                  a * 1024 + q0 + (qs + 1) * 128],
                         in_=psCb[:, 0:128]),
                     reads=["psCb"], writes=["f_attn"])
    x_sb = ctx.sb("f_x", [128, 16 * 512], F32)
    wsl = ctx.sb("f_wsl", [128, 16 * 512], BF16)
    wo_v = io["woT"].ap().rearrange("(a p) n -> p a n", p=128)
    xv = io["x_in"].ap().rearrange("(a p) t -> p a t", p=128)
    xov = io["x_out"].ap().rearrange("(a p) t -> p a t", p=128)
    psW = ctx.ps("psD")
    for chunk in range(2):
        t0 = chunk * 512
        B.dma("sync", x_sb[:].rearrange("p (a t) -> p a t", a=16),
              xv[:, :, t0:t0 + 512], reads=["x_in"], writes=["f_x"])
        for s in range(4):
            B.dma("gpsimd", wsl[:].rearrange("p (a n) -> p a n", a=16),
                  wo_v[:, :, s * 512:(s + 1) * 512], reads=["woT"], writes=["f_wsl"])
            for dc in range(4):
                d = s * 4 + dc
                for a in range(16):
                    B.op("tensor",
                         lambda e, s=s, dc=dc, a=a, t0=t0: e.matmul(
                             psW[:, 0:512],
                             wsl[:, a * 512 + dc * 128:a * 512 + (dc + 1) * 128],
                             attn[:, a * 1024 + t0:a * 1024 + t0 + 512],
                             start=(a == 0), stop=(a == 15)),
                         reads=["f_wsl", "f_attn"], writes=["psD"])
                B.op("vector",
                     lambda e, d=d: e.tensor_add(out=x_sb[:, d * 512:(d + 1) * 512],
                                                 in0=x_sb[:, d * 512:(d + 1) * 512],
                                                 in1=psW[:, 0:512]),
                     reads=["psD"], writes=["f_x"])
                B.dma("scalar", xov[:, d, t0:t0 + 512], x_sb[:, d * 512:(d + 1) * 512],
                      reads=["f_x"], writes=["x_out"])
    return B


def host_a1_inputs(inputs, core):
    wq = np.asarray(inputs["L0_attn_a_wq"], np.float32)
    wkv = np.asarray(inputs["L0_attn_a_wkv"], np.float32)
    wo = np.asarray(inputs["L0_attn_a_wo"], np.float32)
    pe = sinusoidal_pe(T_SEQ, 128)
    blks = core_blocks(core)
    pe_core = np.concatenate([pe[b * BLK:(b + 1) * BLK] for b in blks], 0)
    ab = np.zeros(32, np.float32)
    for qi, qpos in enumerate(blks):
        for c in range(8):
            al = 1.0 if c < qpos else 0.0
            be = 1.0 if c == qpos else 0.0
            ab[qi * 16 + c * 2] = al
            ab[qi * 16 + c * 2 + 1] = be
    ab_rep = np.ascontiguousarray(np.broadcast_to(ab[None, :], (128, 32)))
    dg = np.zeros((128, 4 * 512), np.float32)
    for ks in range(4):
        kk = np.arange(128)[:, None] + ks * 128
        qq = np.arange(512)[None, :]
        dg[:, ks * 512:(ks + 1) * 512] = (kk <= qq).astype(np.float32)
    return {
        "wqT": wq,
        "wkvT": wkv,
        "woT": wo,
        "bq": col_layout(inputs["L0_attn_a_bq"]),
        "bkv": col_layout(inputs["L0_attn_a_bkv"]),
        "peT": np.ascontiguousarray(pe_core.T),
        "identbf_d": np.eye(128, dtype=np.float32),
        "ab": ab_rep,
        "diag": dg,
    }



def build_a3a(B: Builder, io=None):
    ctx = Ctx(B)
    if io is None:
        io = {}
        io["x_in"] = B.dram("x_in", [D_MODEL, TOK_PER_CORE], F32,
                            kind="ExternalInput")
        io["wqT"] = B.dram("wqTp", [2048, 3072], F32, kind="ExternalInput")
        io["wkvaT"] = B.dram("wkvaT", [2048, 576], F32, kind="ExternalInput")
        io["wkvbTk"] = B.dram("wkvbTk", [512, 2048], F32, kind="ExternalInput")
        io["wkvbTv"] = B.dram("wkvbTv", [512, 2048], F32, kind="ExternalInput")
        io["gam"] = B.dram("gam", [128, 4], F32, kind="ExternalInput")
        io["cos2"] = B.dram("cos2", [128, 2], F32, kind="ExternalInput")
        io["csraw"] = B.dram("csraw", [64, 2], F32, kind="ExternalInput")
        io["qn"] = B.dram("qn", [2048, TOK_PER_CORE], BF16, kind="ExternalOutput")
        io["qpe"] = B.dram("qpe", [64, 16 * TOK_PER_CORE], BF16,
                           kind="ExternalOutput")
        io["knope_loc"] = B.dram("knope", [2048, TOK_PER_CORE], BF16,
                                 kind="ExternalOutput")
        io["kpe_loc"] = B.dram("kpe", [64, TOK_PER_CORE], BF16,
                               kind="ExternalOutput")
        io["v_tok_loc"] = B.dram("v_tok", [TOK_PER_CORE, 2048], BF16,
                                 kind="ExternalOutput")
        io["kmax_loc"] = B.dram("kmax", [1, 16], F32, kind="ExternalOutput")
    qn_out = io["qn"]
    qpe_out = io["qpe"]
    kn_out = io["knope_loc"]
    kpe_out = io["kpe_loc"]
    vt_out = io["v_tok_loc"]
    km_out = io["kmax_loc"]
    ctx.io = io
    SC = 192.0 ** (-0.5)
    cs = ctx.sb("a3_cs", [128, 2], F32)
    B.dma("sync", cs[:, :], io["cos2"].ap(), reads=[], writes=["a3_cs"])
    B.op("scalar", lambda e: e.activation(out=cs[:, :], in_=cs[:, :], func=AF.Copy,
                                          scale=SC),
         reads=["a3_cs"], writes=["a3_cs"])
    csr = ctx.sb("a3_csraw", [64, 2], F32)
    B.dma("sync", csr[:, :], io["csraw"].ap(), reads=[], writes=["a3_csraw"])
    io["csraw_s"] = csr
    gam = ctx.sb("a3_gam", [128, 4], F32)
    B.dma("sync", gam[:, :], io["gam"].ap(), reads=[], writes=["a3_gam"])
    wsl = ctx.sb("a3_wsl", [128, 16 * 512], BF16)
    x_sb = ctx.sb("a3_x", [128, 16 * 512], F32)
    xn = ctx.sb("a3_xn", [128, 16 * 512], BF16)
    pq = ctx.ps("psA")
    t32 = ctx.sb("a3_t32", [128, 512], F32)
    rot = ctx.sb("a3_rot", [128, 512], F32)
    tbf = ctx.sb("a3_tbf", [128, 512], BF16)
    ckvn = ctx.sb("a3_ckvn", [128, 4 * 1024], BF16)
    kpe_sb = ctx.sb("a3_kpe", [64, 1024], BF16)
    wq_v = io["wqT"].ap().rearrange("(a p) n -> p a n", p=128)
    wkva_v = io["wkvaT"].ap().rearrange("(a p) n -> p a n", p=128)
    qv = qn_out.ap().rearrange("(a p) t -> p a t", p=128)
    for chunk in range(2):
        t0 = chunk * 512
        xv = io["x_in"].ap().rearrange("(a p) t -> p a t", p=128)
        B.dma("sync", x_sb[:].rearrange("p (a t) -> p a t", a=16),
              xv[:, :, t0:t0 + 512], reads=["x_in"], writes=["a3_x"])
        build_ln(ctx, x_sb, "a3_x", xn, "a3_xn", 512)
        for s in range(6):
            B.dma("gpsimd", wsl[:].rearrange("p (a n) -> p a n", a=16),
                  wq_v[:, :, s * 512:(s + 1) * 512], reads=["wqTp"],
                  writes=["a3_wsl"])
            for dc in range(4):
                d = s * 4 + dc
                for a in range(16):
                    B.op("tensor",
                         lambda e, s=s, dc=dc, a=a: e.matmul(
                             pq[:, 0:512],
                             wsl[:, a * 512 + dc * 128:a * 512 + (dc + 1) * 128],
                             xn[:, a * 512:(a + 1) * 512],
                             start=(a == 0), stop=(a == 15)),
                         reads=["a3_wsl", "a3_xn"], writes=["psA"])
                if d < 16:
                    B.op("scalar",
                         lambda e: e.activation(out=tbf[:, 0:512], in_=pq[:, 0:512],
                                                func=AF.Copy, scale=SC),
                         reads=["psA"], writes=["a3_tbf"])
                    B.dma("scalar", qv[:, d, t0:t0 + 512], tbf[:, 0:512],
                          reads=["a3_tbf"], writes=["qn"])
                else:
                    B.op("vector",
                         lambda e: e.tensor_copy(out=t32[:, 0:512], in_=pq[:, 0:512]),
                         reads=["psA"], writes=["a3_t32"])
                    for (o_, i_, sg) in ((0, 32, -1.0), (32, 0, 1.0),
                                         (64, 96, -1.0), (96, 64, 1.0)):
                        B.op("vector",
                             lambda e, o_=o_, i_=i_, sg=sg: e.tensor_scalar(
                                 out=rot[o_:o_ + 32, 0:512],
                                 in0=t32[i_:i_ + 32, 0:512], scalar1=sg,
                                 scalar2=None, op0=ALU.mult),
                             reads=["a3_t32"], writes=["a3_rot"])
                    B.op("vector",
                         lambda e: e.tensor_scalar(out=t32[:, 0:512],
                                                   in0=t32[:, 0:512],
                                                   scalar1=cs[:, 0:1], scalar2=None,
                                                   op0=ALU.mult),
                         reads=["a3_t32", "a3_cs"], writes=["a3_t32"])
                    B.op("vector",
                         lambda e: e.tensor_scalar(out=rot[:, 0:512],
                                                   in0=rot[:, 0:512],
                                                   scalar1=cs[:, 1:2], scalar2=None,
                                                   op0=ALU.mult),
                         reads=["a3_rot", "a3_cs"], writes=["a3_rot"])
                    B.op("vector",
                         lambda e: e.tensor_add(out=tbf[:, 0:512], in0=t32[:, 0:512],
                                                in1=rot[:, 0:512]),
                         reads=["a3_t32", "a3_rot"], writes=["a3_tbf"])
                    h2 = (d - 16) * 2
                    B.dma("scalar",
                          qpe_out.ap()[:, h2 * TOK_PER_CORE + t0:
                                       h2 * TOK_PER_CORE + t0 + 512],
                          tbf[0:64, 0:512], reads=["a3_tbf"], writes=["qpe"])
                    B.dma("scalar",
                          qpe_out.ap()[:, (h2 + 1) * TOK_PER_CORE + t0:
                                       (h2 + 1) * TOK_PER_CORE + t0 + 512],
                          tbf[64:128, 0:512], reads=["a3_tbf"], writes=["qpe"])
        B.dma("gpsimd", wsl[:].rearrange("p (a n) -> p a n", a=16),
              wkva_v[:, :, 0:512], reads=["wkvaT"], writes=["a3_wsl"])
        ck32 = ctx.sb("a3_ck32", [128, 4 * 512], F32)
        for dc in range(4):
            for a in range(16):
                B.op("tensor",
                     lambda e, dc=dc, a=a: e.matmul(
                         pq[:, 0:512],
                         wsl[:, a * 512 + dc * 128:a * 512 + (dc + 1) * 128],
                         xn[:, a * 512:(a + 1) * 512],
                         start=(a == 0), stop=(a == 15)),
                     reads=["a3_wsl", "a3_xn"], writes=["psA"])
            B.op("vector",
                 lambda e, dc=dc: e.tensor_copy(out=ck32[:, dc * 512:(dc + 1) * 512],
                                                in_=pq[:, 0:512]),
                 reads=["psA"], writes=["a3_ck32"])
        wsl2 = ctx.sb("a3_wsl2", [128, 16 * 64], BF16)
        B.dma("gpsimd", wsl2[:].rearrange("p (a n) -> p a n", a=16),
              wkva_v[:, :, 512:576], reads=["wkvaT"], writes=["a3_wsl2"])
        for a in range(16):
            B.op("tensor",
                 lambda e, a=a: e.matmul(
                     pq[0:64, 0:512], wsl2[:, a * 64:(a + 1) * 64],
                     xn[:, a * 512:(a + 1) * 512],
                     start=(a == 0), stop=(a == 15)),
                 reads=["a3_wsl2", "a3_xn"], writes=["psA"])
        B.op("vector", lambda e: e.tensor_copy(out=t32[0:64, 0:512],
                                               in_=pq[0:64, 0:512]),
             reads=["psA"], writes=["a3_t32"])
        for (o_, i_, sg) in ((0, 32, -1.0), (32, 0, 1.0)):
            B.op("vector",
                 lambda e, o_=o_, i_=i_, sg=sg: e.tensor_scalar(
                     out=rot[o_:o_ + 32, 0:512], in0=t32[i_:i_ + 32, 0:512],
                     scalar1=sg, scalar2=None, op0=ALU.mult),
                 reads=["a3_t32"], writes=["a3_rot"])
        B.op("vector",
             lambda e: e.tensor_scalar(out=t32[0:64, 0:512], in0=t32[0:64, 0:512],
                                       scalar1=io["csraw_s"][:, 0:1], scalar2=None,
                                       op0=ALU.mult),
             reads=["a3_t32", "a3_csraw"], writes=["a3_t32"])
        B.op("vector",
             lambda e: e.tensor_scalar(out=rot[0:64, 0:512], in0=rot[0:64, 0:512],
                                       scalar1=io["csraw_s"][:, 1:2], scalar2=None,
                                       op0=ALU.mult),
             reads=["a3_rot", "a3_csraw"], writes=["a3_rot"])
        B.op("vector",
             lambda e, t0=t0: e.tensor_add(out=kpe_sb[:, t0:t0 + 512],
                                           in0=t32[0:64, 0:512],
                                           in1=rot[0:64, 0:512]),
             reads=["a3_t32", "a3_rot"], writes=["a3_kpe"])
        sq = ctx.sb("ln_sq4", [128, 4 * 512], F32)
        st_q = ctx.ps("psD")
        onec = ctx.onescol_f32()
        for dc in range(4):
            B.op("scalar",
                 lambda e, dc=dc: e.activation(out=sq[:, dc * 512:(dc + 1) * 512],
                                               in_=ck32[:, dc * 512:(dc + 1) * 512],
                                               func=AF.Square),
                 reads=["a3_ck32"], writes=["ln_sq4"])
        for dc in range(4):
            B.op("tensor",
                 lambda e, dc=dc: e.matmul(st_q[0:1, 0:512], onec[:, :],
                                           sq[:, dc * 512:(dc + 1) * 512],
                                           start=(dc == 0), stop=(dc == 3)),
                 reads=["ln_sq4", "c_ones_col32"], writes=["psD"])
        ms = ctx.sb("a3_ms", [1, 512], F32)
        B.op("scalar", lambda e: e.activation(out=ms[:, 0:512], in_=st_q[0:1, 0:512],
                                              func=AF.Copy, scale=1.0 / 512.0),
             reads=["psD"], writes=["a3_ms"])
        eps6 = ctx.constcol(1e-6)
        B.op("scalar", lambda e, eps6=eps6: e.activation(out=ms[:, 0:512],
                                                         in_=ms[:, 0:512],
                                                         func=AF.Ln,
                                                         bias=eps6[:, :]),
             reads=["a3_ms", eps6.name], writes=["a3_ms"])
        B.op("scalar", lambda e: e.activation(out=ms[:, 0:512], in_=ms[:, 0:512],
                                              func=AF.Exp, scale=-0.5),
             reads=["a3_ms"], writes=["a3_ms"])
        psb = ctx.ps("psC")
        onesr = ctx.onesrow_f32()
        rsb = ctx.sb("a3_rsb", [128, 512], F32)
        B.op("tensor", lambda e: e.matmul(psb[:, 0:512], onesr[:, :], ms[:, 0:512],
                                          start=True, stop=True),
             reads=["a3_ms", "c_ones_row32"], writes=["psC"])
        B.op("vector", lambda e: e.tensor_copy(out=rsb[:, 0:512], in_=psb[:, 0:512]),
             reads=["psC"], writes=["a3_rsb"])
        for dc in range(4):
            B.op("vector",
                 lambda e, dc=dc: e.tensor_mul(out=ck32[:, dc * 512:(dc + 1) * 512],
                                               in0=ck32[:, dc * 512:(dc + 1) * 512],
                                               in1=rsb[:, 0:512]),
                 reads=["a3_rsb"], writes=["a3_ck32"])
            B.op("vector",
                 lambda e, dc=dc, t0=t0: e.tensor_scalar(
                     out=ckvn[:, dc * 1024 + t0:dc * 1024 + t0 + 512],
                     in0=ck32[:, dc * 512:(dc + 1) * 512],
                     scalar1=gam[:, dc:dc + 1], scalar2=None, op0=ALU.mult),
                 reads=["a3_ck32", "a3_gam"], writes=["a3_ckvn"])
    B.dma("scalar", kpe_out.ap(), kpe_sb[:, :], reads=["a3_kpe"], writes=["kpe"])
    kbsl = ctx.sb("a3_kbsl", [128, 4 * 2048], BF16)
    B.dma("gpsimd", kbsl[:].rearrange("p (a n) -> p a n", a=4),
          io["wkvbTk"].ap().rearrange("(a p) n -> p a n", p=128), reads=[],
          writes=["a3_kbsl"])
    knv = kn_out.ap().rearrange("(a p) t -> p a t", p=128)
    kn_sb = ctx.sb("a3_kn", [128, 16 * 1024], BF16)
    for d in range(16):
        for tc in range(2):
            t0 = tc * 512
            for a in range(4):
                B.op("tensor",
                     lambda e, d=d, a=a, t0=t0: e.matmul(
                         pq[:, 0:512], kbsl[:, a * 2048 + d * 128:a * 2048 + (d + 1) * 128],
                         ckvn[:, a * 1024 + t0:a * 1024 + t0 + 512],
                         start=(a == 0), stop=(a == 3)),
                     reads=["a3_kbsl", "a3_ckvn"], writes=["psA"])
            B.op("vector",
                 lambda e, d=d, t0=t0: e.tensor_copy(
                     out=kn_sb[:, d * 1024 + t0:d * 1024 + t0 + 512],
                     in_=pq[:, 0:512]),
                 reads=["psA"], writes=["a3_kn"])
    for d in range(16):
        B.dma("scalar", knv[:, d, :], kn_sb[:, d * 1024:(d + 1) * 1024],
              reads=["a3_kn"], writes=["knope"])
    B.dma("gpsimd", kbsl[:].rearrange("p (a n) -> p a n", a=4),
          io["wkvbTv"].ap().rearrange("(a p) n -> p a n", p=128),
          reads=["a3_kn"], writes=["a3_kbsl"])
    vtv = vt_out.ap().rearrange("(s p) d -> p s d", p=128)
    tv = ctx.sb("a3_tv", [128, 512], BF16)
    for ts in range(8):
        tc, sub = ts // 4, ts % 4
        for vs in range(4):
            for a in range(4):
                B.op("tensor",
                     lambda e, ts=ts, tc=tc, sub=sub, vs=vs, a=a: e.matmul(
                         pq[:, 0:512],
                         ckvn[:, a * 1024 + tc * 512 + sub * 128:
                              a * 1024 + tc * 512 + sub * 128 + 128],
                         kbsl[:, a * 2048 + vs * 512:a * 2048 + (vs + 1) * 512],
                         start=(a == 0), stop=(a == 3)),
                     reads=["a3_kbsl", "a3_ckvn"], writes=["psA"])
            B.op("vector", lambda e: e.tensor_copy(out=tv[:, 0:512],
                                                   in_=pq[:, 0:512]),
                 reads=["psA"], writes=["a3_tv"])
            B.dma("scalar", vtv[:, ts, vs * 512:(vs + 1) * 512], tv[:, 0:512],
                  reads=["a3_tv"], writes=["v_tok"])
    ksq = ctx.sb("a3_ksq", [128, 512], F32)
    kpsq = ctx.sb("a3_kpsq", [1, 1024], F32)
    kmx = ctx.sb("a3_kmx", [1, 32], F32)
    kmo = ctx.sb("a3_kmo", [1, 16], F32)
    pk = ctx.ps("psC")
    onec = ctx.onescol_f32()
    for tc in range(2):
        B.op("scalar",
             lambda e, tc=tc: e.activation(out=ksq[0:64, 0:512],
                                           in_=kpe_sb[:, tc * 512:(tc + 1) * 512],
                                           func=AF.Square),
             reads=["a3_kpe"], writes=["a3_ksq"])
        B.op("tensor",
             lambda e: e.matmul(pk[0:1, 0:512], onec[0:64, :], ksq[0:64, 0:512],
                                start=True, stop=True),
             reads=["a3_ksq", "c_ones_col32"], writes=["psC"])
        B.op("vector",
             lambda e, tc=tc: e.tensor_copy(out=kpsq[:, tc * 512:(tc + 1) * 512],
                                            in_=pk[0:1, 0:512]),
             reads=["psC"], writes=["a3_kpsq"])
    for h in range(16):
        for tc in range(2):
            B.op("scalar",
                 lambda e, h=h, tc=tc: e.activation(
                     out=ksq[:, 0:512],
                     in_=kn_sb[:, h * 1024 + tc * 512:h * 1024 + tc * 512 + 512],
                     func=AF.Square),
                 reads=["a3_kn"], writes=["a3_ksq"])
            B.op("tensor",
                 lambda e: e.matmul(pk[0:1, 0:512], onec[:, :], ksq[:, 0:512],
                                    start=True, stop=True),
                 reads=["a3_ksq", "c_ones_col32"], writes=["psC"])
            kk = ctx.sb("a3_kk", [1, 512], F32)
            B.op("vector",
                 lambda e, tc=tc: e.tensor_add(out=kk[:, 0:512],
                                               in0=pk[0:1, 0:512],
                                               in1=kpsq[:, tc * 512:(tc + 1) * 512]),
                 reads=["psC", "a3_kpsq"], writes=["a3_kk"])
            B.op("vector",
                 lambda e, h=h, tc=tc: e.reduce_max(
                     out=kmx[0:1, h * 2 + tc:h * 2 + tc + 1], in_=kk[:, 0:512],
                     axis=AX.X),
                 reads=["a3_kk"], writes=["a3_kmx"])
        B.op("vector",
             lambda e, h=h: e.tensor_max(out=kmo[0:1, h:h + 1],
                                         in0=kmx[0:1, h * 2:h * 2 + 1],
                                         in1=kmx[0:1, h * 2 + 1:h * 2 + 2]),
             reads=["a3_kmx"], writes=["a3_kmo"])
    B.dma("scalar", km_out.ap(), kmo[:, :], reads=["a3_kmo"], writes=["kmax"])
    return B


def build_a3b(B: Builder, io=None):
    ctx = Ctx(B)
    if io is None:
        io = {}
        io["x_in"] = B.dram("x_in", [D_MODEL, TOK_PER_CORE], F32,
                            kind="ExternalInput")
        io["x_out"] = B.dram("x_out", [D_MODEL, TOK_PER_CORE], F32,
                             kind="ExternalOutput")
        io["qn"] = B.dram("qn", [2048, TOK_PER_CORE], BF16, kind="ExternalInput")
        io["qpe"] = B.dram("qpe", [64, 16 * TOK_PER_CORE], BF16,
                           kind="ExternalInput")
        io["knope"] = B.dram("knope", [2048, T_SEQ], BF16, kind="ExternalInput")
        io["kpe"] = B.dram("kpe", [64, T_SEQ], BF16, kind="ExternalInput")
        io["v_tok"] = B.dram("v_tok", [T_SEQ, 2048], BF16, kind="ExternalInput")
        io["woT"] = B.dram("woT", [2048, 2048], F32, kind="ExternalInput")
        io["ab"] = B.dram("ab", [128, 32], F32, kind="ExternalInput")
        io["diag"] = B.dram("diag", [128, 4 * 512], F32, kind="ExternalInput")
        io["identbf"] = B.dram("identbf_d", [128, 128], F32,
                               kind="ExternalInput")
        for i in range(4):
            io[f"kmax{i}"] = B.dram(f"kmax{i}", [1, 16], F32,
                                    kind="ExternalInput").ap()
    ctx.io = io
    NH = 16
    idbf = ctx.sb("f_idbf", [128, 128], BF16)
    B.dma("gpsimd", idbf[:, :], io["identbf"].ap(), reads=[], writes=["f_idbf"])
    ab = ctx.sb("f_ab", [128, 32], F32)
    B.dma("sync", ab[:, :], io["ab"].ap(), reads=[], writes=["f_ab"])
    diag = ctx.sb("f_diag", [128, 4 * 512], BF16)
    B.dma("gpsimd", diag[:].rearrange("p (k t) -> p k t", k=4),
          io["diag"].ap().rearrange("p (k t) -> p k t", k=4), reads=[],
          writes=["f_diag"])
    km = ctx.sb("f_km", [1, 16], F32)
    kmt = ctx.sb("f_kmt", [1, 16], F32)
    B.dma("sync", km[:, :], io["kmax0"], reads=["d_kv3_ag"], writes=["f_km"])
    for i in range(1, 4):
        B.dma("sync", kmt[:, :], io[f"kmax{i}"], reads=["d_kv3_ag"],
              writes=["f_kmt"])
        B.op("vector", lambda e: e.tensor_max(out=km[:, :], in0=km[:, :],
                                              in1=kmt[:, :]),
             reads=["f_kmt"], writes=["f_km"])
    qsb = ctx.sb("f_q", [128, 16 * 512], BF16)
    qpe_sb = ctx.sb("f_qpe", [64, 16 * 512], BF16)
    attn = ctx.sb("f_attn", [128, 16 * 1024], BF16)
    kch = ctx.sb("f_kch", [128, 16 * 512], BF16)
    kpch = ctx.sb("f_kpch", [64, 512], BF16)
    vch = ctx.sb("f_vch", [128, 4 * 2048], BF16)
    vaug = ctx.sb("f_vaug", [128, 4 * 129], BF16)
    for ks in range(4):
        B.op("vector", lambda e, ks=ks: e.memset(
                 vaug[:, ks * 129 + 128:ks * 129 + 129], 1.0),
             writes=["f_vaug"])
    outsb = ctx.sb("f_out", [128, 64 * 129], F32)
    qsq = ctx.sb("f_qsq", [128, 512], F32)
    qmx = ctx.sb("f_qmx", [1, 16], F32)
    bsq = ctx.sb("f_bsq", [1, 16], F32)
    nbc = ctx.sb("f_nbc", [128, 16], F32)
    pt = ctx.sb("f_pt", [128, 512], BF16)
    msk = ctx.sb("f_msk", [128, 4 * 512], BF16)
    stg = ctx.sb("f_stg", [128, 128], BF16)
    psS = ctx.ps("psA")
    psC = ctx.ps("psC")
    psCb = ctx.ps("psCb", dtype=BF16)
    psO = [ctx.ps(f"psO{i}", (128, 512)) for i in range(4)]
    onec = ctx.onescol_f32()
    onesr = ctx.onesrow_f32()
    kn_v = io["knope"].ap().rearrange("(a p) t -> p a t", p=128)
    vt_v = io["v_tok"].ap().rearrange("(s p) d -> p s d", p=128)
    for qb in range(2):
        q0 = qb * 512
        B.dma("sync", qsb[:].rearrange("p (a t) -> p a t", a=16),
              io["qn"].ap().rearrange("(a p) t -> p a t", p=128)[:, :, q0:q0 + 512],
              reads=["d_qn"], writes=["f_q"])
        B.dma("sync", qpe_sb[:].rearrange("p (a t) -> p a t", a=16),
              io["qpe"].ap().rearrange("p (a t) -> p a t", a=16)[:, :, q0:q0 + 512],
              reads=["d_qpe"], writes=["f_qpe"])
        for a in range(NH):
            B.op("scalar",
                 lambda e, a=a: e.activation(
                     out=qsq[:, 0:512], in_=qsb[:, a * 512:(a + 1) * 512],
                     func=AF.Square),
                 reads=["f_q"], writes=["f_qsq"])
            B.op("tensor",
                 lambda e: e.matmul(psC[0:1, 0:512], onec[:, :], qsq[:, 0:512],
                                    start=True, stop=False),
                 reads=["f_qsq", "c_ones_col32"], writes=["psC"])
            B.op("scalar",
                 lambda e, a=a: e.activation(
                     out=qsq[0:64, 0:512],
                     in_=qpe_sb[:, a * 512:(a + 1) * 512],
                     func=AF.Square),
                 reads=["f_qpe"], writes=["f_qsq"])
            B.op("tensor",
                 lambda e: e.matmul(psC[0:1, 0:512], onec[0:64, :], qsq[0:64, 0:512],
                                    start=False, stop=True),
                 reads=["f_qsq", "c_ones_col32"], writes=["psC"])
            B.op("vector",
                 lambda e, a=a: e.reduce_max(out=qmx[0:1, a:a + 1],
                                             in_=psC[0:1, 0:512], axis=AX.X),
                 reads=["psC"], writes=["f_qmx"])
            B.op("vector",
                 lambda e, a=a: e.tensor_mul(out=bsq[0:1, a:a + 1],
                                             in0=qmx[0:1, a:a + 1],
                                             in1=km[0:1, a:a + 1]),
                 reads=["f_qmx", "f_km"], writes=["f_bsq"])
        B.op("scalar", lambda e: e.activation(out=bsq[0:1, 0:16], in_=bsq[0:1, 0:16],
                                              func=AF.Ln),
             reads=["f_bsq"], writes=["f_bsq"])
        B.op("scalar", lambda e: e.activation(out=bsq[0:1, 0:16], in_=bsq[0:1, 0:16],
                                              func=AF.Exp, scale=0.5),
             reads=["f_bsq"], writes=["f_bsq"])
        B.op("scalar", lambda e: e.activation(out=bsq[0:1, 0:16], in_=bsq[0:1, 0:16],
                                              func=AF.Copy, scale=-1.0),
             reads=["f_bsq"], writes=["f_bsq"])
        B.op("tensor", lambda e: e.matmul(psC[:, 0:16], onesr[:, :], bsq[0:1, 0:16],
                                          start=True, stop=True),
             reads=["f_bsq", "c_ones_row32"], writes=["psC"])
        B.op("vector", lambda e: e.tensor_copy(out=nbc[:, 0:16], in_=psC[:, 0:16]),
             reads=["psC"], writes=["f_nbc"])
        first = True
        for c in range(8):
            B.dma("sync", kch[:].rearrange("p (a t) -> p a t", a=16),
                  kn_v[:, :, c * 512:(c + 1) * 512], reads=["d_kv3_ag"],
                  writes=["f_kch"])
            B.dma("sync", kpch[:, :], io["kpe"].ap()[:, c * 512:(c + 1) * 512],
                  reads=["d_kv3_ag"], writes=["f_kpch"])
            B.dma("scalar", vch[:].rearrange("p (s d) -> p s d", s=4),
                  vt_v[:, c * 4:c * 4 + 4, :], reads=["d_kv3_ag"],
                  writes=["f_vch"])
            abidx = qb * 16 + c * 2
            for ks in range(4):
                B.op("vector",
                     lambda e, ks=ks, abidx=abidx: e.tensor_scalar(
                         out=msk[:, ks * 512:(ks + 1) * 512],
                         in0=diag[:, ks * 512:(ks + 1) * 512],
                         scalar1=ab[:, abidx + 1:abidx + 2],
                         scalar2=ab[:, abidx:abidx + 1],
                         op0=ALU.mult, op1=ALU.add),
                     reads=["f_diag", "f_ab"], writes=["f_msk"])
            for a in range(NH):
                for ks in range(4):
                    B.op("tensor",
                         lambda e, a=a, ks=ks: e.matmul(
                             psS[:, 0:512],
                             kch[:, a * 512 + ks * 128:a * 512 + ks * 128 + 128],
                             qsb[:, a * 512:(a + 1) * 512],
                             start=True, stop=False),
                         reads=["f_kch", "f_q"], writes=["psA"])
                    B.op("tensor",
                         lambda e, a=a, ks=ks: e.matmul(
                             psS[:, 0:512],
                             kpch[:, ks * 128:(ks + 1) * 128],
                             qpe_sb[:, a * 512:(a + 1) * 512],
                             start=False, stop=True),
                         reads=["f_kpch", "f_qpe"], writes=["psA"])
                    B.op("scalar",
                         lambda e, a=a: e.activation(
                             out=pt[:, 0:512], in_=psS[:, 0:512], func=AF.Exp,
                             bias=nbc[:, a:a + 1]),
                         reads=["psA", "f_nbc"], writes=["f_pt"])
                    B.op("vector",
                         lambda e, ks=ks: e.tensor_mul(
                             out=pt[:, 0:512], in0=pt[:, 0:512],
                             in1=msk[:, ks * 512:(ks + 1) * 512]),
                         reads=["f_msk"], writes=["f_pt"])
                    B.op("vector",
                         lambda e, a=a, ks=ks: e.tensor_copy(
                             out=vaug[:, ks * 129:ks * 129 + 128],
                             in_=vch[:, ks * 512 + 0:ks * 512 + 0]
                             if False else
                             vch[:].rearrange("p (s d) -> p s d", s=4)[:, ks,
                                                                      a * 128:
                                                                      (a + 1) * 128]),
                         reads=["f_vch"], writes=["f_vaug"])
                    for qs in range(4):
                        B.op("tensor",
                             lambda e, ks=ks, qs=qs: e.matmul(
                                 psO[qs][:, 0:129],
                                 pt[:, qs * 128:(qs + 1) * 128],
                                 vaug[:, ks * 129:(ks + 1) * 129],
                                 start=(ks == 0), stop=(ks == 3)),
                             reads=["f_pt", "f_vaug"], writes=[f"psO{qs}"])
                for qs in range(4):
                    oidx = (a * 4 + qs) * 129
                    if first:
                        B.op("vector",
                             lambda e, qs=qs, oidx=oidx: e.tensor_copy(
                                 out=outsb[:, oidx:oidx + 129],
                                 in_=psO[qs][:, 0:129]),
                             reads=[f"psO{qs}"], writes=["f_out"])
                    else:
                        B.op("vector",
                             lambda e, qs=qs, oidx=oidx: e.tensor_add(
                                 out=outsb[:, oidx:oidx + 129],
                                 in0=outsb[:, oidx:oidx + 129],
                                 in1=psO[qs][:, 0:129]),
                             reads=[f"psO{qs}"], writes=["f_out"])
            first = False
        rcol = ctx.sb("f_rcol", [128, 1], F32)
        for a in range(NH):
            for qs in range(4):
                oidx = (a * 4 + qs) * 129
                B.op("vector",
                     lambda e, oidx=oidx: e.reciprocal(
                         out=rcol[:, :], in_=outsb[:, oidx + 128:oidx + 129]),
                     reads=["f_out"], writes=["f_rcol"])
                B.op("vector",
                     lambda e, oidx=oidx: e.tensor_scalar(
                         out=stg[:, 0:128], in0=outsb[:, oidx:oidx + 128],
                         scalar1=rcol[:, :], scalar2=None, op0=ALU.mult),
                     reads=["f_out", "f_rcol"], writes=["f_stg"])
                B.op("tensor",
                     lambda e: e.transpose(psCb[:, 0:128], stg[:, 0:128],
                                           idbf[:, :]),
                     reads=["f_stg", "f_idbf"], writes=["psCb"])
                B.op("vector",
                     lambda e, a=a, qs=qs, q0=q0: e.tensor_copy(
                         out=attn[:, a * 1024 + q0 + qs * 128:
                                  a * 1024 + q0 + (qs + 1) * 128],
                         in_=psCb[:, 0:128]),
                     reads=["psCb"], writes=["f_attn"])
    x_sb = ctx.sb("f_x", [128, 16 * 512], F32)
    wsl = ctx.sb("f_wsl", [128, 16 * 512], BF16)
    wo_v = io["woT"].ap().rearrange("(a p) n -> p a n", p=128)
    xv = io["x_in"].ap().rearrange("(a p) t -> p a t", p=128)
    xov = io["x_out"].ap().rearrange("(a p) t -> p a t", p=128)
    psW = ctx.ps("psD")
    for chunk in range(2):
        t0 = chunk * 512
        B.dma("sync", x_sb[:].rearrange("p (a t) -> p a t", a=16),
              xv[:, :, t0:t0 + 512], reads=["x_in"], writes=["f_x"])
        for s in range(4):
            B.dma("gpsimd", wsl[:].rearrange("p (a n) -> p a n", a=16),
                  wo_v[:, :, s * 512:(s + 1) * 512], reads=["woT"], writes=["f_wsl"])
            for dc in range(4):
                d = s * 4 + dc
                for a in range(16):
                    B.op("tensor",
                         lambda e, s=s, dc=dc, a=a, t0=t0: e.matmul(
                             psW[:, 0:512],
                             wsl[:, a * 512 + dc * 128:a * 512 + (dc + 1) * 128],
                             attn[:, a * 1024 + t0:a * 1024 + t0 + 512],
                             start=(a == 0), stop=(a == 15)),
                         reads=["f_wsl", "f_attn"], writes=["psD"])
                B.op("vector",
                     lambda e, d=d: e.tensor_add(out=x_sb[:, d * 512:(d + 1) * 512],
                                                 in0=x_sb[:, d * 512:(d + 1) * 512],
                                                 in1=psW[:, 0:512]),
                     reads=["psD"], writes=["f_x"])
                B.dma("scalar", xov[:, d, t0:t0 + 512], x_sb[:, d * 512:(d + 1) * 512],
                      reads=["f_x"], writes=["x_out"])
    return B


def host_a3_inputs(inputs, core):
    Wq = np.asarray(inputs["L1_attn_a_Wq"], np.float32)
    NH, DQ, DN, DR = 16, 192, 128, 64
    perm = np.empty(3072, np.int64)
    for h in range(NH):
        perm[h * DN:(h + 1) * DN] = np.arange(h * DQ, h * DQ + DN)
        perm[2048 + h * DR:2048 + (h + 1) * DR] = np.arange(h * DQ + DN,
                                                            h * DQ + DQ)
    WqP = Wq[perm]
    Wkva = np.asarray(inputs["L1_attn_a_Wkva"], np.float32)
    Wkvb = np.asarray(inputs["L1_attn_a_Wkvb"], np.float32)
    kperm = np.empty(2048, np.int64)
    vperm = np.empty(2048, np.int64)
    for h in range(NH):
        kperm[h * 128:(h + 1) * 128] = np.arange(h * 256, h * 256 + 128)
        vperm[h * 128:(h + 1) * 128] = np.arange(h * 256 + 128, h * 256 + 256)
    Wo = np.asarray(inputs["L1_attn_a_Wo"], np.float32)
    cos = np.asarray(inputs["L1_attn_a_cos"], np.float32).reshape(64)
    sin = np.asarray(inputs["L1_attn_a_sin"], np.float32).reshape(64)
    cos2 = np.stack([np.concatenate([cos, cos]),
                     np.concatenate([sin, sin])], axis=1)
    csraw = np.stack([cos, sin], axis=1)
    blks = core_blocks(core)
    ab = np.zeros(32, np.float32)
    for qi, qpos in enumerate(blks):
        for c in range(8):
            ab[qi * 16 + c * 2] = 1.0 if c < qpos else 0.0
            ab[qi * 16 + c * 2 + 1] = 1.0 if c == qpos else 0.0
    ab_rep = np.ascontiguousarray(np.broadcast_to(ab[None, :], (128, 32)))
    dg = np.zeros((128, 4 * 512), np.float32)
    for ks in range(4):
        kk = np.arange(128)[:, None] + ks * 128
        qq = np.arange(512)[None, :]
        dg[:, ks * 512:(ks + 1) * 512] = (kk <= qq).astype(np.float32)
    return {
        "wqTp": np.ascontiguousarray(WqP.T),
        "wkvaT": np.ascontiguousarray(Wkva.T),
        "wkvbTk": np.ascontiguousarray(Wkvb[kperm].T),
        "wkvbTv": np.ascontiguousarray(Wkvb[vperm].T),
        "gam": col_layout(inputs["L1_attn_a_kv_a_norm_weight"]),
        "cos2": np.ascontiguousarray(cos2),
        "csraw": np.ascontiguousarray(csraw),
        "woT": np.ascontiguousarray(Wo.T),
        "ab": ab_rep,
        "diag": dg,
        "identbf_d": np.eye(128, dtype=np.float32),
    }



def _run(build_fn, maps, cores=None):
    nc = bass.Bass("TRN2", target_bir_lowering=False)
    Bld = Builder(nc)
    build_fn(Bld)
    Bld.finalize()
    return Bld.run(maps, core_ids=cores or list(range(N_CORES)))


def kernel_unfused(**inputs):
    x0 = np.asarray(inputs["L0_attn_a_x"], np.float32)
    shards = shard_tokens(x0)
    x_fm = [np.ascontiguousarray(s.T) for s in shards]

    wres = _run(build_wprep, [host_big_weights(inputs)], cores=[0])
    wbf = {k: wres.results[0][k] for k in
           ("w1T_bf", "w3T_bf", "w2T_bf", "b1T_bf", "b3T_bf", "a2T_bf",
            "wupT_bf", "wgateT_bf", "wdownT_bf")}

    h_a1 = [host_a1_inputs(inputs, c) for c in range(8)]
    h_a2 = host_a2_inputs(inputs)
    h_a3 = [host_a3_inputs(inputs, c) for c in range(8)]

    ka = ("wqT", "wkvT", "bq", "bkv", "peT", "identbf_d")
    r = _run(build_a1a, [dict({"x_in": x_fm[c]}, **{k: h_a1[c][k] for k in ka})
                         for c in range(8)])
    kfull, vfull = {}, {}
    for b in range(2):
        kf = np.empty((512, T_SEQ), dtype=r.results[0]["k_fm"].dtype)
        vf = np.empty((T_SEQ, 512), dtype=r.results[0]["v_tok"].dtype)
        for j in range(4):
            c = b * 4 + j
            for i, blk in enumerate(core_blocks(c)):
                kf[:, blk * 512:(blk + 1) * 512] = \
                    r.results[c]["k_fm"][:, i * 512:(i + 1) * 512]
                vf[blk * 512:(blk + 1) * 512] = \
                    r.results[c]["v_tok"][i * 512:(i + 1) * 512]
        kfull[b], vfull[b] = np.ascontiguousarray(kf), np.ascontiguousarray(vf)
    maps = []
    for c in range(8):
        b = c // 4
        m = {"x_in": x_fm[c], "qp": r.results[c]["qp"], "k_fm": kfull[b],
             "v_tok": vfull[b], "woT": h_a1[c]["woT"], "ab": h_a1[c]["ab"],
             "diag": h_a1[c]["diag"], "identbf_d": h_a1[c]["identbf_d"]}
        for i in range(4):
            m[f"kmax{i}"] = r.results[b * 4 + i]["kmax"]
        maps.append(m)
    r = _run(build_a1b, maps)
    x_fm = [r.results[c]["x_out"] for c in range(8)]

    r = _run(build_f0, [host_f0_inputs(inputs, wbf, x_fm[c]) for c in range(8)])
    x_fm = [r.results[c]["x_out"] for c in range(8)]

    r = _run(build_a2a, [{"x_in": x_fm[c], "wqT": h_a2["wqT"],
                          "wkvT": h_a2["wkvT"]} for c in range(8)])
    maps = []
    for c in range(8):
        grp = (c // 4) * 4
        m = {"x_in": x_fm[c], "q_fm": r.results[c]["q_fm"],
             "wprojT": h_a2["wprojT"], "bproj": h_a2["bproj"]}
        for i in range(4):
            m[f"vkk{i}"] = r.results[grp + i]["vkk"]
        maps.append(m)
    r = _run(build_a2b, maps)
    x_fm = [r.results[c]["x_out"] for c in range(8)]

    r = _run(build_f0, [host_f0_inputs(inputs, wbf, x_fm[c]) for c in range(8)])
    x_fm = [r.results[c]["x_out"] for c in range(8)]

    ka3 = ("wqTp", "wkvaT", "wkvbTk", "wkvbTv", "gam", "cos2", "csraw")
    r = _run(build_a3a, [dict({"x_in": x_fm[c]}, **{k: h_a3[c][k] for k in ka3})
                         for c in range(8)])
    knf, kpf, vtf = {}, {}, {}
    for b in range(2):
        kn = np.empty((2048, T_SEQ), dtype=r.results[0]["knope"].dtype)
        kp = np.empty((64, T_SEQ), dtype=r.results[0]["kpe"].dtype)
        vt = np.empty((T_SEQ, 2048), dtype=r.results[0]["v_tok"].dtype)
        for j in range(4):
            c = b * 4 + j
            for i, blk in enumerate(core_blocks(c)):
                kn[:, blk * 512:(blk + 1) * 512] = \
                    r.results[c]["knope"][:, i * 512:(i + 1) * 512]
                kp[:, blk * 512:(blk + 1) * 512] = \
                    r.results[c]["kpe"][:, i * 512:(i + 1) * 512]
                vt[blk * 512:(blk + 1) * 512] = \
                    r.results[c]["v_tok"][i * 512:(i + 1) * 512]
        knf[b], kpf[b], vtf[b] = map(np.ascontiguousarray, (kn, kp, vt))
    maps = []
    for c in range(8):
        b = c // 4
        m = {"x_in": x_fm[c], "qn": r.results[c]["qn"], "qpe": r.results[c]["qpe"],
             "knope": knf[b], "kpe": kpf[b], "v_tok": vtf[b],
             "woT": h_a3[c]["woT"], "ab": h_a3[c]["ab"], "diag": h_a3[c]["diag"],
             "identbf_d": h_a3[c]["identbf_d"]}
        for i in range(4):
            m[f"kmax{i}"] = r.results[b * 4 + i]["kmax"]
        maps.append(m)
    r = _run(build_a3b, maps)
    x_fm = [r.results[c]["x_out"] for c in range(8)]

    r = _run(build_f1, [host_f1_inputs(inputs, wbf, x_fm[c]) for c in range(8)])
    x_fm = [r.results[c]["x_out"] for c in range(8)]

    xpT = np.ascontiguousarray(np.asarray(inputs["L1_attn_b_x_proj_w"],
                                          np.float32).T)
    r = _run(build_a4a, [{"x_in": x_fm[c], "xprojT": xpT} for c in range(8)])
    xln_shards = [r.results[c]["xln"] for c in range(8)]
    xdbl_shards = [r.results[c]["xdbl"] for c in range(8)]
    r = _run(build_a4b, [host_a4b_inputs(inputs, xln_shards, xdbl_shards, c)
                         for c in range(8)])
    y_slices = [r.results[c]["y"] for c in range(8)]
    r = _run(build_a4c, [{"x_in": x_fm[c],
                          "y_in": host_a4_reshard_y(y_slices, c)}
                         for c in range(8)])
    x_fm = [r.results[c]["x_out"] for c in range(8)]

    r = _run(build_f1, [host_f1_inputs(inputs, wbf, x_fm[c]) for c in range(8)])
    x_fm = [r.results[c]["x_out"] for c in range(8)]

    out = unshard_tokens([np.ascontiguousarray(np.asarray(x).T) for x in x_fm])
    return np.ascontiguousarray(out, dtype=np.float32)



I32 = mybir.dt.int32


def build_a4y(B: Builder, io):
    ctx = Ctx(B)
    yv = io["y_loc"].ap()
    ytv = io["y_T"].ap().rearrange("(c p) d -> p c d", p=128)
    ysb = ctx.sb("y_ysb", [128, T_SEQ], BF16)
    stage = ctx.sb("y_stage", [128, 32 * 128], BF16)
    idbf = ctx.sb("y_idbf", [128, 128], BF16)
    B.dma("gpsimd", idbf[:, :], io["identbf"].ap(), reads=[], writes=["y_idbf"])
    pst = ctx.ps("psA", dtype=BF16)
    for g in range(4):
        B.dma("gpsimd", ysb[:, :], yv[g * 128:(g + 1) * 128, :], reads=[],
              writes=["y_ysb"])
        for cchunk in range(32):
            B.op("tensor",
                 lambda e, cchunk=cchunk: e.transpose(
                     pst[:, 0:128], ysb[:, cchunk * 128:(cchunk + 1) * 128],
                     idbf[:, :]),
                 reads=["y_ysb", "y_idbf"], writes=["psA"])
            B.op("vector",
                 lambda e, cchunk=cchunk: e.tensor_copy(
                     out=stage[:, cchunk * 128:(cchunk + 1) * 128],
                     in_=pst[:, 0:128]),
                 reads=["psA"], writes=["y_stage"])
        B.dma("scalar", ytv[:, :, g * 128:(g + 1) * 128],
              stage[:].rearrange("p (c d) -> p c d", c=32),
              reads=["y_stage"], writes=["y_T"])


def build_a4gather_x(B: Builder, io):
    ctx = Ctx(B)
    gidx = ctx.sb("gx_idx", [128, 16], I32)
    B.dma("gpsimd", gidx[:, :], io["a4_gidx"].ap(), reads=[], writes=["gx_idx"])
    tile = ctx.sb("gx_tile", [128, 1024], BF16)
    agv = io["xlnag"].ap()
    dst = io["xsl_pos"].ap()
    for r in range(4):
        blks = core_blocks(r)
        for sub in range(4):
            B.op("gpsimd",
                 lambda e, r=r, sub=sub: e.indirect_dma_start(
                     tile[:, :], None, agv,
                     IndirectOffsetOnAxis(gidx[:, r * 4 + sub:r * 4 + sub + 1],
                                          0)),
                 reads=["gx_idx", "cc_out"], writes=["gx_tile"], kind="dma")
            for slot in range(2):
                b = blks[slot]
                B.dma("sync", dst[sub * 128:(sub + 1) * 128,
                                  b * 512:(b + 1) * 512],
                      tile[:, slot * 512:(slot + 1) * 512],
                      reads=["gx_tile"], writes=["gx_tile"])
    for i in range(4):
        lo, hi = core_blocks(i)
        B.dma("scalar", io["dt_pos"].ap()[:, lo * 512:(lo + 1) * 512],
              io["xdblag"].ap()[i * 160:i * 160 + 128, 0:512],
              reads=["cc_out"], writes=["dtp"])
        B.dma("scalar", io["dt_pos"].ap()[:, hi * 512:(hi + 1) * 512],
              io["xdblag"].ap()[i * 160:i * 160 + 128, 512:1024],
              reads=["cc_out"], writes=["dtp"])
        B.dma("gpsimd", io["bc_pos"].ap()[:, lo * 512:(lo + 1) * 512],
              io["xdblag"].ap()[i * 160 + 128:i * 160 + 160, 0:512],
              reads=["cc_out"], writes=["bcp"])
        B.dma("gpsimd", io["bc_pos"].ap()[:, hi * 512:(hi + 1) * 512],
              io["xdblag"].ap()[i * 160 + 128:i * 160 + 160, 512:1024],
              reads=["cc_out"], writes=["bcp"])


def build_a4gather_y(B: Builder, io):
    ctx = Ctx(B)
    yidx = ctx.sb("gy_idx", [128, 32], I32)
    B.dma("gpsimd", yidx[:, :], io["a4_yidx"].ap(), reads=[], writes=["gy_idx"])
    tile = ctx.sb("gy_tile", [128, 512], BF16)
    agv = io["yTag"].ap()
    dst = io["y_finT"].ap()
    for r in range(4):
        for sub in range(8):
            B.op("gpsimd",
                 lambda e, r=r, sub=sub: e.indirect_dma_start(
                     tile[:, 0:512], None, agv,
                     IndirectOffsetOnAxis(yidx[:, r * 8 + sub:r * 8 + sub + 1], 0)),
                 reads=["gy_idx", "cc_out"], writes=["gy_tile"], kind="dma")
            B.dma("sync", dst[sub * 128:(sub + 1) * 128, r * 512:(r + 1) * 512],
                  tile[:, 0:512], reads=["gy_tile"], writes=["gy_tile"])


def build_a4c_fused(B: Builder, io):
    ctx = Ctx(B)
    x_sb = ctx.sb("c_x", [128, 16 * 512], F32)
    y_sb = ctx.sb("c_y", [128, 16 * 128], BF16)
    idbf = ctx.sb("c_idbf", [128, 128], BF16)
    B.dma("gpsimd", idbf[:, :], io["identbf"].ap(), reads=[], writes=["c_idbf"])
    pst = ctx.ps("psA", dtype=BF16)
    xv = io["x_in"].ap().rearrange("(a p) t -> p a t", p=128)
    ytv = io["y_finT"].ap().rearrange("(s p) d -> p s d", p=128)
    xov = io["x_out"].ap().rearrange("(a p) t -> p a t", p=128)
    for chunk in range(2):
        t0 = chunk * 512
        B.dma("sync", x_sb[:].rearrange("p (a t) -> p a t", a=16),
              xv[:, :, t0:t0 + 512], reads=["cfx_in"], writes=["c_x"])
        for sub in range(4):
            gs = chunk * 4 + sub
            B.dma("scalar", y_sb[:].rearrange("p (a d) -> p a d", a=16),
                  ytv[:, gs, :].rearrange("p (a d) -> p a d", a=16),
                  reads=["cfy"], writes=["c_y"])
            for a in range(16):
                B.op("tensor",
                     lambda e, a=a: e.transpose(pst[:, 0:128],
                                                y_sb[:, a * 128:(a + 1) * 128],
                                                idbf[:, :]),
                     reads=["c_y", "c_idbf"], writes=["psA"])
                B.op("vector",
                     lambda e, a=a, sub=sub: e.tensor_add(
                         out=x_sb[:, a * 512 + sub * 128:a * 512 + (sub + 1) * 128],
                         in0=x_sb[:, a * 512 + sub * 128:a * 512 + (sub + 1) * 128],
                         in1=pst[:, 0:128]),
                     reads=["psA"], writes=["c_x"])
        for a in range(16):
            B.dma("gpsimd", xov[:, a, t0:t0 + 512], x_sb[:, a * 512:(a + 1) * 512],
                  reads=["c_x"], writes=["cfx_out"])



def rank_of_block(b):
    return b if b < 4 else 7 - b


def slot_of_block(b):
    return 0 if b < 4 else 1


def build_fused(B: Builder, upto=99):
    nc = B.nc
    dr = B.dram
    io = {}
    x0 = dr("x0", [D_MODEL, TOK_PER_CORE], F32, kind="ExternalInput")
    xout = dr("x_final", [D_MODEL, TOK_PER_CORE], F32, kind="ExternalOutput")
    wio = {}
    for name, (din, dout) in BIG_WEIGHTS.items():
        wio[name] = dr(name, [din, dout], F32, kind="ExternalInput")
        wio[name + "_bf"] = dr("i_" + name + "_bf", [din, dout], BF16)
    ext = {}
    for nm, shp in (
        ("gateT_d", (2048, 8)), ("a1l_d", (2048, 128)), ("a3l_d", (2048, 128)),
        ("b2l_d", (16, 8 * 2048)), ("ident32_d", (128, 128)),
        ("sel32_d", (8, 8 * 128)),
        ("p_su_ug_d", (128, 16)), ("p_su_dp_d", (128, 64)),
        ("p_wsu_up_d", (128, 64)), ("p_wsu_gate_d", (128, 64)),
        ("p_wsdp_d", (128, 16)), ("c_h128a_d", (128, 128)),
        ("c_h128b_d", (128, 128)),
        ("a1_wqT", (2048, 2048)), ("a1_wkvT", (2048, 1024)),
        ("a1_bq", (128, 16)), ("a1_bkv", (128, 8)),
        ("a1_peT", (128, TOK_PER_CORE)), ("a1_woT", (2048, 2048)),
        ("ab_d", (128, 32)), ("diag_d", (128, 4 * 512)),
        ("identbf_d", (128, 128)),
        ("a2_wqT", (2048, 2048)), ("a2_wkvT", (2048, 4096)),
        ("a2_wprojT", (2048, 2048)), ("a2_bproj", (128, 16)),
        ("a3_wqTp", (2048, 3072)), ("a3_wkvaT", (2048, 576)),
        ("a3_wkvbTk", (512, 2048)), ("a3_wkvbTv", (512, 2048)),
        ("a3_gam", (128, 4)), ("a3_cos2", (128, 2)), ("a3_csraw", (64, 2)),
        ("a3_woT", (2048, 2048)),
        ("a4_xprojT", (2048, 160)), ("a4_dtpT", (128, 512)),
        ("a4_dtb", (128, 4)), ("a4_alog", (128, 64)), ("a4_dcol", (128, 4)),
    ):
        ext[nm] = dr(nm, list(shp), F32, kind="ExternalInput")
    xa = dr("i_xa", [D_MODEL, TOK_PER_CORE], F32)
    xb = dr("i_xb", [D_MODEL, TOK_PER_CORE], F32)
    qp = dr("i_qp", [2048, TOK_PER_CORE], BF16)
    a1_kfl = dr("i_a1_kfl", [512, TOK_PER_CORE], BF16)
    a1_vtl = dr("i_a1_vtl", [TOK_PER_CORE, 512], BF16)
    a1_kml = dr("i_a1_kml", [1, 4], F32)
    a1_kag = dr("i_a1_kag", [4 * 512, TOK_PER_CORE], BF16)
    a1_vag = dr("i_a1_vag", [4 * TOK_PER_CORE, 512], BF16)
    a1_kmag = dr("i_a1_kmag", [4, 4], F32)
    a1_kpos = dr("i_a1_kpos", [512, T_SEQ], BF16)
    a1_vpos = dr("i_a1_vpos", [T_SEQ, 512], BF16)
    qfm = dr("i_qfm", [2048, TOK_PER_CORE], BF16)
    vkkl = dr("i_vkkl", [32, 64 * 33], F32)
    vkkag = dr("i_vkkag", [128, 64 * 33], F32)
    qn = dr("i_qn", [2048, TOK_PER_CORE], BF16)
    qpe = dr("i_qpe", [64, 16 * TOK_PER_CORE], BF16)
    knl = dr("i_knl", [2048, TOK_PER_CORE], BF16)
    kpel = dr("i_kpel", [64, TOK_PER_CORE], BF16)
    vt3l = dr("i_vt3l", [TOK_PER_CORE, 2048], BF16)
    km3l = dr("i_km3l", [1, 16], F32)
    knag = dr("i_knag", [4 * 2048, TOK_PER_CORE], BF16)
    kpeag = dr("i_kpeag", [4 * 64, TOK_PER_CORE], BF16)
    vt3ag = dr("i_vt3ag", [4 * TOK_PER_CORE, 2048], BF16)
    km3ag = dr("i_km3ag", [4, 16], F32)
    knpos = dr("i_knpos", [2048, T_SEQ], BF16)
    kpepos = dr("i_kpepos", [64, T_SEQ], BF16)
    vt3pos = dr("i_vt3pos", [T_SEQ, 2048], BF16)
    xln = dr("i_xln", [D_MODEL, TOK_PER_CORE], BF16)
    xdbl = dr("i_xdbl", [160, TOK_PER_CORE], F32)
    xlnag = dr("i_xlnag", [4 * D_MODEL, TOK_PER_CORE], BF16)
    xsl_pos = dr("i_xsl_pos", [512, T_SEQ], BF16)
    xdblag = dr("i_xdblag", [4 * 160, TOK_PER_CORE], F32)
    dt_pos = dr("i_dt_pos", [128, T_SEQ], F32)
    bc_pos = dr("i_bc_pos", [32, T_SEQ], F32)
    y_loc = dr("i_y_loc", [512, T_SEQ], F32)
    y_T = dr("i_y_T", [T_SEQ, 512], BF16)
    yTag = dr("i_yTag", [4 * T_SEQ, 512], BF16)
    y_finT = dr("i_y_finT", [TOK_PER_CORE, 2048], BF16)
    ext_i = {}
    ext_i["a4_gidx"] = dr("a4_gidx", [128, 16], I32, kind="ExternalInput")
    ext_i["a4_yidx"] = dr("a4_yidx", [128, 32], I32, kind="ExternalInput")

    def phase(fn, pio):
        B.begin_phase()
        fn(B, pio)
        B.end_phase()

    def cc_phase(ops, dmas=()):
        B.begin_phase()
        for kind, op, in_ap, out_ap in ops:
            B.cc(kind, op, GROUPS4, [in_ap.opt()], [out_ap.opt()],
                 reads=[], writes=["cc_out"])
        for q, dst, srcp in dmas:
            B.dma(q, dst, srcp, reads=["cc_out"], writes=["cc_out"])
        B.end_phase()

    def finish_early(xcur):
        B.begin_phase()
        B.dma("sync", xout.ap(), xcur.ap(), reads=[], writes=["fin"])
        B.end_phase()

    if upto >= 1:
        pio = dict(wio)
        phase(build_wprep, pio)
    if upto == 1:
        finish_early(x0)
        return B

    pio = {"x_in": x0, "wqT": ext["a1_wqT"], "wkvT": ext["a1_wkvT"],
           "bq": ext["a1_bq"], "bkv": ext["a1_bkv"], "peT": ext["a1_peT"],
           "qp": qp, "k_fm_loc": a1_kfl, "v_tok_loc": a1_vtl,
           "kmax_loc": a1_kml, "identbf": ext["identbf_d"]}
    phase(build_a1a, pio)
    dmas = []
    for b in range(8):
        r, s = rank_of_block(b), slot_of_block(b)
        dmas.append(("sync",
                     a1_kpos.ap()[:, b * 512:(b + 1) * 512],
                     a1_kag.ap()[r * 512:(r + 1) * 512, s * 512:(s + 1) * 512]))
        dmas.append(("scalar",
                     a1_vpos.ap()[b * 512:(b + 1) * 512, :],
                     a1_vag.ap()[r * 1024 + s * 512:r * 1024 + s * 512 + 512, :]))
    cc_phase([("AllGather", ALU.bypass, a1_kfl.ap(), a1_kag.ap()),
              ("AllGather", ALU.bypass, a1_vtl.ap(), a1_vag.ap()),
              ("AllGather", ALU.bypass, a1_kml.ap(), a1_kmag.ap())], dmas)
    pio = {"x_in": x0, "x_out": xa, "qp": qp, "k_fm": a1_kpos, "v_tok": a1_vpos,
           "woT": ext["a1_woT"], "ab": ext["ab_d"], "diag": ext["diag_d"],
           "identbf": ext["identbf_d"]}
    for i in range(4):
        pio[f"kmax{i}"] = a1_kmag.ap()[i:i + 1, :]
    phase(build_a1b, pio)
    if upto == 2:
        finish_early(xa)
        return B

    pio = {"x_in": xa, "x_out": xb}
    for nm in ("w1T_bf", "w3T_bf", "w2T_bf", "b1T_bf", "b3T_bf", "a2T_bf"):
        pio[nm] = wio[nm]
    for nm in ("gateT_d", "a1l_d", "a3l_d", "b2l_d", "ident32_d", "sel32_d"):
        pio[nm] = ext[nm]
    phase(build_f0, pio)
    if upto == 3:
        finish_early(xb)
        return B

    pio = {"x_in": xb, "wqT": ext["a2_wqT"], "wkvT": ext["a2_wkvT"],
           "q_fm": qfm, "vkk": vkkl}
    phase(build_a2a, pio)
    cc_phase([("AllGather", ALU.bypass, vkkl.ap(), vkkag.ap())])
    pio = {"x_in": xb, "x_out": xa, "q_fm": qfm, "wprojT": ext["a2_wprojT"],
           "bproj": ext["a2_bproj"]}
    for i in range(4):
        pio[f"vkk{i}"] = vkkag.ap()[i * 32:(i + 1) * 32, :]
    phase(build_a2b, pio)
    if upto == 4:
        finish_early(xa)
        return B

    pio = {"x_in": xa, "x_out": xb}
    for nm in ("w1T_bf", "w3T_bf", "w2T_bf", "b1T_bf", "b3T_bf", "a2T_bf"):
        pio[nm] = wio[nm]
    for nm in ("gateT_d", "a1l_d", "a3l_d", "b2l_d", "ident32_d", "sel32_d"):
        pio[nm] = ext[nm]
    phase(build_f0, pio)
    if upto == 5:
        finish_early(xb)
        return B

    pio = {"x_in": xb, "wqT": ext["a3_wqTp"], "wkvaT": ext["a3_wkvaT"],
           "wkvbTk": ext["a3_wkvbTk"], "wkvbTv": ext["a3_wkvbTv"],
           "gam": ext["a3_gam"], "cos2": ext["a3_cos2"], "csraw": ext["a3_csraw"],
           "qn": qn, "qpe": qpe, "knope_loc": knl, "kpe_loc": kpel,
           "v_tok_loc": vt3l, "kmax_loc": km3l}
    phase(build_a3a, pio)
    dmas = []
    for b in range(8):
        r, s = rank_of_block(b), slot_of_block(b)
        dmas.append(("sync",
                     knpos.ap()[:, b * 512:(b + 1) * 512],
                     knag.ap()[r * 2048:(r + 1) * 2048, s * 512:(s + 1) * 512]))
        dmas.append(("scalar",
                     kpepos.ap()[:, b * 512:(b + 1) * 512],
                     kpeag.ap()[r * 64:(r + 1) * 64, s * 512:(s + 1) * 512]))
        dmas.append(("gpsimd",
                     vt3pos.ap()[b * 512:(b + 1) * 512, :],
                     vt3ag.ap()[r * 1024 + s * 512:r * 1024 + s * 512 + 512, :]))
    cc_phase([("AllGather", ALU.bypass, knl.ap(), knag.ap()),
              ("AllGather", ALU.bypass, kpel.ap(), kpeag.ap()),
              ("AllGather", ALU.bypass, vt3l.ap(), vt3ag.ap()),
              ("AllGather", ALU.bypass, km3l.ap(), km3ag.ap())], dmas)
    pio = {"x_in": xb, "x_out": xa, "qn": qn, "qpe": qpe, "knope": knpos,
           "kpe": kpepos, "v_tok": vt3pos, "woT": ext["a3_woT"],
           "ab": ext["ab_d"], "diag": ext["diag_d"], "identbf": ext["identbf_d"]}
    for i in range(4):
        pio[f"kmax{i}"] = km3ag.ap()[i:i + 1, :]
    phase(build_a3b, pio)
    if upto == 6:
        finish_early(xa)
        return B

    pio = {"x_in": xa, "x_out": xb, "wupT_bf": wio["wupT_bf"],
           "wgateT_bf": wio["wgateT_bf"], "wdownT_bf": wio["wdownT_bf"]}
    for nm in ("p_su_ug_d", "p_su_dp_d", "p_wsu_up_d", "p_wsu_gate_d",
               "p_wsdp_d", "c_h128a_d", "c_h128b_d"):
        pio[nm] = ext[nm]
    phase(build_f1, pio)
    if upto == 7:
        finish_early(xb)
        return B

    pio = {"x_in": xb, "xln": xln, "xdbl": xdbl, "xprojT": ext["a4_xprojT"]}
    phase(build_a4a, pio)
    cc_phase([("AllGather", ALU.bypass, xln.ap(), xlnag.ap()),
              ("AllGather", ALU.bypass, xdbl.ap(), xdblag.ap())])
    pio = {"a4_gidx": ext_i["a4_gidx"], "xlnag": xlnag, "xsl_pos": xsl_pos,
           "xdblag": xdblag, "dt_pos": dt_pos, "bc_pos": bc_pos}
    phase(build_a4gather_x, pio)
    pio = {"xsl": xsl_pos, "dt": dt_pos, "bc": bc_pos, "dtpT": ext["a4_dtpT"],
           "dtb": ext["a4_dtb"], "alog": ext["a4_alog"], "dcol": ext["a4_dcol"],
           "y": y_loc}
    phase(build_a4b, pio)
    pio = {"y_loc": y_loc, "y_T": y_T, "identbf": ext["identbf_d"]}
    phase(build_a4y, pio)
    cc_phase([("AllGather", ALU.bypass, y_T.ap(), yTag.ap())])
    pio = {"a4_yidx": ext_i["a4_yidx"], "yTag": yTag, "y_finT": y_finT}
    phase(build_a4gather_y, pio)
    pio = {"x_in": xb, "y_finT": y_finT, "x_out": xa, "identbf": ext["identbf_d"]}
    phase(build_a4c_fused, pio)
    if upto == 8:
        finish_early(xa)
        return B

    pio = {"x_in": xa, "x_out": xout, "wupT_bf": wio["wupT_bf"],
           "wgateT_bf": wio["wgateT_bf"], "wdownT_bf": wio["wdownT_bf"]}
    for nm in ("p_su_ug_d", "p_su_dp_d", "p_wsu_up_d", "p_wsu_gate_d",
               "p_wsdp_d", "c_h128a_d", "c_h128b_d"):
        pio[nm] = ext[nm]
    phase(build_f1, pio)
    return B


def host_fused_inputs(inputs, core):
    m = {}
    m.update(host_big_weights(inputs))
    h0 = host_f0_inputs(inputs, {k: None for k in ()}, None) if False else None
    a1 = np.asarray(inputs["L0_ffn_a1"], np.float32)
    a3 = np.asarray(inputs["L0_ffn_a3"], np.float32)
    b2 = np.asarray(inputs["L0_ffn_b2"], np.float32)
    sel = np.zeros((8, 8 * 128), np.float32)
    for ee in range(8):
        sel[ee, ee * 128:(ee + 1) * 128] = 1.0
    m["gateT_d"] = np.ascontiguousarray(np.asarray(inputs["L0_ffn_gate_w"],
                                                   np.float32).T)
    m["a1l_d"] = np.ascontiguousarray(a1.transpose(2, 0, 1).reshape(2048, 128))
    m["a3l_d"] = np.ascontiguousarray(a3.transpose(2, 0, 1).reshape(2048, 128))
    m["b2l_d"] = np.ascontiguousarray(b2.transpose(2, 0, 1).reshape(16, 8 * 2048))
    m["ident32_d"] = np.eye(128, dtype=np.float32)
    m["sel32_d"] = sel
    I = 8192
    ws = np.asarray(inputs["L1_ffn_Wscale_ug"], np.float32)
    m["p_su_ug_d"] = col_layout(inputs["L1_ffn_SU_ug"])
    m["p_su_dp_d"] = col_layout(inputs["L1_ffn_SU_dp"])
    m["p_wsu_up_d"] = col_layout(ws[:I])
    m["p_wsu_gate_d"] = col_layout(ws[I:])
    m["p_wsdp_d"] = col_layout(inputs["L1_ffn_Wscale_dp"])
    m["c_h128a_d"] = hadamard(128) * np.float32(1.0 / (math.sqrt(2048) * 64.0))
    m["c_h128b_d"] = hadamard(128) * np.float32(1.0 / (math.sqrt(8192) * 64.0))
    h1 = host_a1_inputs(inputs, core)
    m["a1_wqT"] = h1["wqT"]
    m["a1_wkvT"] = h1["wkvT"]
    m["a1_bq"] = h1["bq"]
    m["a1_bkv"] = h1["bkv"]
    m["a1_peT"] = h1["peT"]
    m["a1_woT"] = h1["woT"]
    m["ab_d"] = h1["ab"]
    m["diag_d"] = h1["diag"]
    m["identbf_d"] = h1["identbf_d"]
    h2 = host_a2_inputs(inputs)
    m["a2_wqT"] = h2["wqT"]
    m["a2_wkvT"] = h2["wkvT"]
    m["a2_wprojT"] = h2["wprojT"]
    m["a2_bproj"] = h2["bproj"]
    h3 = host_a3_inputs(inputs, core)
    m["a3_wqTp"] = h3["wqTp"]
    m["a3_wkvaT"] = h3["wkvaT"]
    m["a3_wkvbTk"] = h3["wkvbTk"]
    m["a3_wkvbTv"] = h3["wkvbTv"]
    m["a3_gam"] = h3["gam"]
    m["a3_cos2"] = h3["cos2"]
    m["a3_csraw"] = h3["csraw"]
    m["a3_woT"] = h3["woT"]
    ch0 = (core % 4) * 512
    alog = np.asarray(inputs["L1_attn_b_A_log"], np.float32)[ch0:ch0 + 512]
    dpar = np.asarray(inputs["L1_attn_b_D_param"], np.float32)[ch0:ch0 + 512]
    dtw = np.asarray(inputs["L1_attn_b_dt_proj_w"], np.float32)[ch0:ch0 + 512]
    dtb = np.asarray(inputs["L1_attn_b_dt_proj_b"], np.float32)[ch0:ch0 + 512]
    m["a4_xprojT"] = np.ascontiguousarray(
        np.asarray(inputs["L1_attn_b_x_proj_w"], np.float32).T)
    m["a4_dtpT"] = np.ascontiguousarray(dtw.T)
    m["a4_dtb"] = col_layout(dtb)
    m["a4_alog"] = np.ascontiguousarray(
        alog.reshape(4, 128, 16).transpose(1, 0, 2).reshape(128, 64))
    m["a4_dcol"] = col_layout(dpar)
    gidx = np.empty((128, 16), np.int32)
    for r in range(4):
        for sub in range(4):
            gidx[:, r * 4 + sub] = 2048 * r + ch0 + sub * 128 + np.arange(128)
    m["a4_gidx"] = gidx
    yidx = np.empty((128, 32), np.int32)
    blks = core_blocks(core)
    for r in range(4):
        for sub in range(8):
            loc = sub * 128 + np.arange(128)
            pos = np.where(loc < 512, blks[0] * 512 + loc,
                           blks[1] * 512 + (loc - 512))
            yidx[:, r * 8 + sub] = 4096 * r + pos
    m["a4_yidx"] = yidx
    return m


def kernel_fused(upto=99, **inputs):
    x0 = np.asarray(inputs["L0_attn_a_x"], np.float32)
    shards = shard_tokens(x0)
    x_fm = [np.ascontiguousarray(s.T) for s in shards]
    nc = bass.Bass("TRN2", target_bir_lowering=False)
    Bld = Builder(nc)
    build_fused(Bld, upto=upto)
    Bld.finalize()
    maps = []
    for c in range(8):
        m = host_fused_inputs(inputs, c)
        m["x0"] = x_fm[c]
        maps.append(m)
    res = Bld.run(maps)
    outs = [np.ascontiguousarray(np.asarray(res.results[c]["x_final"]).T)
            for c in range(8)]
    return np.ascontiguousarray(unshard_tokens(outs), dtype=np.float32)


def kernel(**inputs):
    return kernel_fused(**inputs)
